# version 11
# baseline (speedup 1.0000x reference)
import numpy as np
import ml_dtypes
import concourse.bass as bass
import concourse.mybir as mybir

F32 = mybir.dt.float32
BF16 = mybir.dt.bfloat16
AF = mybir.ActivationFunctionType
ALU = mybir.AluOpType
AX = mybir.AxisListType
BF = ml_dtypes.bfloat16

D = 2048
T_CORE = 2048
NH, HD = 16, 128
ENGS = ("sync", "gpsimd", "vector", "scalar", "tensor")


class Sem:
    def __init__(self, h, name):
        self.h = h; self.n = 0; self.name = name


class Ctx:
    def __init__(self, nc, stack):
        self.nc = nc
        self.stack = stack
        self.perm = stack
        self._sems = {}
        self.phase_idx = 0
        self.s_phase = self.sem("phase")

    def sem(self, name):
        if name not in self._sems:
            h = self.perm.enter_context(self.nc.semaphore(name))
            self._sems[name] = Sem(h, name)
        return self._sems[name]

    def scope(self):
        import contextlib
        @contextlib.contextmanager
        def cm():
            from contextlib import ExitStack
            old = self.stack
            with ExitStack() as st:
                self.stack = st
                try:
                    yield st
                finally:
                    self.stack = old
        return cm()

    def run_phase(self, phase):
        i = self.phase_idx
        sp = self.s_phase
        phase.begin({k: s.n for k, s in self._sems.items()})
        with self.nc.Block(name=getattr(phase, "name", f"ph{i}")) as block:
            for eng_name in ENGS:
                body = getattr(phase, eng_name, None)
                def mk(body=body):
                    def run(e):
                        if i > 0:
                            e.wait_ge(sp.h, 5 * i)
                        if body is not None:
                            body(e)
                        e.nop().then_inc(sp.h, 1)
                    return run
                getattr(block, eng_name)(mk())
        for name, total in phase.totals().items():
            self._sems[name].n += total
        sp.n += 5
        self.phase_idx += 1


class PhaseBase:
    def begin(self, bases):
        self.b = bases
    def totals(self):
        return self._totals
    def w(self, e, sem, rel):
        e.wait_ge(sem.h, self.b.get(sem.name, 0) + rel)


def host_consts():
    c = {}
    c["ones_bf"] = np.ones((128, 1), dtype=BF)
    c["ones1_f"] = np.ones((1, 128), dtype=np.float32)
    c["ident_bf"] = np.eye(128, dtype=BF)
    c["iota8"] = np.tile(np.arange(8, dtype=np.float32)[None, :], (128, 1))
    c["cmask"] = np.where(np.arange(128)[None, :] > np.arange(128)[:, None],
                          np.float32(-1e30), np.float32(0.0))
    c["ident_f"] = np.eye(128, dtype=np.float32)
    c["iota8x4"] = np.tile(np.arange(8, dtype=np.float32)[None, :], (128, 16))
    c["iotam99"] = c["iota8x4"] - 99.0
    m = np.arange(128)
    c["b32a"] = np.stack([(m // 32 == e).astype(BF) for e in range(4)] +
                         [np.zeros(128, BF) for _ in range(4)], 0)
    c["b32b"] = np.stack([np.zeros(128, BF) for _ in range(4)] +
                         [(m // 32 == (e - 4)).astype(BF) for e in range(4, 8)], 0)
    return c


class ConstPool(PhaseBase):
    name = "consts"
    def __init__(self, ctx, aps):
        nc, st = ctx.nc, ctx.stack
        self.ones_bf = st.enter_context(nc.sbuf_tensor("c_ones_bf", [128, 1], BF16))
        self.ones1_f = st.enter_context(nc.sbuf_tensor("c_ones1_f", [1, 128], F32))
        self.ident = st.enter_context(nc.sbuf_tensor("c_ident", [128, 128], BF16))
        self.iota8 = st.enter_context(nc.sbuf_tensor("c_iota8", [128, 8], F32))
        self.eps_rms = st.enter_context(nc.sbuf_tensor("c_eps_rms", [128, 1], F32))
        self.cmask = st.enter_context(nc.sbuf_tensor("c_cmask", [128, 128], F32))
        self.ident_f = st.enter_context(nc.sbuf_tensor("c_ident_f", [128, 128], F32))
        self.iota8x4 = st.enter_context(nc.sbuf_tensor("c_iota84", [128, 128], F32))
        self.iotam99 = st.enter_context(nc.sbuf_tensor("c_iom99", [128, 128], F32))
        self.b32a = st.enter_context(nc.sbuf_tensor("c_b32a", [8, 128], BF16))
        self.b32b = st.enter_context(nc.sbuf_tensor("c_b32b", [8, 128], BF16))
        self.aps = aps
        self.s = ctx.sem("dma_sync")
        self.sv = ctx.sem("v1")
        self._totals = {"dma_sync": 160, "v1": 1}
    def sync(self, e):
        e.dma_start(out=self.ones_bf[:, :], in_=self.aps["ones_bf"]).then_inc(self.s.h, 16)
        e.dma_start(out=self.ones1_f[:, :], in_=self.aps["ones1_f"]).then_inc(self.s.h, 16)
        e.dma_start(out=self.ident[:, :], in_=self.aps["ident_bf"]).then_inc(self.s.h, 16)
        e.dma_start(out=self.iota8[:, :], in_=self.aps["iota8"]).then_inc(self.s.h, 16)
        e.dma_start(out=self.cmask[:, :], in_=self.aps["cmask"]).then_inc(self.s.h, 16)
        e.dma_start(out=self.ident_f[:, :], in_=self.aps["ident_f"]).then_inc(self.s.h, 16)
        e.dma_start(out=self.iota8x4[:, :], in_=self.aps["iota8x4"]).then_inc(self.s.h, 16)
        e.dma_start(out=self.iotam99[:, :], in_=self.aps["iotam99"]).then_inc(self.s.h, 16)
        e.dma_start(out=self.b32a[:, :], in_=self.aps["b32a"]).then_inc(self.s.h, 16)
        e.dma_start(out=self.b32b[:, :], in_=self.aps["b32b"]).then_inc(self.s.h, 16)
        self.w(e, self.s, 160)
        self.w(e, self.sv, 1)

    def vector(self, e):
        e.memset(self.eps_rms[:, :], 1.1920929e-7).then_inc(self.sv.h, 1)


class LNPhase(PhaseBase):
    NCH = 4

    def __init__(self, ctx, consts, x_fm, xn_out, tag):
        self.c = consts
        self.x = x_fm; self.xn = xn_out
        self.name = f"ln_{tag}"
        nc, st = ctx.nc, ctx.stack
        t = tag
        self.xf = st.enter_context(nc.sbuf_tensor(f"ln_xf_{t}", [128, 16 * 512], F32))
        self.xb = st.enter_context(nc.sbuf_tensor(f"ln_xb_{t}", [128, 16 * 512], BF16))
        self.x2b = st.enter_context(nc.sbuf_tensor(f"ln_x2b_{t}", [128, 16 * 512], BF16))
        self.sm = st.enter_context(nc.sbuf_tensor(f"ln_sm_{t}", [1, 512 * 4], F32))
        self.xnb = st.enter_context(nc.sbuf_tensor(f"ln_xnb_{t}", [128, 16 * 512], BF16))
        self.scr = st.enter_context(nc.sbuf_tensor(f"ln_scr_{t}", [128, 512], F32))
        self.ps_s1 = st.enter_context(nc.psum_tensor(f"ln_s1_{t}", [1, 512], F32))
        self.ps_s2 = st.enter_context(nc.psum_tensor(f"ln_s2_{t}", [1, 512], F32))
        self.ps_mb = st.enter_context(nc.psum_tensor(f"ln_mb_{t}", [128, 512], F32))
        self.ps_rb = st.enter_context(nc.psum_tensor(f"ln_rb_{t}", [128, 512], F32))
        self.s_ld = ctx.sem("dma_sync")
        self.s_cst = ctx.sem("dma_gp")
        self.s_sq = ctx.sem("v1")
        self.s_stats = ctx.sem("t1")
        self.s_sm = ctx.sem("v2")
        self.s_sm2 = ctx.sem("sc1")
        self.s_bc = ctx.sem("t2")
        self.s_xn = ctx.sem("v3")
        self.s_st = ctx.sem("dma_out")
        n = self.NCH
        self._totals = {"dma_sync": 32 * n, "dma_gp": 32 * n, "v1": n, "t1": 2 * n,
                        "v2": 2 * n, "sc1": n, "t2": 2 * n, "v3": 16 * n,
                        "dma_out": 16 * n}

    def sync(self, e):
        for i in range(self.NCH):
            if i > 0:
                self.w(e, self.s_xn, 16 * i)
            src = self.x.rearrange("(kc p) t -> p kc t", p=128)
            for half in range(2):
                e.dma_start(
                    out=self.xf[:, :].rearrange("p (kc t) -> p kc t", kc=16)[:, 8*half:8*half+8, :],
                    in_=src[:, 8*half:8*half+8, 512*i:512*(i+1)],
                ).then_inc(self.s_ld.h, 16)
            if i > 0:
                self._store(e, i - 1)
        self.w(e, self.s_xn, 16 * self.NCH)
        self._store(e, self.NCH - 1)
        self.w(e, self.s_st, 16 * self.NCH)

    def _store(self, e, i):
        dst = self.xn.rearrange("(kc p) t -> p kc t", p=128)
        e.dma_start(
            out=dst[:, :, 512*i:512*(i+1)],
            in_=self.xnb[:, :].rearrange("p (kc t) -> p kc t", kc=16),
        ).then_inc(self.s_st.h, 16)

    def gpsimd(self, e):
        for i in range(self.NCH):
            if i > 0:
                self.w(e, self.s_stats, 2 * i)
            src = self.x.rearrange("(kc p) t -> p kc t", p=128)
            for half in range(2):
                e.dma_start(
                    out=self.xb[:, :].rearrange("p (kc t) -> p kc t", kc=16)[:, 8*half:8*half+8, :],
                    in_=src[:, 8*half:8*half+8, 512*i:512*(i+1)],
                ).then_inc(self.s_cst.h, 16)

    def vector(self, e):
        for i in range(self.NCH):
            self.w(e, self.s_cst, 32 * (i + 1))
            if i > 0:
                self.w(e, self.s_stats, 2 * i)
            e.tensor_mul(self.x2b[:, :], self.xb[:, :], self.xb[:, :]).then_inc(self.s_sq.h, 1)
            self.w(e, self.s_stats, 2 * (i + 1))
            negm = self.sm[:, 0:512]; e2 = self.sm[:, 512:1024]
            var = self.sm[:, 1024:1536]; rstd = self.sm[:, 1536:2048]
            e.tensor_scalar_mul(negm, self.ps_s1[:, :], -1.0 / 2048.0)
            e.tensor_scalar_mul(e2, self.ps_s2[:, :], 1.0 / 2048.0)
            t = self.scr[0:1, 0:512]
            e.tensor_mul(t, negm, negm)
            e.tensor_sub(var, e2, t)
            e.tensor_scalar_add(var, var, 1e-5).then_inc(self.s_sm.h, 1)
            self.w(e, self.s_sm2, i + 1)
            e.reciprocal(rstd, var).then_inc(self.s_sm.h, 1)
            self.w(e, self.s_bc, 2 * (i + 1))
            self.w(e, self.s_st, 16 * i)
            for kc in range(16):
                sl = slice(512 * kc, 512 * (kc + 1))
                e.tensor_add(self.scr[:, :], self.xf[:, sl], self.ps_mb[:, :])
                e.tensor_mul(self.xnb[:, sl], self.scr[:, :], self.ps_rb[:, :]).then_inc(self.s_xn.h, 1)

    def scalar(self, e):
        for i in range(self.NCH):
            self.w(e, self.s_sm, 2 * i + 1)
            var = self.sm[:, 1024:1536]
            e.activation(out=var, in_=var, func=AF.Sqrt).then_inc(self.s_sm2.h, 1)

    def tensor(self, e):
        for i in range(self.NCH):
            self.w(e, self.s_cst, 32 * (i + 1))
            if i > 0:
                self.w(e, self.s_sm, 2 * i - 1)
            for kc in range(16):
                sl = slice(512 * kc, 512 * (kc + 1))
                e.matmul(self.ps_s1[:, :], self.c.ones_bf[:, :], self.xb[:, sl],
                         start=(kc == 0), stop=(kc == 15))
            self.w(e, self.s_sq, i + 1)
            last = None
            for kc in range(16):
                sl = slice(512 * kc, 512 * (kc + 1))
                last = e.matmul(self.ps_s2[:, :], self.c.ones_bf[:, :], self.x2b[:, sl],
                                start=(kc == 0), stop=(kc == 15))
            last.then_inc(self.s_stats.h, 2)
            self.w(e, self.s_sm, 2 * (i + 1))
            if i > 0:
                self.w(e, self.s_xn, 16 * i)
            negm = self.sm[:, 0:512]; rstd = self.sm[:, 1536:2048]
            e.matmul(self.ps_mb[:, :], self.c.ones1_f[:, :], negm, start=True, stop=True)
            e.matmul(self.ps_rb[:, :], self.c.ones1_f[:, :], rstd, start=True, stop=True).then_inc(self.s_bc.h, 2)


class Plan(PhaseBase):
    def __init__(self, ctx, name):
        self.ctx = ctx
        self.name = name
        self.steps = []
        self.cnt = {}
        self._totals = self.cnt

    def sem(self, name):
        return self.ctx.sem(name)

    def step(self, engine, fn, waits=(), inc=None, manual=()):
        self.steps.append((engine, fn, tuple(waits), inc))
        if inc is not None:
            s, k = inc
            self.cnt[s.name] = self.cnt.get(s.name, 0) + k
        for (s, k) in manual:
            self.cnt[s.name] = self.cnt.get(s.name, 0) + k

    def rel(self, sem):
        return self.cnt.get(sem.name, 0)

    def _body(self, eng_name):
        def run(e):
            for (en, fn, waits, inc) in self.steps:
                if en != eng_name:
                    continue
                for (s, rv) in waits:
                    if rv > 0:
                        e.wait_ge(s.h, self.b.get(s.name, 0) + rv)
                ins = fn(e)
                if inc is not None:
                    s, k = inc
                    ins.then_inc(s.h, k)
        return run

    def __getattr__(self, name):
        if name in ENGS:
            return self._body(name)
        raise AttributeError(name)


_qs_ctr = [0]
def make_qscale_phase(ctx, consts, logpos_ap, scaler_ap, qscale_sb):
    nc, st = ctx.nc, ctx.stack
    _qs_ctr[0] += 1
    u = _qs_ctr[0]
    p = Plan(ctx, f"qscale{u}")
    lp = st.enter_context(nc.sbuf_tensor(f"qs_lp{u}", [1, 2048], F32))
    sc = st.enter_context(nc.sbuf_tensor(f"qs_sc{u}", [1, 16], F32))
    ps = st.enter_context(nc.psum_tensor(f"qs_ps{u}", [128, 256], F32))
    sd = p.sem("dma_sync"); st_ = p.sem("t1"); sv = p.sem("v1")
    p.step("sync", lambda e: e.dma_start(out=lp[:, :], in_=logpos_ap), inc=(sd, 16))
    p.step("sync", lambda e: e.dma_start(out=sc[:, :], in_=scaler_ap), inc=(sd, 16))
    def mms(e):
        last = None
        for tt in range(16):
            last = e.matmul(ps[:, 16*tt:16*(tt+1)], lp[:, 128*tt:128*(tt+1)], sc[:, :],
                            start=True, stop=True)
        return last
    p.step("tensor", mms, waits=[(sd, 32)], inc=(st_, 1))
    p.step("vector", lambda e: e.tensor_copy(qscale_sb[:, :], ps[:, :]),
           waits=[(st_, p.rel(st_))], inc=(sv, 1))
    return p


def make_lambda_phase(ctx, consts, lam_aps, neglam_sb):
    nc, st = ctx.nc, ctx.stack
    p = Plan(ctx, "lambda")
    lv = st.enter_context(nc.sbuf_tensor("lam_v", [1, 256], F32))
    scr = st.enter_context(nc.sbuf_tensor("lam_scr", [1, 128], F32))
    red = st.enter_context(nc.sbuf_tensor("lam_red", [1, 4], F32))
    p.red_dump = red
    ps = st.enter_context(nc.psum_tensor("lam_ps", [128, 1], F32))
    sd = p.sem("dma_sync"); sv = p.sem("v1"); ss = p.sem("sc1"); st_ = p.sem("t1")
    for i, ap in enumerate(lam_aps):
        p.step("sync", lambda e, ap=ap, i=i: e.dma_start(out=lv[:, 64*i:64*(i+1)], in_=ap), inc=(sd, 16))
    def v1(e):
        e.tensor_mul(scr[:, 0:64], lv[:, 0:64], lv[:, 64:128])
        e.tensor_mul(scr[:, 64:128], lv[:, 128:192], lv[:, 192:256])
        e.reduce_sum(red[:, 0:1], scr[:, 0:64], AX.X)
        return e.reduce_sum(red[:, 1:2], scr[:, 64:128], AX.X)
    p.step("vector", v1, waits=[(sd, 64)], inc=(sv, 1))
    p.step("scalar", lambda e: e.activation(out=red[:, 2:4], in_=red[:, 0:2], func=AF.Exp),
           waits=[(sv, 1)], inc=(ss, 1))
    def v2(e):
        return e.scalar_tensor_tensor(red[:, 0:1], red[:, 3:4], -0.2, red[:, 2:3],
                                      ALU.add, ALU.subtract)
    p.step("vector", v2, waits=[(ss, 1)], inc=(sv, 1))
    p.step("tensor", lambda e: e.matmul(ps[:, :], consts.ones1_f[:, :], red[:, 0:1],
                                        start=True, stop=True),
           waits=[(sv, 2)], inc=(st_, 1))
    p.step("vector", lambda e: e.tensor_copy(neglam_sb[:, :], ps[:, :]),
           waits=[(st_, 1)], inc=(sv, 1))
    return p


def make_qkv_phase(ctx, consts, tag, xn_dram, jobs, cos_sb, sin_sb, qscale_sb):
    nc, st = ctx.nc, ctx.stack
    p = Plan(ctx, f"qkv_{tag}")
    xn_sb = st.enter_context(nc.sbuf_tensor(f"qkv_xn_{tag}", [128, 16 * 2048], BF16))
    w_sb = st.enter_context(nc.sbuf_tensor(f"qkv_w_{tag}", [128, 16 * 2048], BF16))
    qf = [st.enter_context(nc.sbuf_tensor(f"qkv_qf{i}_{tag}", [128, 2048], F32)) for i in range(2)]
    q2f = st.enter_context(nc.sbuf_tensor(f"qkv_q2f_{tag}", [128, 2048], F32))
    rot = st.enter_context(nc.sbuf_tensor(f"qkv_rot_{tag}", [128, 2048], F32))
    t1 = st.enter_context(nc.sbuf_tensor(f"qkv_t1_{tag}", [128, 1024], F32))
    t2 = st.enter_context(nc.sbuf_tensor(f"qkv_t2_{tag}", [128, 1024], F32))
    ssq = st.enter_context(nc.sbuf_tensor(f"qkv_ssq_{tag}", [128, 64], F32))
    comb = st.enter_context(nc.sbuf_tensor(f"qkv_comb_{tag}", [128, 16], F32))
    ob = [st.enter_context(nc.sbuf_tensor(f"qkv_ob{i}_{tag}", [128, 2048], BF16)) for i in range(2)]
    ps = [st.enter_context(nc.psum_tensor(f"qkv_ps{i}_{tag}", [128, 2048], F32)) for i in range(2)]
    sd = p.sem("dma_sync"); so = p.sem("dma_out")
    smm = p.sem("t1"); sev = p.sem("sc1"); sssq = p.sem("v1"); srstd = p.sem("sc2")
    spp = p.sem("v2")

    xn_src = xn_dram.rearrange("(kc p) t -> p kc t", p=128)
    xn_dst = xn_sb[:, :].rearrange("p (kc t) -> p kc t", kc=16)
    for q in range(4):
        p.step("sync", lambda e, q=q: e.dma_start(out=xn_dst[:, 4*q:4*q+4, :],
                                                  in_=xn_src[:, 4*q:4*q+4, :]), inc=(sd, 16))

    for js, job in enumerate(jobs):
        mode = job["mode"]
        w_src = job["w"].rearrange("(kc p) n -> p kc n", p=128)
        w_dst = w_sb[:, :].rearrange("p (kc n) -> p kc n", kc=16)
        wld_target = None
        for q in range(4):
            waits = [(smm, p.rel(smm))] if (js > 0 and q == 0) else ()
            p.step("sync", lambda e, q=q, w_src=w_src: e.dma_start(out=w_dst[:, 4*q:4*q+4, :],
                                                      in_=w_src[:, 4*q:4*q+4, :]),
                   waits=waits, inc=(sd, 16))
        wld_target = p.rel(sd)

        for tt in range(16):
            pb = ps[tt % 2]
            def mms(e, tt=tt, pb=pb):
                last = None
                for kc in range(16):
                    lhs = xn_sb[:, 2048*kc + 128*tt: 2048*kc + 128*(tt+1)]
                    for nch in range(4):
                        last = e.matmul(pb[:, 512*nch:512*(nch+1)], lhs,
                                        w_sb[:, 2048*kc + 512*nch: 2048*kc + 512*(nch+1)],
                                        start=(kc == 0), stop=(kc == 15))
                return last
            waits = [(sd, wld_target)]
            if p.rel(smm) >= 2:
                waits.append((sev, p.rel(smm) - 1))
            p.step("tensor", mms, waits=waits, inc=(smm, 1))

            if mode == "v":
                def ev(e, tt=tt, pb=pb):
                    return e.copy(ob[tt % 2][:, :], pb[:, :])
                waits = [(smm, p.rel(smm))]
                if p.rel(so) >= 32:
                    waits.append((so, p.rel(so) - 16))
                p.step("scalar", ev, waits=waits, inc=(sev, 1))
                p.step("sync", lambda e, tt=tt, job=job: e.dma_start(
                    out=job["out"][128*tt:128*(tt+1), :], in_=ob[tt % 2][:, :]),
                    waits=[(sev, p.rel(sev))], inc=(so, 16))
                continue

            def ev(e, tt=tt, pb=pb):
                return e.copy(qf[tt % 2][:, :], pb[:, :])
            waits = [(smm, p.rel(smm))]
            if p.rel(spp) >= 2:
                waits.append((spp, p.rel(spp) - 1))
            p.step("scalar", ev, waits=waits, inc=(sev, 1))

            par = tt % 2
            qfb = qf[par]
            do_rms = mode in ("q0", "k0")
            if tt == 0 and "dbg_qf" in job:
                p.step("sync", lambda e, qfb=qfb, job=job: e.dma_start(out=job["dbg_qf"], in_=qfb[:, :]),
                       waits=[(sev, p.rel(sev))], inc=(so, 16))
            if do_rms:
                def vssq(e, qfb=qfb, par=par):
                    e.tensor_mul(q2f[:, :], qfb[:, :], qfb[:, :])
                    return e.tensor_reduce(ssq[:, 32*par:32*par+16],
                                           q2f[:, :].rearrange("p (h d) -> p h d", h=16),
                                           AX.X, ALU.add)
                p.step("vector", vssq, waits=[(sev, p.rel(sev))], inc=(sssq, 1))
                def srs(e, par=par):
                    e.activation(out=ssq[:, 32*par+16:32*par+32], in_=ssq[:, 32*par:32*par+16],
                                 func=AF.Ln, scale=1.0/128.0, bias=consts.eps_rms[:, :])
                    return e.activation(out=ssq[:, 32*par+16:32*par+32],
                                        in_=ssq[:, 32*par+16:32*par+32],
                                        func=AF.Exp, scale=-0.5)
                p.step("scalar", srs, waits=[(sssq, p.rel(sssq))], inc=(srstd, 1))

            def vrot(e, qfb=qfb, mode=mode, tt=tt):
                cosb = cos_sb[:, 64*tt:64*(tt+1)].rearrange("p (g d) -> p g d", g=1).broadcast_to([128, 16, 64])
                sinb = sin_sb[:, 64*tt:64*(tt+1)].rearrange("p (g d) -> p g d", g=1).broadcast_to([128, 16, 64])
                lo = qfb[:, :].rearrange("p (h d) -> p h d", h=16)[:, :, 0:64]
                hi = qfb[:, :].rearrange("p (h d) -> p h d", h=16)[:, :, 64:128]
                rlo = rot[:, :].rearrange("p (h d) -> p h d", h=16)[:, :, 0:64]
                rhi = rot[:, :].rearrange("p (h d) -> p h d", h=16)[:, :, 64:128]
                tv1 = t1[:, :].rearrange("p (h d) -> p h d", h=16)
                tv2 = t2[:, :].rearrange("p (h d) -> p h d", h=16)
                e.tensor_tensor(tv1, lo, cosb, ALU.mult)
                e.tensor_tensor(tv2, hi, sinb, ALU.mult)
                if mode in ("q0", "k0"):
                    e.tensor_add(rlo, tv1, tv2)
                else:
                    e.tensor_sub(rlo, tv1, tv2)
                e.tensor_tensor(tv1, lo, sinb, ALU.mult)
                e.tensor_tensor(tv2, hi, cosb, ALU.mult)
                if mode in ("q0", "k0"):
                    return e.tensor_sub(rhi, tv2, tv1)
                else:
                    return e.tensor_add(rhi, tv1, tv2)
            p.step("vector", vrot, waits=[(sev, p.rel(sev))], inc=(sssq, 1) if not do_rms else None)

            waits = []
            if do_rms:
                waits.append((srstd, p.rel(srstd)))
            if p.rel(so) >= 32:
                waits.append((so, p.rel(so) - 16))
            def vapply(e, mode=mode, par=par, tt=tt):
                last = None
                obb = ob[par]
                rstd_t = ssq[:, 32*par+16:32*par+32]
                if mode == "q0":
                    for h in range(16):
                        last = e.tensor_scalar(obb[:, 128*h:128*(h+1)],
                                               rot[:, 128*h:128*(h+1)],
                                               rstd_t[:, h:h+1],
                                               qscale_sb[:, 16*tt+h:16*tt+h+1],
                                               ALU.mult, ALU.mult)
                elif mode == "k0":
                    for h in range(16):
                        last = e.tensor_scalar_mul(obb[:, 128*h:128*(h+1)],
                                                   rot[:, 128*h:128*(h+1)],
                                                   rstd_t[:, h:h+1])
                else:
                    last = e.tensor_scalar_mul(obb[:, :], rot[:, :], 0.08838834764831845)
                return last
            p.step("vector", vapply, waits=waits, inc=(spp, 1))
            if tt == 0 and "dbg_rot" in job:
                p.step("sync", lambda e, job=job: e.dma_start(out=job["dbg_rot"], in_=rot[:, :]),
                       waits=[(spp, p.rel(spp))], inc=(so, 16))
            p.step("sync", lambda e, tt=tt, par=par, job=job: e.dma_start(
                out=job["out"][128*tt:128*(tt+1), :], in_=ob[par][:, :]),
                waits=[(spp, p.rel(spp))], inc=(so, 16))

    p.step("sync", lambda e: e.nop(), waits=[(so, p.rel(so))])
    return p




def ktiles_uniform(j, W=384):
    out = []
    base = 256 * j
    n = base // W
    for i in range(n):
        out.append((W * i, W, False))
    rem = base % W
    if rem:
        out.append((n * W, rem, False))
    out.append((base, 256, True))
    return out


def _k_rhs(kbuf_pair, k0, kw):
    out = []
    for c in range(kw // 128):
        gc = k0 // 128 + c
        blk = gc % 2
        off = (gc // 2) * 128
        out.append((kbuf_pair[blk][:, off:off + 128], 128 * c))
    return out


def make_attn_l1(ctx, consts, tag, q_tm, k_tm, v_tm, amask_ap, o_fm_sb):
    if not isinstance(k_tm, (list, tuple)):
        k_tm = [k_tm[0:2048, :], k_tm[2048:4096, :]]
        v_tm = [v_tm[0:2048, :], v_tm[2048:4096, :]]
    nc, st = ctx.nc, ctx.stack
    p = Plan(ctx, f"attn1_{tag}")
    kb = [[st.enter_context(nc.sbuf_tensor(f"at1_k{i}{blk}_{tag}", [128, 2048], BF16))
           for blk in range(2)] for i in range(2)]
    vb = [[st.enter_context(nc.sbuf_tensor(f"at1_v{i}{blk}_{tag}", [128, 16 * 129], BF16))
           for blk in range(2)] for i in range(2)]
    qbuf = [st.enter_context(nc.sbuf_tensor(f"at1_q{i}_{tag}", [128, 2048], BF16)) for i in range(2)]
    amask = st.enter_context(nc.sbuf_tensor(f"at1_am_{tag}", [128, 256], F32))
    p_sb = [st.enter_context(nc.sbuf_tensor(f"at1_p{i}_{tag}", [128, 512], BF16)) for i in range(2)]
    pt_sb = [st.enter_context(nc.sbuf_tensor(f"at1_pt{i}_{tag}", [128, 512], BF16)) for i in range(2)]
    m_sb = st.enter_context(nc.sbuf_tensor(f"at1_m_{tag}", [128, 3], F32))
    o_acc = st.enter_context(nc.sbuf_tensor(f"at1_oacc_{tag}", [128, 132], F32))
    corr = st.enter_context(nc.sbuf_tensor(f"at1_corr_{tag}", [128, 2], F32))
    rcp = st.enter_context(nc.sbuf_tensor(f"at1_rcp_{tag}", [128, 1], F32))
    o_bf = st.enter_context(nc.sbuf_tensor(f"at1_obf_{tag}", [128, 128], BF16))
    ps_s = [st.enter_context(nc.psum_tensor(f"at1_s{i}_{tag}", [128, 512], F32)) for i in range(2)]
    ps_pt = [st.enter_context(nc.psum_tensor(f"at1_ppt{i}_{tag}", [128, 512], BF16)) for i in range(2)]
    ps_o = [st.enter_context(nc.psum_tensor(f"at1_o{i}_{tag}", [128, 132], F32)) for i in range(2)]
    ps_y = st.enter_context(nc.psum_tensor(f"at1_y_{tag}", [128, 128], BF16))

    sd = p.sem("dma_sync"); gp1 = p.sem("gp1")
    t_s = p.sem("t1"); v_st = p.sem("v1"); sc_p = p.sem("sc1")
    t_tr = p.sem("t2"); v_pt = p.sem("v2"); t_o = p.sem("t3"); v_up = p.sem("v3")
    v_fin = p.sem("v4"); t_y = p.sem("t4"); v_ev = p.sem("v5"); sc_fin = p.sem("sc2")

    def ms(e):
        for i in range(2):
            for blk in range(2):
                e.memset(vb[i][blk][:, :], 1.0)
        return e.dma_start(out=amask[:, :], in_=amask_ap)
    p.step("gpsimd", ms, inc=(gp1, 16))

    kt_end = []
    gk = 0
    fin = 0
    pend = []

    def front(b, j, k0, kw, msk, kl, g, load_tgt):
        par = g % 2
        cur = g % 3; prev = (g - 1) % 3
        waits = [(sd, load_tgt)]
        if g >= 2:
            waits.append((sc_p, g - 1))
        def tsm(e, b=b, j=j, k0=k0, kw=kw, par=par):
            last = None
            for (ap, off) in _k_rhs(kb[b], k0, kw):
                last = e.matmul(ps_s[par][:, off:off+128],
                                qbuf[b][:, 128*j:128*(j+1)], ap, start=True, stop=True)
            return last
        p.step("tensor", tsm, waits=waits, inc=(t_s, 1))
        def vstats(e, par=par, kw=kw, msk=msk, kl=kl, cur=cur, prev=prev):
            if msk:
                e.tensor_add(ps_s[par][:, 0:256], ps_s[par][:, 0:256], amask[:, :])
            if kl == 0:
                return e.reduce_max(m_sb[:, cur:cur+1], ps_s[par][:, 0:kw], AX.X, negate=True)
            e.tensor_scalar_mul(ps_s[par][:, kw:kw+1], m_sb[:, prev:prev+1], -1.0)
            return e.reduce_max(m_sb[:, cur:cur+1], ps_s[par][:, 0:kw+1], AX.X, negate=True)
        p.step("vector", vstats, waits=[(t_s, g + 1)], inc=(v_st, 1))
        def sp(e, par=par, kw=kw, kl=kl, cur=cur, prev=prev, g=g):
            if kl > 0:
                e.activation(out=corr[:, g % 2:g % 2 + 1], in_=m_sb[:, prev:prev+1],
                             func=AF.Exp, scale=-1.0, bias=m_sb[:, cur:cur+1])
            return e.activation(out=p_sb[par][:, 0:kw], in_=ps_s[par][:, 0:kw],
                                func=AF.Exp, bias=m_sb[:, cur:cur+1])
        waits = [(v_st, g + 1)]
        if g >= 2:
            waits.append((t_tr, g - 1))
        p.step("scalar", sp, waits=waits, inc=(sc_p, 1))

    def back(b, k0, kw, kl, g):
        par = g % 2
        def ttr(e, kw=kw, par=par):
            last = None
            for c in range(kw // 128):
                last = e.transpose(ps_pt[par][:, 128*c:128*(c+1)], p_sb[par][:, 128*c:128*(c+1)],
                                   consts.ident[:, :])
            return last
        waits = [(sc_p, g + 1)]
        if g >= 2:
            waits.append((v_pt, g - 1))
        p.step("tensor", ttr, waits=waits, inc=(t_tr, 1))
        waits = [(t_tr, g + 1)]
        if g >= 2:
            waits.append((t_o, g - 1))
        p.step("vector", lambda e, kw=kw, par=par: e.tensor_copy(pt_sb[par][:, 0:kw], ps_pt[par][:, 0:kw]),
               waits=waits, inc=(v_pt, 1))
        def tom(e, k0=k0, kw=kw, par=par, b=b):
            last = None
            nc_ = kw // 128
            for c in range(nc_):
                gc = k0 // 128 + c
                blk = gc % 2
                off = 129 * (gc // 2)
                last = e.matmul(ps_o[par][:, 0:129], pt_sb[par][:, 128*c:128*(c+1)],
                                vb[b][blk][:, off:off+129],
                                start=(c == 0), stop=(c == nc_ - 1))
            return last
        waits = [(v_pt, g + 1)]
        if g >= 2:
            waits.append((v_up, g - 1))
        p.step("tensor", tom, waits=waits, inc=(t_o, 1))
        def vup(e, par=par, kl=kl, g=g):
            if kl == 0:
                return e.tensor_scalar(o_acc[:, 0:129], ps_o[par][:, 0:129], 1.0, None, ALU.mult)
            return e.scalar_tensor_tensor(o_acc[:, 0:129], o_acc[:, 0:129],
                                          corr[:, g % 2:g % 2 + 1], ps_o[par][:, 0:129],
                                          ALU.mult, ALU.add)
        p.step("vector", vup, waits=[(t_o, g + 1)], inc=(v_up, 1))

    for h in range(16):
        b = h % 2
        waits = [(gp1, 16)] if h < 2 else [(t_o, kt_end[h - 2])]
        p.step("sync", lambda e, b=b, h=h: e.dma_start(
            out=qbuf[b][:, :], in_=q_tm[:, 128*h:128*(h+1)], transpose=True),
            waits=waits, inc=(sd, 16))
        for blk in range(2):
            p.step("sync", lambda e, b=b, h=h, blk=blk: e.dma_start(
                out=kb[b][blk][:, :], in_=k_tm[blk][:, 128*h:128*(h+1)],
                transpose=True), inc=(sd, 16))
            vv = vb[b][blk][:, :].rearrange("p (kc n) -> p kc n", n=129)
            vsrc = v_tm[blk].rearrange("(kc p) n -> p kc n", p=128)
            p.step("sync", lambda e, b=b, h=h, blk=blk, vv=vv, vsrc=vsrc: e.dma_start(
                out=vv[:, :, 0:128], in_=vsrc[:, :, 128*h:128*(h+1)]),
                inc=(sd, 16))
        load_tgt = p.rel(sd)

        for j in range(16):
            kts = ktiles_uniform(j)
            for kl, (k0, kw, msk) in enumerate(kts):
                front(b, j, k0, kw, msk, kl, gk, load_tgt)
                for fn in pend:
                    fn()
                pend = [lambda b=b, k0=k0, kw=kw, kl=kl, g=gk: back(b, k0, kw, kl, g)]
                gk += 1
            for fn in pend:
                fn()
            pend = []
            def sfin(e):
                e.activation(out=rcp[:, :], in_=o_acc[:, 128:129], func=AF.Ln)
                return e.activation(out=rcp[:, :], in_=rcp[:, :], func=AF.Exp, scale=-1.0)
            p.step("scalar", sfin, waits=[(v_up, gk)], inc=(sc_fin, 1))
            def vfin(e):
                return e.tensor_tensor(o_bf[:, :], o_acc[:, 0:128],
                                       rcp[:, 0:1].broadcast_to([128, 128]), ALU.mult)
            p.step("vector", vfin, waits=[(sc_fin, fin + 1)], inc=(v_fin, 1))
            waits = [(v_fin, fin + 1)]
            if fin >= 1:
                waits.append((v_ev, fin))
            p.step("tensor", lambda e: e.transpose(ps_y[:, :], o_bf[:, :], consts.ident[:, :]),
                   waits=waits, inc=(t_y, 1))
            p.step("vector", lambda e, h=h, j=j: e.tensor_copy(
                o_fm_sb[:, 2048*h + 128*j: 2048*h + 128*(j+1)], ps_y[:, :]),
                waits=[(t_y, fin + 1)], inc=(v_ev, 1))
            fin += 1
        kt_end.append(gk)
    p.step("sync", lambda e: e.nop(), waits=[(v_ev, fin)])
    return p


def make_attn_l0(ctx, consts, tag, q_tm, k_tm, v_tm, amask_ap, neglam_sb, s_fm_out):
    if not isinstance(k_tm, (list, tuple)):
        k_tm = [k_tm[0:2048, :], k_tm[2048:4096, :]]
        v_tm = [v_tm[0:2048, :], v_tm[2048:4096, :]]
    nc, st = ctx.nc, ctx.stack
    p = Plan(ctx, f"attn0_{tag}")
    ka = [[st.enter_context(nc.sbuf_tensor(f"at0_ka{i}{blk}_{tag}", [128, 2048], BF16))
           for blk in range(2)] for i in range(2)]
    kbb = [[st.enter_context(nc.sbuf_tensor(f"at0_kb{i}{blk}_{tag}", [128, 2048], BF16))
            for blk in range(2)] for i in range(2)]
    qa = [st.enter_context(nc.sbuf_tensor(f"at0_qa{i}_{tag}", [128, 2048], BF16)) for i in range(2)]
    qb = [st.enter_context(nc.sbuf_tensor(f"at0_qb{i}_{tag}", [128, 2048], BF16)) for i in range(2)]
    vbf = [[st.enter_context(nc.sbuf_tensor(f"at0_v{i}{blk}_{tag}", [128, 16 * 257], BF16))
            for blk in range(2)] for i in range(2)]
    amask = st.enter_context(nc.sbuf_tensor(f"at0_am_{tag}", [128, 256], F32))
    p_sb = [st.enter_context(nc.sbuf_tensor(f"at0_p{i}_{tag}", [128, 1024], BF16)) for i in range(2)]
    pt_sb = [st.enter_context(nc.sbuf_tensor(f"at0_pt{i}_{tag}", [128, 1024], BF16)) for i in range(2)]
    m_a = st.enter_context(nc.sbuf_tensor(f"at0_ma_{tag}", [128, 3], F32))
    m_b = st.enter_context(nc.sbuf_tensor(f"at0_mb_{tag}", [128, 3], F32))
    corr_a = st.enter_context(nc.sbuf_tensor(f"at0_ca_{tag}", [128, 2], F32))
    corr_b = st.enter_context(nc.sbuf_tensor(f"at0_cb_{tag}", [128, 2], F32))
    oa = st.enter_context(nc.sbuf_tensor(f"at0_oa_{tag}", [128, 260], F32))
    ob_ = st.enter_context(nc.sbuf_tensor(f"at0_ob_{tag}", [128, 260], F32))
    rcp_a = st.enter_context(nc.sbuf_tensor(f"at0_ra_{tag}", [128, 1], F32))
    rcp_b = st.enter_context(nc.sbuf_tensor(f"at0_rb_{tag}", [128, 1], F32))
    o2n = st.enter_context(nc.sbuf_tensor(f"at0_o2n_{tag}", [128, 256], F32))
    y_tm = st.enter_context(nc.sbuf_tensor(f"at0_ytm_{tag}", [128, 256], F32))
    y_stage = st.enter_context(nc.sbuf_tensor(f"at0_yst_{tag}", [128, 2 * 2048], F32))
    ps_sa = [st.enter_context(nc.psum_tensor(f"at0_psa{i}_{tag}", [128, 512], F32)) for i in range(2)]
    ps_sb_ = [st.enter_context(nc.psum_tensor(f"at0_psb{i}_{tag}", [128, 512], F32)) for i in range(2)]
    ps_pt = st.enter_context(nc.psum_tensor(f"at0_ppt_{tag}", [128, 1024], BF16))
    ps_oa = st.enter_context(nc.psum_tensor(f"at0_poa_{tag}", [128, 260], F32))
    ps_ob = st.enter_context(nc.psum_tensor(f"at0_pob_{tag}", [128, 260], F32))
    ps_y = st.enter_context(nc.psum_tensor(f"at0_py_{tag}", [128, 256], F32))

    sd = p.sem("dma_sync"); gp1 = p.sem("gp1"); so = p.sem("dma_out")
    t_s = p.sem("t1"); v_st = p.sem("v1"); sc_p = p.sem("sc1")
    t_tr = p.sem("t2"); v_pt = p.sem("v2"); t_o = p.sem("t3"); v_up = p.sem("v3")
    v_fin = p.sem("v4"); t_y = p.sem("t4"); v_ev = p.sem("v5"); sc_fin = p.sem("sc2")

    def ms(e):
        for i in range(2):
            for blk in range(2):
                e.memset(vbf[i][blk][:, :], 1.0)
        return e.dma_start(out=amask[:, :], in_=amask_ap)
    p.step("gpsimd", ms, inc=(gp1, 16))

    kt_end = []
    gk = 0
    fin = 0
    pend = []

    def front(bpar, j, k0, kw, msk, kl, g, load_tgt):
        par = g % 2
        cur = g % 3; prev = (g - 1) % 3
        waits = [(sd, load_tgt)]
        if g >= 2:
            waits.append((sc_p, g - 1))
        def tsm(e, bpar=bpar, j=j, k0=k0, kw=kw, par=par):
            last = None
            for (ap, off) in _k_rhs(ka[bpar], k0, kw):
                last = e.matmul(ps_sa[par][:, off:off+128],
                                qa[bpar][:, 128*j:128*(j+1)], ap, start=True, stop=True)
            for (ap, off) in _k_rhs(kbb[bpar], k0, kw):
                last = e.matmul(ps_sb_[par][:, off:off+128],
                                qb[bpar][:, 128*j:128*(j+1)], ap, start=True, stop=True)
            return last
        p.step("tensor", tsm, waits=waits, inc=(t_s, 1))
        def vstats(e, par=par, kw=kw, msk=msk, kl=kl, cur=cur, prev=prev):
            last = None
            for ps, m_ in ((ps_sa[par], m_a), (ps_sb_[par], m_b)):
                if msk:
                    e.tensor_add(ps[:, 0:256], ps[:, 0:256], amask[:, :])
                if kl == 0:
                    last = e.reduce_max(m_[:, cur:cur+1], ps[:, 0:kw], AX.X, negate=True)
                else:
                    e.tensor_scalar_mul(ps[:, kw:kw+1], m_[:, prev:prev+1], -1.0)
                    last = e.reduce_max(m_[:, cur:cur+1], ps[:, 0:kw+1], AX.X, negate=True)
            return last
        p.step("vector", vstats, waits=[(t_s, g + 1)], inc=(v_st, 1))
        def sp(e, par=par, kw=kw, kl=kl, cur=cur, prev=prev, g=g):
            last = None
            for ps, m_, co, off in ((ps_sa[par], m_a, corr_a, 0), (ps_sb_[par], m_b, corr_b, 512)):
                if kl > 0:
                    e.activation(out=co[:, g % 2:g % 2 + 1], in_=m_[:, prev:prev+1],
                                 func=AF.Exp, scale=-1.0, bias=m_[:, cur:cur+1])
                last = e.activation(out=p_sb[par][:, off:off+kw], in_=ps[:, 0:kw],
                                    func=AF.Exp, bias=m_[:, cur:cur+1])
            return last
        waits = [(v_st, g + 1)]
        if g >= 2:
            waits.append((t_tr, g - 1))
        p.step("scalar", sp, waits=waits, inc=(sc_p, 1))

    def back(bpar, k0, kw, kl, g):
        par = g % 2
        def ttr(e, kw=kw, par=par):
            last = None
            for off in (0, 512):
                for c in range(kw // 128):
                    last = e.transpose(ps_pt[:, off + 128*c: off + 128*(c+1)],
                                       p_sb[par][:, off + 128*c: off + 128*(c+1)],
                                       consts.ident[:, :])
            return last
        waits = [(sc_p, g + 1)]
        if g >= 1:
            waits.append((v_pt, g))
        p.step("tensor", ttr, waits=waits, inc=(t_tr, 1))
        def vpt(e, kw=kw, par=par):
            e.tensor_copy(pt_sb[par][:, 0:kw], ps_pt[:, 0:kw])
            return e.tensor_copy(pt_sb[par][:, 512:512+kw], ps_pt[:, 512:512+kw])
        waits = [(t_tr, g + 1)]
        if g >= 2:
            waits.append((t_o, g - 1))
        p.step("vector", vpt, waits=waits, inc=(v_pt, 1))
        def tom(e, k0=k0, kw=kw, bpar=bpar, par=par):
            last = None
            nc_ = kw // 128
            for ps_o_, off in ((ps_oa, 0), (ps_ob, 512)):
                for c in range(nc_):
                    gc = k0 // 128 + c
                    blk = gc % 2
                    voff = 257 * (gc // 2)
                    last = e.matmul(ps_o_[:, 0:257], pt_sb[par][:, off + 128*c: off + 128*(c+1)],
                                    vbf[bpar][blk][:, voff:voff+257],
                                    start=(c == 0), stop=(c == nc_ - 1))
            return last
        waits = [(v_pt, g + 1)]
        if g >= 1:
            waits.append((v_up, g))
        p.step("tensor", tom, waits=waits, inc=(t_o, 1))
        def vup(e, kl=kl, g=g):
            last = None
            for ps_o_, o_, co in ((ps_oa, oa, corr_a), (ps_ob, ob_, corr_b)):
                if kl == 0:
                    last = e.tensor_scalar(o_[:, 0:257], ps_o_[:, 0:257], 1.0, None, ALU.mult)
                else:
                    last = e.scalar_tensor_tensor(o_[:, 0:257], o_[:, 0:257],
                                                  co[:, g % 2:g % 2 + 1],
                                                  ps_o_[:, 0:257], ALU.mult, ALU.add)
            return last
        p.step("vector", vup, waits=[(t_o, g + 1)], inc=(v_up, 1))

    for jv in range(8):
        bpar = jv % 2
        ha, hb = jv, jv + 8
        waits = [(gp1, 16)] if jv < 2 else [(t_o, kt_end[jv - 2])]
        p.step("sync", lambda e, bpar=bpar, ha=ha: e.dma_start(
            out=qa[bpar][:, :], in_=q_tm[:, 128*ha:128*(ha+1)], transpose=True),
            waits=waits, inc=(sd, 16))
        p.step("sync", lambda e, bpar=bpar, hb=hb: e.dma_start(
            out=qb[bpar][:, :], in_=q_tm[:, 128*hb:128*(hb+1)], transpose=True), inc=(sd, 16))
        for blk in range(2):
            p.step("sync", lambda e, bpar=bpar, ha=ha, blk=blk: e.dma_start(
                out=ka[bpar][blk][:, :], in_=k_tm[blk][:, 128*ha:128*(ha+1)],
                transpose=True), inc=(sd, 16))
            p.step("sync", lambda e, bpar=bpar, hb=hb, blk=blk: e.dma_start(
                out=kbb[bpar][blk][:, :], in_=k_tm[blk][:, 128*hb:128*(hb+1)],
                transpose=True), inc=(sd, 16))
            vv = vbf[bpar][blk][:, :].rearrange("p (kc n) -> p kc n", n=257)
            vsrc = v_tm[blk].rearrange("(kc p) n -> p kc n", p=128)
            p.step("sync", lambda e, jv=jv, blk=blk, vv=vv, vsrc=vsrc, bpar=bpar: e.dma_start(
                out=vv[:, :, 0:256], in_=vsrc[:, :, 256*jv:256*(jv+1)]),
                inc=(sd, 16))
        load_tgt = p.rel(sd)

        for j in range(16):
            kts = ktiles_uniform(j)
            for kl, (k0, kw, msk) in enumerate(kts):
                front(bpar, j, k0, kw, msk, kl, gk, load_tgt)
                for fn in pend:
                    fn()
                pend = [lambda bpar=bpar, k0=k0, kw=kw, kl=kl, g=gk: back(bpar, k0, kw, kl, g)]
                gk += 1
            for fn in pend:
                fn()
            pend = []
            def sfin(e):
                e.activation(out=rcp_a[:, :], in_=oa[:, 256:257], func=AF.Ln)
                e.activation(out=rcp_a[:, :], in_=rcp_a[:, :], func=AF.Exp, scale=-1.0)
                e.activation(out=rcp_b[:, :], in_=ob_[:, 256:257], func=AF.Ln)
                return e.activation(out=rcp_b[:, :], in_=rcp_b[:, :], func=AF.Exp, scale=-1.0)
            p.step("scalar", sfin, waits=[(v_up, gk)], inc=(sc_fin, 1))
            def vfin(e):
                e.tensor_scalar(o2n[:, :], ob_[:, 0:256], rcp_b[:, 0:1], neglam_sb[:, 0:1],
                                ALU.mult, ALU.mult)
                return e.scalar_tensor_tensor(y_tm[:, :], oa[:, 0:256], rcp_a[:, 0:1],
                                              o2n[:, :], ALU.mult, ALU.add)
            p.step("vector", vfin, waits=[(sc_fin, fin + 1)], inc=(v_fin, 1))
            def tyr(e):
                e.transpose(ps_y[:, 0:128], y_tm[:, 0:128], consts.ident_f[:, :])
                return e.transpose(ps_y[:, 128:256], y_tm[:, 128:256], consts.ident_f[:, :])
            waits = [(v_fin, fin + 1)]
            if fin >= 1:
                waits.append((v_ev, fin))
            p.step("tensor", tyr, waits=waits, inc=(t_y, 1))
            def vev(e, j=j):
                e.tensor_copy(y_stage[:, 128*j:128*(j+1)], ps_y[:, 0:128])
                return e.tensor_copy(y_stage[:, 2048 + 128*j:2048 + 128*(j+1)], ps_y[:, 128:256])
            p.step("vector", vev, waits=[(t_y, fin + 1), (so, p.rel(so))], inc=(v_ev, 1))
            fin += 1

        for rb_ in range(2):
            p.step("sync", lambda e, jv=jv, rb_=rb_: e.dma_start(
                out=s_fm_out[256*jv + 128*rb_: 256*jv + 128*(rb_+1), :],
                in_=y_stage[:, 2048*rb_:2048*(rb_+1)]),
                waits=[(v_ev, fin)], inc=(so, 16))
        kt_end.append(gk)
    p.step("sync", lambda e: e.nop(), waits=[(so, p.rel(so))])
    return p

def make_wo_phase(ctx, consts, tag, o_fm_sb, wo_dram, s_fm_out):
    nc, st = ctx.nc, ctx.stack
    p = Plan(ctx, f"wo_{tag}")
    wo_sb = st.enter_context(nc.sbuf_tensor(f"wo_w_{tag}", [128, 16 * 2048], BF16))
    sev = [st.enter_context(nc.sbuf_tensor(f"wo_ev{i}_{tag}", [128, 512], F32)) for i in range(2)]
    ps = [st.enter_context(nc.psum_tensor(f"wo_ps{i}_{tag}", [128, 512], F32)) for i in range(2)]
    sd = p.sem("dma_sync"); so = p.sem("dma_out")
    smm = p.sem("t1"); sevs = p.sem("sc1")
    wsrc = wo_dram.rearrange("(kc p) n -> p kc n", p=128)
    wdst = wo_sb[:, :].rearrange("p (kc n) -> p kc n", kc=16)
    for q in range(4):
        p.step("sync", lambda e, q=q: e.dma_start(out=wdst[:, 4*q:4*q+4, :],
                                                  in_=wsrc[:, 4*q:4*q+4, :]), inc=(sd, 16))
    wt = p.rel(sd)
    g = 0
    for mc in range(16):
        for tc in range(4):
            def mm(e, mc=mc, tc=tc, g=g):
                last = None
                for kc in range(16):
                    last = e.matmul(ps[g % 2][:, :], wo_sb[:, 2048*kc + 128*mc: 2048*kc + 128*(mc+1)],
                                    o_fm_sb[:, 2048*kc + 512*tc: 2048*kc + 512*(tc+1)],
                                    start=(kc == 0), stop=(kc == 15))
                return last
            waits = [(sd, wt)]
            if g >= 2:
                waits.append((sevs, g - 1))
            p.step("tensor", mm, waits=waits, inc=(smm, 1))
            def ev(e, g=g):
                return e.copy(sev[g % 2][:, :], ps[g % 2][:, :])
            waits = [(smm, g + 1)]
            if g >= 2:
                waits.append((so, 16 * (g - 1)))
            p.step("scalar", ev, waits=waits, inc=(sevs, 1))
            p.step("sync", lambda e, mc=mc, tc=tc, g=g: e.dma_start(
                out=s_fm_out[128*mc:128*(mc+1), 512*tc:512*(tc+1)], in_=sev[g % 2][:, :]),
                waits=[(sevs, g + 1)], inc=(so, 16))
            g += 1
    p.step("sync", lambda e: e.nop(), waits=[(so, 16 * g)])
    return p


def make_ffn_l1(ctx, consts, tag, xn_dram, wg_dram, wu_dram, wd_dram, g_fm_out):
    nc, st = ctx.nc, ctx.stack
    p = Plan(ctx, f"ffn1_{tag}")
    xn_sb = st.enter_context(nc.sbuf_tensor(f"f1_xn_{tag}", [128, 16 * 512], BF16))
    wbuf = [st.enter_context(nc.sbuf_tensor(f"f1_w{i}_{tag}", [128, 16 * 1024], BF16)) for i in range(2)]
    hs = st.enter_context(nc.sbuf_tensor(f"f1_hs_{tag}", [128, 64 * 512], BF16))
    hj = [st.enter_context(nc.sbuf_tensor(f"f1_hj{i}_{tag}", [128, 512], F32)) for i in range(4)]
    aj = [st.enter_context(nc.sbuf_tensor(f"f1_aj{i}_{tag}", [128, 512], F32)) for i in range(4)]
    hg = st.enter_context(nc.sbuf_tensor(f"f1_hg_{tag}", [128, 512], F32))
    hgj = [st.enter_context(nc.sbuf_tensor(f"f1_hgj{i}_{tag}", [128, 512], F32)) for i in range(4)]
    scr = [st.enter_context(nc.sbuf_tensor(f"f1_scr{i}_{tag}", [128, 512], F32)) for i in range(7)]
    wdbuf = wbuf
    gsb = [st.enter_context(nc.sbuf_tensor(f"f1_g{i}_{tag}", [128, 512], F32)) for i in range(2)]
    pg = [st.enter_context(nc.psum_tensor(f"f1_pg{i}_{tag}", [128, 512], F32)) for i in range(4)]
    pu = [st.enter_context(nc.psum_tensor(f"f1_pu{i}_{tag}", [128, 512], F32)) for i in range(4)]
    ps_d = [pg[0], pg[1]]
    sd = p.sem("dma_sync"); so = p.sem("dma_out")
    t_mm = p.sem("t1"); v_h = p.sem("v1"); sc_s = p.sem("sc1"); v_sp = p.sem("v2")
    t_d = p.sem("t2"); sc_d = p.sem("sc2")

    xn_src = xn_dram.rearrange("(kc p) t -> p kc t", p=128)
    wg_src = wg_dram.rearrange("(kc p) n -> p kc n", p=128)
    wu_src = wu_dram.rearrange("(kc p) n -> p kc n", p=128)
    wd_src = wd_dram.rearrange("(ic p) n -> p ic n", p=128)

    for tc in range(4):
        base_v_sp = p.rel(v_sp)
        def ldxn(e, tc=tc):
            return e.dma_start(out=xn_sb[:, :].rearrange("p (kc t) -> p kc t", kc=16),
                               in_=xn_src[:, :, 512*tc:512*(tc+1)])
        waits = []
        if tc > 0:
            waits.append((t_mm, p.rel(t_mm)))
        p.step("sync", ldxn, waits=waits, inc=(sd, 16))
        xn_tgt = p.rel(sd)

        gu = 0
        for c in range(16):
            wb = wbuf[c % 2]
            waits = [(t_mm, p.rel(t_mm) - 4)] if (tc > 0 or c >= 2) else []
            if tc > 0 and c < 2:
                waits.append((t_d, 16 * tc - (1 if c == 0 else 0)))
            wv_ = wb[:, :].rearrange("p (kc two n) -> p kc two n", two=2, n=512)
            p.step("sync", lambda e, c=c, wv_=wv_: e.dma_start(
                out=wv_[:, :, 0, :], in_=wg_src[:, :, 512*c:512*(c+1)]),
                waits=waits, inc=(sd, 16))
            p.step("sync", lambda e, c=c, wv_=wv_: e.dma_start(
                out=wv_[:, :, 1, :], in_=wu_src[:, :, 512*c:512*(c+1)]),
                inc=(sd, 16))
            w_tgt = p.rel(sd)
            for j in range(4):
                pg_ = pg[j]; pu_ = pu[j]
                def mm(e, c=c, j=j, wb=wb, pg_=pg_, pu_=pu_):
                    last = None
                    for kc in range(16):
                        last = e.matmul(pg_[:, :], wb[:, 1024*kc + 128*j: 1024*kc + 128*(j+1)],
                                        xn_sb[:, 512*kc:512*(kc+1)],
                                        start=(kc == 0), stop=(kc == 15))
                    for kc in range(16):
                        last = e.matmul(pu_[:, :], wb[:, 1024*kc + 512 + 128*j: 1024*kc + 512 + 128*(j+1)],
                                        xn_sb[:, 512*kc:512*(kc+1)],
                                        start=(kc == 0), stop=(kc == 15))
                    return last
                waits = [(sd, max(w_tgt, xn_tgt))]
                if tc > 0 or gu >= 4:
                    waits.append((v_h, p.rel(v_h) - 3))
                p.step("tensor", mm, waits=waits, inc=(t_mm, 1))
                p.step("scalar", lambda e, j=j, pg_=pg_: e.activation(
                    out=hgj[j][:, :], in_=pg_[:, :], func=AF.Silu),
                    waits=[(t_mm, p.rel(t_mm)), (v_h, p.rel(v_h) - (3 if (tc > 0 or gu >= 4) else gu))],
                    inc=(sc_s, 1))
                p.step("vector", lambda e, j=j, pu_=pu_: e.tensor_mul(hj[j][:, :], hgj[j][:, :], pu_[:, :]),
                       waits=[(sc_s, p.rel(sc_s))], inc=(v_h, 1))
                gu += 1
            def sabs(e):
                last = None
                for j in range(4):
                    last = e.activation(out=aj[j][:, :], in_=hj[j][:, :], func=AF.Abs)
                return last
            p.step("scalar", sabs, waits=[(v_h, p.rel(v_h))], inc=(sc_s, 1))
            def vsparse(e, c=c, tc=tc):
                MIN = ALU.min
                e.tensor_tensor(scr[0][:, :], aj[0][:, :], aj[1][:, :], MIN)
                e.tensor_tensor(scr[1][:, :], aj[2][:, :], aj[3][:, :], MIN)
                e.tensor_tensor(scr[2][:, :], aj[0][:, :], aj[2][:, :], MIN)
                e.tensor_tensor(scr[3][:, :], aj[1][:, :], aj[3][:, :], MIN)
                e.tensor_tensor(scr[4][:, :], aj[0][:, :], aj[3][:, :], MIN)
                e.tensor_tensor(scr[5][:, :], aj[1][:, :], aj[2][:, :], MIN)
                e.tensor_max(scr[6][:, :], scr[0][:, :], scr[1][:, :])
                e.tensor_max(scr[0][:, :], scr[2][:, :], scr[3][:, :])
                e.tensor_max(scr[3][:, :], scr[4][:, :], scr[5][:, :])
                e.tensor_max(scr[2][:, :], scr[6][:, :], scr[0][:, :])
                e.tensor_max(scr[1][:, :], scr[2][:, :], scr[3][:, :])
                gsel = scr[5]
                last = None
                for j in range(4):
                    e.tensor_tensor(gsel[:, :], aj[j][:, :], scr[1][:, :], ALU.is_ge)
                    last = e.tensor_mul(hs[:, 512*(4*c+j):512*(4*c+j+1)], hj[j][:, :], gsel[:, :])
                return last
            waits = [(sc_s, p.rel(sc_s))]
            if tc > 0:
                waits.append((t_d, 16 * tc))
            p.step("vector", vsparse, waits=waits, inc=(v_sp, 1))
        if tc == 0 and getattr(ctx, "dbg_hs", None) is not None:
            p.step("sync", lambda e: e.dma_start(out=ctx.dbg_hs, in_=hs[:, :]),
                   waits=[(v_sp, base_v_sp + 16)], inc=(so, 16))
        for oc in range(16):
            wdb = wdbuf[oc % 2]
            waits = [(v_sp, base_v_sp + 16)]
            if tc > 0 or oc >= 2:
                waits.append((t_d, p.rel(t_d) - 1))
            p.step("sync", lambda e, oc=oc, wdb=wdb: e.dma_start(
                out=wdb[:, 0:56*128].rearrange("p (ic n) -> p ic n", n=128),
                in_=wd_src[:, 0:56, 128*oc:128*(oc+1)]),
                waits=waits, inc=(sd, 16))
            p.step("sync", lambda e, oc=oc, wdb=wdb: e.dma_start(
                out=wdb[:, 56*128:64*128].rearrange("p (ic n) -> p ic n", n=128),
                in_=wd_src[:, 56:64, 128*oc:128*(oc+1)]),
                inc=(sd, 16))
            wd_tgt = p.rel(sd)
            def mmd(e, oc=oc, wdb=wdb):
                last = None
                for ic in range(64):
                    last = e.matmul(ps_d[oc % 2][:, :], wdb[:, 128*ic:128*(ic+1)],
                                    hs[:, 512*ic:512*(ic+1)],
                                    start=(ic == 0), stop=(ic == 63))
                return last
            waits = [(sd, wd_tgt)]
            if tc > 0 or oc >= 2:
                waits.append((sc_d, p.rel(sc_d) - 1))
            p.step("tensor", mmd, waits=waits, inc=(t_d, 1))
            def ev(e, oc=oc):
                return e.activation(out=gsb[oc % 2][:, :], in_=ps_d[oc % 2][:, :], func=AF.Silu)
            waits = [(t_d, p.rel(t_d))]
            if p.rel(so) >= 32:
                waits.append((so, p.rel(so) - 16))
            p.step("scalar", ev, waits=waits, inc=(sc_d, 1))
            p.step("sync", lambda e, oc=oc, tc=tc: e.dma_start(
                out=g_fm_out[128*oc:128*(oc+1), 512*tc:512*(tc+1)], in_=gsb[oc % 2][:, :]),
                waits=[(sc_d, p.rel(sc_d))], inc=(so, 16))
    p.step("sync", lambda e: e.nop(), waits=[(so, p.rel(so))])
    return p


def make_ffn_l0(ctx, consts, tag, xn_dram, w, g_fm_out):
    nc, st = ctx.nc, ctx.stack
    p = Plan(ctx, f"ffn0_{tag}")
    F = 7168; NFS = 56
    xn_sb = st.enter_context(nc.sbuf_tensor(f"f0_xn_{tag}", [128, 16 * 512], BF16))
    wg8_sb = st.enter_context(nc.sbuf_tensor(f"f0_wg8_{tag}", [128, 16 * 8], BF16))
    b2w = st.enter_context(nc.sbuf_tensor(f"f0_b2w_{tag}", [128, 2 * 2048], BF16))
    u_sb = st.enter_context(nc.sbuf_tensor(f"f0_u_{tag}", [128, 128], F32))
    u1_sb = st.enter_context(nc.sbuf_tensor(f"f0_u1_{tag}", [128, 128], F32))
    nmx = st.enter_context(nc.sbuf_tensor(f"f0_nmx_{tag}", [128, 16], F32))
    nm1_sb = st.enter_context(nc.sbuf_tensor(f"f0_nm1_{tag}", [128, 16], F32))
    nm2_sb = st.enter_context(nc.sbuf_tensor(f"f0_nm2_{tag}", [128, 16], F32))
    r_t1 = st.enter_context(nc.sbuf_tensor(f"f0_rt1_{tag}", [128, 128], F32))
    r_t2 = st.enter_context(nc.sbuf_tensor(f"f0_rt2_{tag}", [128, 128], F32))
    nidx = st.enter_context(nc.sbuf_tensor(f"f0_nidx_{tag}", [128, 32], F32))
    idx_sb = st.enter_context(nc.sbuf_tensor(f"f0_idx_{tag}", [128, 32], F32))
    m1_sb = st.enter_context(nc.sbuf_tensor(f"f0_m1_{tag}", [128, 16], F32))
    m2_sb = st.enter_context(nc.sbuf_tensor(f"f0_m2_{tag}", [128, 16], F32))
    negden = st.enter_context(nc.sbuf_tensor(f"f0_den_{tag}", [128, 16], F32))
    rcpd_sb = st.enter_context(nc.sbuf_tensor(f"f0_rcpd_{tag}", [128, 16], F32))
    rw0_sb = st.enter_context(nc.sbuf_tensor(f"f0_rw0_{tag}", [128, 16], F32))
    rw1_sb = st.enter_context(nc.sbuf_tensor(f"f0_rw1_{tag}", [128, 16], F32))
    e0_sb = st.enter_context(nc.sbuf_tensor(f"f0_e0_{tag}", [128, 128], F32))
    e1_sb = st.enter_context(nc.sbuf_tensor(f"f0_e1_{tag}", [128, 128], F32))
    e0t = st.enter_context(nc.sbuf_tensor(f"f0_e0t_{tag}", [8, 512], BF16))
    e1t = st.enter_context(nc.sbuf_tensor(f"f0_e1t_{tag}", [8, 512], BF16))
    rwt = st.enter_context(nc.sbuf_tensor(f"f0_rwt_{tag}", [1, 1024], F32))
    rwb0 = st.enter_context(nc.sbuf_tensor(f"f0_rwb0_{tag}", [128, 512], F32))
    rwb1 = st.enter_context(nc.sbuf_tensor(f"f0_rwb1_{tag}", [128, 512], F32))
    em0 = st.enter_context(nc.sbuf_tensor(f"f0_em0_{tag}", [128, 1024], BF16))
    em1 = st.enter_context(nc.sbuf_tensor(f"f0_em1_{tag}", [128, 1024], BF16))
    ma1 = [st.enter_context(nc.sbuf_tensor(f"f0_ma1{k}_{tag}", [128, 1024], BF16)) for k in range(2)]
    ma3 = [st.enter_context(nc.sbuf_tensor(f"f0_ma3{k}_{tag}", [128, 1024], BF16)) for k in range(2)]
    a2c = st.enter_context(nc.sbuf_tensor(f"f0_a2c_{tag}", [128, 1024], BF16))
    wbf = [st.enter_context(nc.sbuf_tensor(f"f0_wb{i}_{tag}", [128, 8960], BF16)) for i in range(2)]
    xx0 = [st.enter_context(nc.sbuf_tensor(f"f0_xx0{i}_{tag}", [128, 512], BF16)) for i in range(2)]
    xx1 = [st.enter_context(nc.sbuf_tensor(f"f0_xx1{i}_{tag}", [128, 512], BF16)) for i in range(2)]
    xz = st.enter_context(nc.sbuf_tensor(f"f0_xz_{tag}", [128, NFS * 512], BF16))
    base1s = st.enter_context(nc.sbuf_tensor(f"f0_b1s_{tag}", [128, 512], F32))
    base3s = st.enter_context(nc.sbuf_tensor(f"f0_b3s_{tag}", [128, 512], F32))
    x1s = [st.enter_context(nc.sbuf_tensor(f"f0_x1s{k}_{tag}", [128, 512], F32)) for k in range(2)]
    x3s = [st.enter_context(nc.sbuf_tensor(f"f0_x3s{k}_{tag}", [128, 512], F32)) for k in range(2)]
    sil_s = [st.enter_context(nc.sbuf_tensor(f"f0_sil{k}_{tag}", [128, 512], F32)) for k in range(2)]
    a2acc = [[st.enter_context(nc.sbuf_tensor(f"f0_a2a{k}{i}_{tag}", [128, 1024], F32))
              for i in range(2)] for k in range(2)]
    gs = [st.enter_context(nc.sbuf_tensor(f"f0_gs{i}_{tag}", [128, 512], F32)) for i in range(2)]
    pA = st.enter_context(nc.psum_tensor(f"f0_pA_{tag}", [128, 1024], F32))
    pB0 = st.enter_context(nc.psum_tensor(f"f0_pB0_{tag}", [128, 1024], F32))
    pB1 = st.enter_context(nc.psum_tensor(f"f0_pB1_{tag}", [128, 1024], F32))
    pa2 = st.enter_context(nc.psum_tensor(f"f0_pa2_{tag}", [128, 1024], F32))
    pB = pB0
    pC = pB1
    pD = pa2

    sd = p.sem("dma_sync"); so = p.sem("dma_out")
    t1 = p.sem("t1"); t2 = p.sem("t2"); t3 = p.sem("t3"); t4 = p.sem("t4"); t5 = p.sem("t5")
    v1 = p.sem("v1"); v2 = p.sem("v2"); v3 = p.sem("v3"); v4 = p.sem("v4"); v5 = p.sem("v5")
    sc1 = p.sem("sc1"); sc2 = p.sem("sc2"); sc3 = p.sem("sc3")
    gp = p.sem("gp1")

    xn_src = xn_dram.rearrange("(kc p) t -> p kc t", p=128)
    p.step("sync", lambda e: e.dma_start(out=wg8_sb[:, :].rearrange("p (kc n) -> p kc n", n=8),
                                         in_=w["wg8"].rearrange("(kc p) n -> p kc n", p=128)), inc=(sd, 16))
    p.step("sync", lambda e: e.dma_start(out=b2w[:, :].rearrange("p (h n) -> p h n", h=2),
                                         in_=w["b2"].rearrange("(h p) n -> p h n", p=128)), inc=(sd, 16))
    res_tgt = p.rel(sd)


    for tcq in range(4):
        p.step("sync", lambda e, tcq=tcq: e.dma_start(
            out=xn_sb[:, :].rearrange("p (kc t) -> p kc t", kc=16),
            in_=xn_src[:, :, 512*tcq:512*(tcq+1)]),
            waits=[(t1, p.rel(t1))], inc=(sd, 16))
        xn_tgt_r = p.rel(sd)
        def rmm(e, tcq=tcq):
            last = None
            for t4_ in range(4):
                col = 32*tcq + 8*t4_
                for kc in range(16):
                    last = e.matmul(pA[:, col:col+8],
                                    xn_sb[:, 512*kc + 128*t4_: 512*kc + 128*(t4_+1)],
                                    wg8_sb[:, 8*kc:8*(kc+1)],
                                    start=(kc == 0), stop=(kc == 15))
            return last
        p.step("tensor", rmm, waits=[(sd, max(res_tgt, xn_tgt_r))], inc=(t1, 1))
    p.step("vector", lambda e: e.tensor_reduce(
        nmx[:, :], pA[:, 0:128].rearrange("p (g n) -> p g n", g=16), AX.X, ALU.max, negate=True),
        waits=[(t1, p.rel(t1))], inc=(v1, 1))
    def sexp(e):
        last = None
        for g in range(16):
            last = e.activation(out=u_sb[:, 8*g:8*(g+1)], in_=pA[:, 8*g:8*(g+1)],
                                func=AF.Exp, bias=nmx[:, g:g+1])
        return last
    p.step("scalar", sexp, waits=[(v1, p.rel(v1))], inc=(sc1, 1))
    p.step("vector", lambda e: e.tensor_reduce(
        nm1_sb[:, :], u_sb[:, :].rearrange("p (g n) -> p g n", g=16), AX.X, ALU.max, negate=True),
        waits=[(sc1, p.rel(sc1))], inc=(v1, 1))
    p.step("scalar", lambda e: e.mul(m1_sb[:, :], nm1_sb[:, :], -1.0),
           waits=[(v1, p.rel(v1))], inc=(sc1, 1))
    def vA(e):
        u3 = u_sb[:, :].rearrange("p (g n) -> p g n", g=16)
        e.tensor_tensor(r_t1[:, :].rearrange("p (g n) -> p g n", g=16), u3,
                        m1_sb[:, :].rearrange("p (g n) -> p g n", g=16)[:, :, 0:1].broadcast_to([128, 16, 8]),
                        ALU.is_ge)
        e.tensor_mul(r_t2[:, :], r_t1[:, :], consts.iotam99[:, :])
        e.tensor_scalar_add(r_t1[:, :], r_t2[:, :], 99.0)
        return e.tensor_reduce(nidx[:, 0:16], r_t1[:, :].rearrange("p (g n) -> p g n", g=16),
                               AX.X, ALU.min)
    p.step("vector", vA, waits=[(sc1, p.rel(sc1))], inc=(v1, 1))
    p.step("scalar", lambda e: e.copy(idx_sb[:, 0:16], nidx[:, 0:16]),
           waits=[(v1, p.rel(v1))], inc=(sc1, 1))
    def vB(e):
        e.tensor_tensor(e0_sb[:, :].rearrange("p (g n) -> p g n", g=16),
                        consts.iota8x4[:, :].rearrange("p (g n) -> p g n", g=16),
                        idx_sb[:, 0:16].rearrange("p (g n) -> p g n", g=16)[:, :, 0:1].broadcast_to([128, 16, 8]),
                        ALU.is_equal)
        e.tensor_scalar_mul(r_t1[:, :], e0_sb[:, :], -1e30)
        e.tensor_add(u1_sb[:, :], u_sb[:, :], r_t1[:, :])
        return e.tensor_reduce(nm2_sb[:, :], u1_sb[:, :].rearrange("p (g n) -> p g n", g=16),
                               AX.X, ALU.max, negate=True)
    p.step("vector", vB, waits=[(sc1, p.rel(sc1))], inc=(v1, 1))
    p.step("scalar", lambda e: e.mul(m2_sb[:, :], nm2_sb[:, :], -1.0),
           waits=[(v1, p.rel(v1))], inc=(sc1, 1))
    def vC(e):
        u3 = u1_sb[:, :].rearrange("p (g n) -> p g n", g=16)
        e.tensor_tensor(r_t1[:, :].rearrange("p (g n) -> p g n", g=16), u3,
                        m2_sb[:, :].rearrange("p (g n) -> p g n", g=16)[:, :, 0:1].broadcast_to([128, 16, 8]),
                        ALU.is_ge)
        e.tensor_mul(r_t2[:, :], r_t1[:, :], consts.iotam99[:, :])
        e.tensor_scalar_add(r_t1[:, :], r_t2[:, :], 99.0)
        return e.tensor_reduce(nidx[:, 16:32], r_t1[:, :].rearrange("p (g n) -> p g n", g=16),
                               AX.X, ALU.min)
    p.step("vector", vC, waits=[(sc1, p.rel(sc1))], inc=(v1, 1))
    p.step("scalar", lambda e: e.copy(idx_sb[:, 16:32], nidx[:, 16:32]),
           waits=[(v1, p.rel(v1))], inc=(sc1, 1))
    def vD(e):
        e.tensor_tensor(e1_sb[:, :].rearrange("p (g n) -> p g n", g=16),
                        consts.iota8x4[:, :].rearrange("p (g n) -> p g n", g=16),
                        idx_sb[:, 16:32].rearrange("p (g n) -> p g n", g=16)[:, :, 0:1].broadcast_to([128, 16, 8]),
                        ALU.is_equal)
        return e.tensor_add(negden[:, :], nm1_sb[:, :], nm2_sb[:, :])
    p.step("vector", vD, waits=[(sc1, p.rel(sc1))], inc=(v1, 1))
    def srcp(e):
        e.activation(out=rcpd_sb[:, :], in_=negden[:, :], func=AF.Ln, scale=-1.0)
        return e.activation(out=rcpd_sb[:, :], in_=rcpd_sb[:, :], func=AF.Exp, scale=-1.0)
    p.step("scalar", srcp, waits=[(v1, p.rel(v1))], inc=(sc1, 1))
    def vE(e):
        e.tensor_mul(rw0_sb[:, :], m1_sb[:, :], rcpd_sb[:, :])
        return e.tensor_mul(rw1_sb[:, :], m2_sb[:, :], rcpd_sb[:, :])
    p.step("vector", vE, waits=[(sc1, p.rel(sc1))], inc=(v1, 1))

    for tc in range(4):
        waits = [(t4, p.rel(t4))] if tc > 0 else [(t1, p.rel(t1))]
        p.step("sync", lambda e, tc=tc: e.dma_start(
            out=xn_sb[:, :].rearrange("p (kc t) -> p kc t", kc=16),
            in_=xn_src[:, :, 512*tc:512*(tc+1)]), waits=waits, inc=(sd, 16))
        xn_tgt = p.rel(sd)

        def ttr(e, tc=tc):
            for t4_ in range(4):
                g = 4*tc + t4_
                e.transpose(pB[0:8, 128*t4_:128*(t4_+1)], e0_sb[:, 8*g:8*(g+1)], consts.ident_f[:, :])
            for t4_ in range(4):
                g = 4*tc + t4_
                e.transpose(pB[0:8, 512 + 128*t4_: 512 + 128*(t4_+1)], e1_sb[:, 8*g:8*(g+1)], consts.ident_f[:, :])
            for t4_ in range(4):
                g = 4*tc + t4_
                e.transpose(pC[0:1, 128*t4_:128*(t4_+1)], rw0_sb[:, g:g+1], consts.ident_f[:, :])
            last = None
            for t4_ in range(4):
                g = 4*tc + t4_
                last = e.transpose(pC[0:1, 512 + 128*t4_: 512 + 128*(t4_+1)], rw1_sb[:, g:g+1], consts.ident_f[:, :])
            return last
        waits = [(v1, p.rel(v1)), (sd, xn_tgt)]
        if tc > 0:
            waits.append((v3, p.rel(v3)))
        p.step("tensor", ttr, waits=waits, inc=(t2, 1))
        def vevt(e):
            e.tensor_copy(e0t[:, :], pB[0:8, 0:512])
            e.tensor_copy(e1t[:, :], pB[0:8, 512:1024])
            return e.tensor_copy(rwt[:, :], pC[0:1, 0:1024])
        p.step("vector", vevt, waits=[(t2, p.rel(t2))], inc=(v2, 1))
        def tbc(e):
            e.matmul(pD[:, 0:512], consts.ones1_f[:, :], rwt[:, 0:512], start=True, stop=True)
            return e.matmul(pD[:, 512:1024], consts.ones1_f[:, :], rwt[:, 512:1024], start=True, stop=True)
        p.step("tensor", tbc, waits=[(v2, p.rel(v2))], inc=(t2, 1))
        def vevb(e):
            e.tensor_copy(rwb0[:, :], pD[:, 0:512])
            return e.tensor_copy(rwb1[:, :], pD[:, 512:1024])
        p.step("vector", vevb, waits=[(t2, p.rel(t2))], inc=(v2, 1))

        a1w = wbf[0][:, 0:4096]
        a3w = wbf[1][:, 0:4096]
        waits = []
        if tc > 0:
            waits.append((t3, p.rel(t3)))
            waits.append((t4, p.rel(t4)))
        def lda13(e):
            e.dma_start(out=wbf[0][:, 0:4096].rearrange("p (kc n) -> p kc n", n=256),
                        in_=w["a1"].rearrange("(kc p) n -> p kc n", p=128)).then_inc(sd.h, 16)
            return e.dma_start(out=wbf[1][:, 0:4096].rearrange("p (kc n) -> p kc n", n=256),
                               in_=w["a3"].rearrange("(kc p) n -> p kc n", p=128))
        p.step("sync", lda13, waits=waits, inc=(sd, 16), manual=((sd, 16),))
        a13_tgt = p.rel(sd)
        def tma(e):
            last = None
            for half in range(2):
                for kc in range(16):
                    last = e.matmul(pA[:, 512*half:512*half+512],
                                    a1w[:, 256*kc + 128*half: 256*kc + 128*(half+1)],
                                    xn_sb[:, 512*kc:512*(kc+1)], start=(kc == 0), stop=(kc == 15))
            for half in range(2):
                for kc in range(16):
                    last = e.matmul(pB[:, 512*half:512*half+512],
                                    a3w[:, 256*kc + 128*half: 256*kc + 128*(half+1)],
                                    xn_sb[:, 512*kc:512*(kc+1)], start=(kc == 0), stop=(kc == 15))
            e.matmul(pC[:, 0:512], consts.b32a[:, :], e0t[:, :], start=True, stop=True)
            e.matmul(pC[:, 512:1024], consts.b32b[:, :], e0t[:, :], start=True, stop=True)
            e.matmul(pD[:, 0:512], consts.b32a[:, :], e1t[:, :], start=True, stop=True)
            last = e.matmul(pD[:, 512:1024], consts.b32b[:, :], e1t[:, :], start=True, stop=True)
            return last
        p.step("tensor", tma, waits=[(v2, p.rel(v2)), (v1, p.rel(v1)), (sd, a13_tgt)], inc=(t3, 1))
        def vem(e):
            e.tensor_copy(em0[:, :], pC[:, :])
            e.tensor_copy(em1[:, :], pD[:, :])
            e.tensor_mul(ma1[0][:, :], pA[:, :], em0[:, :])
            e.tensor_mul(ma1[1][:, :], pA[:, :], em1[:, :])
            e.tensor_mul(ma3[0][:, :], pB[:, :], em0[:, :])
            return e.tensor_mul(ma3[1][:, :], pB[:, :], em1[:, :])
        p.step("vector", vem, waits=[(t3, p.rel(t3))], inc=(v3, 1))

        w1_src = w["w1"].rearrange("(kc p) f -> p kc f", p=128)
        w3_src = w["w3"].rearrange("(kc p) f -> p kc f", p=128)
        b1_src = w["b1"].rearrange("(h p) f -> p h f", p=128)
        b3_src = w["b3"].rearrange("(h p) f -> p h f", p=128)
        a2_src = w["a2"].rearrange("(fs p) n -> p fs n", p=128)
        fs_base_t1 = p.rel(t1); fs_base_sc1 = p.rel(sc1); fs_base_v4 = p.rel(v4)
        fs_base_t4 = p.rel(t4); fs_base_sc3 = p.rel(sc3); fs_base_v5 = p.rel(v5)
        fs_base_t2 = p.rel(t2); fs_base_t3 = p.rel(t3); fs_base_v2 = p.rel(v2)
        fs_base_v3 = p.rel(v3); fs_base_t5 = p.rel(t5)
        pend_a2 = []
        for fs in range(NFS):
            wb = wbf[fs % 2]
            waits = []
            if fs >= 2:
                waits.append((t3, fs_base_t3 + fs - 1))
            else:
                waits.append((t3, fs_base_t3))
            def ld(e, fs=fs, wb=wb):
                e.dma_start(out=wb[:, 0:2048].rearrange("p (kc n) -> p kc n", kc=16),
                            in_=w1_src[:, :, 128*fs:128*(fs+1)]).then_inc(sd.h, 16)
                e.dma_start(out=wb[:, 2048:4096].rearrange("p (kc n) -> p kc n", kc=16),
                            in_=w3_src[:, :, 128*fs:128*(fs+1)]).then_inc(sd.h, 16)
                e.dma_start(out=wb[:, 4096:4352].rearrange("p (h n) -> p h n", h=2),
                            in_=b1_src[:, :, 128*fs:128*(fs+1)]).then_inc(sd.h, 16)
                e.dma_start(out=wb[:, 4352:4608].rearrange("p (h n) -> p h n", h=2),
                            in_=b3_src[:, :, 128*fs:128*(fs+1)]).then_inc(sd.h, 16)
                return e.dma_start(out=wb[:, 4608:4864], in_=a2_src[:, fs, :])
            p.step("sync", ld, waits=waits, inc=(sd, 16), manual=((sd, 64),))
            ld_tgt = p.rel(sd)
            def s1a(e, fs=fs, wb=wb):
                last = None
                for kc in range(16):
                    e.matmul(pA[:, 0:512], wb[:, 128*kc:128*(kc+1)],
                             xn_sb[:, 512*kc:512*(kc+1)], start=(kc == 0), stop=(kc == 15))
                for kc in range(16):
                    last = e.matmul(pA[:, 512:1024], wb[:, 2048 + 128*kc: 2048 + 128*(kc+1)],
                                    xn_sb[:, 512*kc:512*(kc+1)], start=(kc == 0), stop=(kc == 15))
                return last
            waits = [(sd, ld_tgt), (v3, fs_base_v3)]
            if fs >= 1:
                waits.append((sc1, fs_base_sc1 + fs))
            p.step("tensor", s1a, waits=waits, inc=(t1, 1))
            def s1b(e, fs=fs, wb=wb):
                last = None
                for k, pBk, ma1k, ma3k in ((0, pB0, ma1[0], ma3[0]), (1, pB1, ma1[1], ma3[1])):
                    for h in range(2):
                        e.matmul(pBk[:, 0:512], wb[:, 4096 + 128*h: 4096 + 128*(h+1)],
                                 ma1k[:, 512*h:512*(h+1)], start=(h == 0), stop=(h == 1))
                    for h in range(2):
                        last = e.matmul(pBk[:, 512:1024], wb[:, 4352 + 128*h: 4352 + 128*(h+1)],
                                        ma3k[:, 512*h:512*(h+1)], start=(h == 0), stop=(h == 1))
                return last
            waits = []
            if fs >= 1:
                waits.append((v4, fs_base_v4 + fs))
            p.step("tensor", s1b, waits=waits, inc=(t5, 1))
            def s2(e):
                e.copy(base1s[:, :], pA[:, 0:512])
                return e.copy(base3s[:, :], pA[:, 512:1024])
            waits = [(t1, fs_base_t1 + fs + 1)]
            if fs >= 1:
                waits.append((v4, fs_base_v4 + fs))
            p.step("scalar", s2, waits=waits, inc=(sc1, 1))
            def s3(e):
                e.tensor_add(x1s[0][:, :], base1s[:, :], pB0[:, 0:512])
                e.tensor_add(x3s[0][:, :], base3s[:, :], pB0[:, 512:1024])
                e.tensor_add(x1s[1][:, :], base1s[:, :], pB1[:, 0:512])
                return e.tensor_add(x3s[1][:, :], base3s[:, :], pB1[:, 512:1024])
            waits = [(sc1, fs_base_sc1 + fs + 1), (t5, fs_base_t5 + fs + 1)]
            if fs >= 1:
                waits.append((sc3, fs_base_sc3 + fs))
            p.step("vector", s3, waits=waits, inc=(v4, 1))
            def s4(e):
                e.activation(out=sil_s[0][:, :], in_=x1s[0][:, :], func=AF.Silu)
                return e.activation(out=sil_s[1][:, :], in_=x1s[1][:, :], func=AF.Silu)
            p.step("scalar", s4, waits=[(v4, fs_base_v4 + fs + 1)], inc=(sc3, 1))
            xpar = fs % 2
            def s5(e, fs=fs, xpar=xpar):
                e.tensor_mul(x1s[0][:, :], sil_s[0][:, :], x3s[0][:, :])
                e.tensor_mul(xx0[xpar][:, :], x1s[0][:, :], rwb0[:, :])
                e.tensor_mul(x1s[1][:, :], sil_s[1][:, :], x3s[1][:, :])
                e.tensor_mul(xx1[xpar][:, :], x1s[1][:, :], rwb1[:, :])
                return e.tensor_add(xz[:, 512*fs:512*(fs+1)], xx0[xpar][:, :], xx1[xpar][:, :])
            waits = [(sc3, fs_base_sc3 + fs + 1)]
            if fs >= 2:
                waits.append((t3, fs_base_t3 + fs - 1))
            p.step("vector", s5, waits=waits, inc=(v5, 1))
            for fn in pend_a2:
                fn()
            def mk_tail(fs=fs, wb=wb, xpar=xpar):
                def emit():
                    def s6(e, fs=fs, wb=wb, xpar=xpar):
                        last = None
                        for h in range(2):
                            last = e.matmul(pa2[:, 512*h:512*h+512],
                                            wb[:, 4608 + 128*h: 4608 + 128*(h+1)],
                                            xx0[xpar][:, :], start=True, stop=True)
                        return last
                    waits = [(v5, fs_base_v5 + fs + 1)]
                    if fs >= 1:
                        waits.append((v3, fs_base_v3 + fs))
                    p.step("tensor", s6, waits=waits, inc=(t2, 1))
                    def s7(e, fs=fs):
                        dst = a2acc[0][fs % 2]
                        if fs == 0:
                            return e.tensor_scalar(dst[:, :], pa2[:, :], 1.0, None, ALU.mult)
                        return e.tensor_add(dst[:, :], a2acc[0][1 - fs % 2][:, :], pa2[:, :])
                    p.step("vector", s7, waits=[(t2, fs_base_t2 + fs + 1)], inc=(v2, 1))
                    def s8(e, fs=fs, wb=wb, xpar=xpar):
                        last = None
                        for h in range(2):
                            last = e.matmul(pa2[:, 512*h:512*h+512],
                                            wb[:, 4608 + 128*h: 4608 + 128*(h+1)],
                                            xx1[xpar][:, :], start=True, stop=True)
                        return last
                    p.step("tensor", s8, waits=[(v2, fs_base_v2 + fs + 1)], inc=(t3, 1))
                    def s9(e, fs=fs):
                        dst = a2acc[1][fs % 2]
                        if fs == 0:
                            return e.tensor_scalar(dst[:, :], pa2[:, :], 1.0, None, ALU.mult)
                        return e.tensor_add(dst[:, :], a2acc[1][1 - fs % 2][:, :], pa2[:, :])
                    p.step("vector", s9, waits=[(t3, fs_base_t3 + fs + 1)], inc=(v3, 1))
                return emit
            pend_a2 = [mk_tail()]
        for fn in pend_a2:
            fn()
        pend_a2 = []

        def va2(e):
            f0 = a2acc[0][(NFS - 1) % 2]; f1 = a2acc[1][(NFS - 1) % 2]
            e.tensor_mul(x1s[0][:, :], f0[:, 0:512], em0[:, 0:512])
            e.tensor_mul(x1s[1][:, :], f1[:, 0:512], em1[:, 0:512])
            e.tensor_add(a2c[:, 0:512], x1s[0][:, :], x1s[1][:, :])
            e.tensor_mul(x3s[0][:, :], f0[:, 512:1024], em0[:, 512:1024])
            e.tensor_mul(x3s[1][:, :], f1[:, 512:1024], em1[:, 512:1024])
            return e.tensor_add(a2c[:, 512:1024], x3s[0][:, :], x3s[1][:, :])
        p.step("vector", va2, waits=[(v3, p.rel(v3))], inc=(v3, 1))

        w2_src = w["w2"].rearrange("(fs p) n -> p fs n", p=128)
        for oc in range(16):
            wb = wbf[oc % 2]
            waits = [(v5, p.rel(v5))]
            if oc < 2:
                waits.append((t3, p.rel(t3)))
            else:
                waits.append((t4, p.rel(t4) - 1))
            p.step("sync", lambda e, oc=oc, wb=wb: e.dma_start(
                out=wb[:, 0:NFS*128].rearrange("p (fs n) -> p fs n", n=128),
                in_=w2_src[:, :, 128*oc:128*(oc+1)]), waits=waits, inc=(sd, 16))
            wd_tgt = p.rel(sd)
            def dmm2(e, oc=oc, wb=wb):
                par = oc % 2
                last = None
                for fs in range(NFS):
                    last = e.matmul(pA[:, 512*par:512*par+512], wb[:, 128*fs:128*(fs+1)],
                                    xz[:, 512*fs:512*(fs+1)], start=(fs == 0), stop=False)
                last = e.matmul(pA[:, 512*par:512*par+512],
                                b2w[:, 2048*0 + 128*oc: 2048*0 + 128*(oc+1)],
                                a2c[:, 0:512], start=False, stop=False)
                last = e.matmul(pA[:, 512*par:512*par+512],
                                b2w[:, 2048*1 + 128*oc: 2048*1 + 128*(oc+1)],
                                a2c[:, 512:1024], start=False, stop=True)
                return last
            waits = [(sd, wd_tgt), (v3, p.rel(v3)), (sc1, p.rel(sc1))]
            if oc >= 2:
                waits.append((sc2, p.rel(sc2) - 1))
            p.step("tensor", dmm2, waits=waits, inc=(t4, 1))
            def dev(e, oc=oc):
                return e.activation(out=gs[oc % 2][:, :], in_=pA[:, 512*(oc % 2):512*(oc % 2)+512],
                                    func=AF.Silu)
            waits = [(t4, p.rel(t4))]
            if p.rel(so) >= 32:
                waits.append((so, p.rel(so) - 16))
            p.step("scalar", dev, waits=waits, inc=(sc2, 1))
            p.step("sync", lambda e, oc=oc, tc=tc: e.dma_start(
                out=g_fm_out[128*oc:128*(oc+1), 512*tc:512*(tc+1)], in_=gs[oc % 2][:, :]),
                waits=[(sc2, p.rel(sc2))], inc=(so, 16))
    p.step("sync", lambda e: e.nop(), waits=[(so, p.rel(so))])
    return p


def make_combine(ctx, consts, tag, x_fm, s_fm, g_fm, out_fm):
    nc, st = ctx.nc, ctx.stack
    p = Plan(ctx, f"comb_{tag}")
    xb = [st.enter_context(nc.sbuf_tensor(f"cb_x{i}_{tag}", [128, 2048], F32)) for i in range(2)]
    sb = [st.enter_context(nc.sbuf_tensor(f"cb_s{i}_{tag}", [128, 2048], F32)) for i in range(2)]
    gb = [st.enter_context(nc.sbuf_tensor(f"cb_g{i}_{tag}", [128, 2048], F32)) for i in range(2)]
    ob = [st.enter_context(nc.sbuf_tensor(f"cb_o{i}_{tag}", [128, 2048], F32)) for i in range(2)]
    tb = [st.enter_context(nc.sbuf_tensor(f"cb_t{i}_{tag}", [128, 2048], F32)) for i in range(2)]
    sd = p.sem("dma_sync"); sg = p.sem("dma_gp"); so = p.sem("dma_out"); v1 = p.sem("v1")
    for kc in range(16):
        par = kc % 2
        waits = []
        if kc >= 2:
            waits.append((v1, p.rel(v1) - 1))
        def ld(e, kc=kc, par=par):
            e.dma_start(out=xb[par][:, :], in_=x_fm[128*kc:128*(kc+1), :]).then_inc(sd.h, 16)
            return e.dma_start(out=sb[par][:, :], in_=s_fm[128*kc:128*(kc+1), :])
        p.step("sync", ld, waits=waits, inc=(sd, 16), manual=((sd, 16),))
        p.step("gpsimd", lambda e, kc=kc, par=par: e.dma_start(
            out=gb[par][:, :], in_=g_fm[128*kc:128*(kc+1), :]), waits=waits, inc=(sg, 16))
        def cmb(e, par=par):
            e.tensor_mul(tb[par][:, :], sb[par][:, :], gb[par][:, :])
            return e.tensor_add(ob[par][:, :], tb[par][:, :], xb[par][:, :])
        waits2 = [(sd, p.rel(sd)), (sg, p.rel(sg))]
        if kc >= 2:
            waits2.append((so, p.rel(so) - 16))
        p.step("vector", cmb, waits=waits2, inc=(v1, 1))
        p.step("sync", lambda e, kc=kc, par=par: e.dma_start(
            out=out_fm[128*kc:128*(kc+1), :], in_=ob[par][:, :]),
            waits=[(v1, p.rel(v1))], inc=(so, 16))
    p.step("sync", lambda e: e.nop(), waits=[(so, p.rel(so))])
    return p


def make_precast(ctx, tag, pairs, wait=True):
    p = Plan(ctx, f"precast_{tag}")
    sg = p.sem("gp2")
    p.targets = {}
    for (src, dst) in pairs:
        p.step("gpsimd", lambda e, src=src, dst=dst: e.dma_start(out=dst, in_=src), inc=(sg, 16))
        p.targets[id(dst)] = ctx._sems["gp2"].n + p.rel(sg)
    if wait:
        p.step("gpsimd", lambda e: e.nop(), waits=[(sg, p.rel(sg))])
    return p


class Fence(PhaseBase):
    def __init__(self, ctx, sem_name, absolute):
        self.name = f"fence_{sem_name}_{absolute}"
        self.s = ctx.sem(sem_name)
        self.absolute = absolute
        self._totals = {}
    def sync(self, e):
        e.wait_ge(self.s.h, self.absolute)


def make_allgather(ctx, tag, in_ap, out_ap, wait=True):
    p = Plan(ctx, f"ag_{tag}")
    cc = p.sem("cc1")
    p.step("gpsimd", lambda e: e.collective_compute(
        "AllGather", mybir.AluOpType.bypass,
        replica_groups=[[0, 1], [2, 3], [4, 5], [6, 7]],
        ins=[in_ap], outs=[out_ap]), inc=(cc, 1))
    p.target = ctx._sems["cc1"].n + 1
    if wait:
        p.step("gpsimd", lambda e: e.nop(), waits=[(cc, p.rel(cc))])
    return p
from contextlib import ExitStack
from concourse.bass_utils import run_bass_kernel_spmd

POS_H = [
    np.concatenate([np.arange(256*i, 256*i+128) for i in range(16)]),
    np.concatenate([np.arange(256*i+128, 256*i+256) for i in range(16)]),
]
_CM = np.where(np.arange(128)[None, :] > np.arange(128)[:, None],
               np.float32(-1e30), np.float32(0.0))
AMASK_H = [np.concatenate([_CM, np.full((128, 128), -1e30, np.float32)], 1),
           np.concatenate([np.zeros((128, 128), np.float32), _CM], 1)]


def _tabs(pos):
    freqs = 1.0 / 10000.0 ** (np.arange(64, dtype=np.float64) / 64)
    ang = pos[:, None].astype(np.float64) * freqs[None, :]
    c = np.cos(ang).astype(np.float32)
    s = np.sin(ang).astype(np.float32)
    cd = np.ascontiguousarray(c.reshape(16, 128, 64).transpose(1, 0, 2).reshape(128, 1024))
    sd_ = np.ascontiguousarray(s.reshape(16, 128, 64).transpose(1, 0, 2).reshape(128, 1024))
    lp = np.ascontiguousarray((np.log(pos + 1.0) / np.sqrt(128.0)).astype(np.float32)[None, :])
    return cd, sd_, lp


def _cin_dram(nc, cin):
    return {k: nc.dram_tensor("cc_" + k, list(v.shape), BF16 if v.dtype == BF else F32,
                              kind="ExternalInput").ap() for k, v in cin.items()}


def _f32in(nc, name, shape):
    return nc.dram_tensor(name, list(shape), F32, kind="ExternalInput").ap()


def _precast_pair(nc, name, shape):
    s = nc.dram_tensor(name + "_f", list(shape), F32, kind="ExternalInput").ap()
    d = nc.dram_tensor(name + "_b", list(shape), BF16, kind="Internal").ap()
    return s, d


def _tabs_phase(ctx, cosd, sind, cos_sb, sin_sb):
    pl = Plan(ctx, "tabs")
    sd = pl.sem("dma_sync")
    pl.step("sync", lambda e: e.dma_start(out=cos_sb[:, :], in_=cosd), inc=(sd, 16))
    pl.step("sync", lambda e: e.dma_start(out=sin_sb[:, :], in_=sind), inc=(sd, 16))
    pl.step("sync", lambda e: e.nop(), waits=[(sd, pl.rel(sd))])
    return pl


def _build_launch_a(cin):
    nc = bass.Bass("TRN2", target_bir_lowering=False, debug=False)
    x = _f32in(nc, "x", [2048, 2048])
    wq = _precast_pair(nc, "wq", [2048, 2048])
    wk = _precast_pair(nc, "wk", [2048, 2048])
    wv = _precast_pair(nc, "wv", [2048, 2048])
    cosd = _f32in(nc, "cosd", [128, 1024]); sind = _f32in(nc, "sind", [128, 1024])
    logpos = _f32in(nc, "logpos", [1, 2048]); scaler = _f32in(nc, "scaler", [1, 16])
    xn0 = nc.dram_tensor("xn0", [2048, 2048], BF16, kind="ExternalOutput").ap()
    q0 = nc.dram_tensor("q0", [2048, 2048], BF16, kind="ExternalOutput").ap()
    kv0 = nc.dram_tensor("kv0", [4096, 2048], BF16, kind="ExternalOutput").ap()
    caps = _cin_dram(nc, cin)
    with ExitStack() as stack:
        ctx = Ctx(nc, stack)
        consts = ConstPool(ctx, caps)
        cos_sb = stack.enter_context(nc.sbuf_tensor("cos_sb", [128, 1024], F32))
        sin_sb = stack.enter_context(nc.sbuf_tensor("sin_sb", [128, 1024], F32))
        qscale_sb = stack.enter_context(nc.sbuf_tensor("qscale_sb", [128, 256], F32))
        ctx.run_phase(consts)
        ctx.run_phase(make_precast(ctx, "a", [wq, wk, wv]))
        ctx.run_phase(_tabs_phase(ctx, cosd, sind, cos_sb, sin_sb))
        with ctx.scope():
            ctx.run_phase(make_qscale_phase(ctx, consts, logpos, scaler, qscale_sb))
        with ctx.scope():
            ctx.run_phase(LNPhase(ctx, consts, x, xn0, "l0"))
        with ctx.scope():
            jobs = [{"w": wq[1], "out": q0, "mode": "q0"},
                    {"w": wk[1], "out": kv0[0:2048, :], "mode": "k0"},
                    {"w": wv[1], "out": kv0[2048:4096, :], "mode": "v"}]
            ctx.run_phase(make_qkv_phase(ctx, consts, "a", xn0, jobs, cos_sb, sin_sb, qscale_sb))
    return nc


def _build_launch_b(cin):
    nc = bass.Bass("TRN2", target_bir_lowering=False, debug=False)
    x = _f32in(nc, "x", [2048, 2048])
    xn0 = nc.dram_tensor("xn0", [2048, 2048], BF16, kind="ExternalInput").ap()
    q0 = nc.dram_tensor("q0", [2048, 2048], BF16, kind="ExternalInput").ap()
    ktm0 = nc.dram_tensor("ktm0", [4096, 2048], BF16, kind="ExternalInput").ap()
    vtm0 = nc.dram_tensor("vtm0", [4096, 2048], BF16, kind="ExternalInput").ap()
    lam = [_f32in(nc, f"lam{i}", [1, 64]) for i in range(4)]
    amask = _f32in(nc, "amask", [128, 256])
    cosd = _f32in(nc, "cosd", [128, 1024]); sind = _f32in(nc, "sind", [128, 1024])
    shapes0 = {"wg8": [2048, 8], "w1": [2048, 7168], "w3": [2048, 7168], "w2": [7168, 2048],
               "a1": [2048, 256], "a3": [2048, 256], "b1": [256, 7168], "b3": [256, 7168],
               "a2": [7168, 256], "b2": [256, 2048]}
    wp = {k: _precast_pair(nc, k, v) for k, v in shapes0.items()}
    wq1 = _precast_pair(nc, "wq1", [2048, 2048])
    wk1 = _precast_pair(nc, "wk1", [2048, 2048])
    wv1 = _precast_pair(nc, "wv1", [2048, 2048])
    x1 = nc.dram_tensor("x1", [2048, 2048], F32, kind="ExternalOutput").ap()
    xn1 = nc.dram_tensor("xn1", [2048, 2048], BF16, kind="ExternalOutput").ap()
    q1 = nc.dram_tensor("q1", [2048, 2048], BF16, kind="ExternalOutput").ap()
    kv1 = nc.dram_tensor("kv1", [4096, 2048], BF16, kind="ExternalOutput").ap()
    s0 = nc.dram_tensor("s0", [2048, 2048], F32, kind="Internal").ap()
    g0 = nc.dram_tensor("g0", [2048, 2048], F32, kind="Internal").ap()
    caps = _cin_dram(nc, cin)
    with ExitStack() as stack:
        ctx = Ctx(nc, stack)
        consts = ConstPool(ctx, caps)
        cos_sb = stack.enter_context(nc.sbuf_tensor("cos_sb", [128, 1024], F32))
        sin_sb = stack.enter_context(nc.sbuf_tensor("sin_sb", [128, 1024], F32))
        neglam = stack.enter_context(nc.sbuf_tensor("neglam", [128, 1], F32))
        ctx.run_phase(consts)
        ctx.run_phase(make_precast(ctx, "b", list(wp.values()) + [wq1, wk1, wv1]))
        ctx.run_phase(_tabs_phase(ctx, cosd, sind, cos_sb, sin_sb))
        with ctx.scope():
            ctx.run_phase(make_lambda_phase(ctx, consts, lam, neglam))
        with ctx.scope():
            ctx.run_phase(make_attn_l0(ctx, consts, "b", q0, ktm0, vtm0, amask, neglam, s0))
        with ctx.scope():
            ctx.run_phase(make_ffn_l0(ctx, consts, "b", xn0,
                                      {k: v[1] for k, v in wp.items()}, g0))
        with ctx.scope():
            ctx.run_phase(make_combine(ctx, consts, "b", x, s0, g0, x1))
        with ctx.scope():
            ctx.run_phase(LNPhase(ctx, consts, x1, xn1, "l1"))
        with ctx.scope():
            jobs = [{"w": wq1[1], "out": q1, "mode": "q1"},
                    {"w": wk1[1], "out": kv1[0:2048, :], "mode": "k1"},
                    {"w": wv1[1], "out": kv1[2048:4096, :], "mode": "v"}]
            ctx.run_phase(make_qkv_phase(ctx, consts, "b", xn1, jobs, cos_sb, sin_sb, None))
    return nc


def _build_launch_c(cin):
    nc = bass.Bass("TRN2", target_bir_lowering=False, debug=False)
    x1 = _f32in(nc, "x1", [2048, 2048])
    xn1 = nc.dram_tensor("xn1", [2048, 2048], BF16, kind="ExternalInput").ap()
    q1 = nc.dram_tensor("q1", [2048, 2048], BF16, kind="ExternalInput").ap()
    ktm1 = nc.dram_tensor("ktm1", [4096, 2048], BF16, kind="ExternalInput").ap()
    vtm1 = nc.dram_tensor("vtm1", [4096, 2048], BF16, kind="ExternalInput").ap()
    amask = _f32in(nc, "amask", [128, 256])
    wo = _precast_pair(nc, "wo", [2048, 2048])
    wgte = _precast_pair(nc, "wgte", [2048, 8192])
    wup = _precast_pair(nc, "wup", [2048, 8192])
    wdn = _precast_pair(nc, "wdn", [8192, 2048])
    out = nc.dram_tensor("out", [2048, 2048], F32, kind="ExternalOutput").ap()
    s1 = nc.dram_tensor("s1", [2048, 2048], F32, kind="Internal").ap()
    g1 = nc.dram_tensor("g1", [2048, 2048], F32, kind="Internal").ap()
    caps = _cin_dram(nc, cin)
    with ExitStack() as stack:
        ctx = Ctx(nc, stack)
        consts = ConstPool(ctx, caps)
        ctx.run_phase(consts)
        ctx.run_phase(make_precast(ctx, "c", [wo, wgte, wup, wdn]))
        with ctx.scope():
            o_fm = ctx.stack.enter_context(nc.sbuf_tensor("o_fm", [128, 16 * 2048], BF16))
            with ctx.scope():
                ctx.run_phase(make_attn_l1(ctx, consts, "c", q1, ktm1, vtm1, amask, o_fm))
            with ctx.scope():
                ctx.run_phase(make_wo_phase(ctx, consts, "c", o_fm, wo[1], s1))
        with ctx.scope():
            ctx.run_phase(make_ffn_l1(ctx, consts, "c", xn1, wgte[1], wup[1], wdn[1], g1))
        with ctx.scope():
            ctx.run_phase(make_combine(ctx, consts, "c", x1, s1, g1, out))
    return nc


_cache = {}


def kernel(**inputs):
    cin = host_consts()
    cink = {"cc_" + k: v for k, v in cin.items()}
    C = np.ascontiguousarray

    x = np.asarray(inputs["L0_attn_x"], np.float32)
    wq_d = C(np.asarray(inputs["L0_attn_Wq"], np.float32))
    wk_d = C(np.asarray(inputs["L0_attn_Wk"], np.float32))
    wv_d = C(np.asarray(inputs["L0_attn_Wv"], np.float32))
    scaler_d = C(np.asarray(inputs["L0_attn_softmax_scaler"], np.float32).reshape(1, 16))
    lam_d = [C(np.asarray(inputs[k], np.float32).reshape(1, 64)) for k in
             ("L0_attn_lambda_q1", "L0_attn_lambda_k1", "L0_attn_lambda_q2", "L0_attn_lambda_k2")]
    wg8_d = C(np.asarray(inputs["L0_ffn_Wg"], np.float32).T)
    w1_d = C(np.asarray(inputs["L0_ffn_W1"], np.float32).T)
    w3_d = C(np.asarray(inputs["L0_ffn_W3"], np.float32).T)
    w2_d = C(np.asarray(inputs["L0_ffn_W2"], np.float32).T)
    a1_d = C(np.asarray(inputs["L0_ffn_A1"], np.float32).transpose(2, 0, 1).reshape(2048, 256))
    a3_d = C(np.asarray(inputs["L0_ffn_A3"], np.float32).transpose(2, 0, 1).reshape(2048, 256))
    b1_d = C(np.asarray(inputs["L0_ffn_B1"], np.float32).transpose(0, 2, 1).reshape(256, 7168))
    b3_d = C(np.asarray(inputs["L0_ffn_B3"], np.float32).transpose(0, 2, 1).reshape(256, 7168))
    a2_d = C(np.asarray(inputs["L0_ffn_A2"], np.float32).transpose(2, 0, 1).reshape(7168, 256))
    b2_d = C(np.asarray(inputs["L0_ffn_B2"], np.float32).transpose(0, 2, 1).reshape(256, 2048))
    wq1_d = C(np.asarray(inputs["L1_attn_wq"], np.float32).reshape(2048, 2048))
    wk1_d = C(np.asarray(inputs["L1_attn_wk"], np.float32).reshape(2048, 2048))
    wv1_d = C(np.asarray(inputs["L1_attn_wv"], np.float32).reshape(2048, 2048))
    wo_d = C(np.asarray(inputs["L1_attn_wo"], np.float32).reshape(2048, 2048))
    wg_d = C(np.asarray(inputs["L1_ffn_w_gate"], np.float32).reshape(2048, 16, 128, 4)
             .transpose(0, 1, 3, 2).reshape(2048, 8192))
    wu_d = C(np.asarray(inputs["L1_ffn_w_up"], np.float32).reshape(2048, 16, 128, 4)
             .transpose(0, 1, 3, 2).reshape(2048, 8192))
    wd_d = C(np.asarray(inputs["L1_ffn_w_down"], np.float32).reshape(16, 128, 4, 2048)
             .transpose(0, 2, 1, 3).reshape(8192, 2048))

    tabs = [_tabs(POS_H[0]), _tabs(POS_H[1])]
    x_core = [C(x[c // 2][POS_H[c % 2]].T) for c in range(8)]

    if FUSED:
        wdevs = {"wq_f": wq_d, "wk_f": wk_d, "wv_f": wv_d,
                 "wg8_f": wg8_d, "w1_f": w1_d, "w3_f": w3_d, "w2_f": w2_d,
                 "a1_f": a1_d, "a3_f": a3_d, "b1_f": b1_d, "b3_f": b3_d,
                 "a2_f": a2_d, "b2_f": b2_d,
                 "wq1_f": wq1_d, "wk1_f": wk1_d, "wv1_f": wv1_d,
                 "wo_f": wo_d, "wgte_f": wg_d, "wup_f": wu_d, "wdn_f": wd_d}
        return kernel_fused(inputs, cin, cink, (x_core, tabs, wdevs, scaler_d, lam_d))

    if "A" not in _cache:
        _cache["A"] = _build_launch_a(cin)
    in_maps = []
    for c in range(8):
        h = c % 2
        in_maps.append({"x": x_core[c], "wq_f": wq_d, "wk_f": wk_d, "wv_f": wv_d,
                        "cosd": tabs[h][0], "sind": tabs[h][1], "logpos": tabs[h][2],
                        "scaler": scaler_d, **cink})
    ra = run_bass_kernel_spmd(_cache["A"], in_maps, core_ids=list(range(8)))
    kt0, vt0 = [], []
    for b in range(4):
        k_full = np.concatenate([ra.results[2*b]["kv0"][0:2048], ra.results[2*b+1]["kv0"][0:2048]], 0)
        v_full = np.concatenate([ra.results[2*b]["kv0"][2048:4096], ra.results[2*b+1]["kv0"][2048:4096]], 0)
        kt0.append(k_full); vt0.append(v_full)

    if "B" not in _cache:
        _cache["B"] = _build_launch_b(cin)
    in_maps = []
    for c in range(8):
        b, h = c // 2, c % 2
        in_maps.append({"x": x_core[c], "xn0": ra.results[c]["xn0"], "q0": ra.results[c]["q0"],
                        "ktm0": kt0[b], "vtm0": vt0[b], "amask": AMASK_H[h],
                        **{f"lam{i}": lam_d[i] for i in range(4)},
                        "cosd": tabs[h][0], "sind": tabs[h][1],
                        "wg8_f": wg8_d, "w1_f": w1_d, "w3_f": w3_d, "w2_f": w2_d,
                        "a1_f": a1_d, "a3_f": a3_d, "b1_f": b1_d, "b3_f": b3_d,
                        "a2_f": a2_d, "b2_f": b2_d,
                        "wq1_f": wq1_d, "wk1_f": wk1_d, "wv1_f": wv1_d, **cink})
    resb = run_bass_kernel_spmd(_cache["B"], in_maps, core_ids=list(range(8)))
    rb = list(resb.results)

    kt1, vt1 = [], []
    for b in range(4):
        kt1.append(np.concatenate([rb[2*b]["kv1"][0:2048], rb[2*b+1]["kv1"][0:2048]], 0))
        vt1.append(np.concatenate([rb[2*b]["kv1"][2048:4096], rb[2*b+1]["kv1"][2048:4096]], 0))

    if "C" not in _cache:
        _cache["C"] = _build_launch_c(cin)
    ims = []
    for c in range(8):
        b, h = c // 2, c % 2
        ims.append({"x1": rb[c]["x1"], "xn1": rb[c]["xn1"], "q1": rb[c]["q1"],
                    "ktm1": kt1[b], "vtm1": vt1[b], "amask": AMASK_H[h],
                    "wo_f": wo_d, "wgte_f": wg_d, "wup_f": wu_d, "wdn_f": wd_d, **cink})
    resc = run_bass_kernel_spmd(_cache["C"], ims, core_ids=list(range(8)))
    rc = list(resc.results)

    out = np.empty((4, 4096, 2048), np.float32)
    for c in range(8):
        b, h = c // 2, c % 2
        out[b][POS_H[h]] = rc[c]["out"].T
    return out


def _build_fused(cin):
    nc = bass.Bass("TRN2", target_bir_lowering=False, debug=False)
    x = _f32in(nc, "x", [2048, 2048])
    wq = _precast_pair(nc, "wq", [2048, 2048])
    wk = _precast_pair(nc, "wk", [2048, 2048])
    wv = _precast_pair(nc, "wv", [2048, 2048])
    cosd = _f32in(nc, "cosd", [128, 1024]); sind = _f32in(nc, "sind", [128, 1024])
    logpos = _f32in(nc, "logpos", [1, 2048]); scaler = _f32in(nc, "scaler", [1, 16])
    lam = [_f32in(nc, f"lam{i}", [1, 64]) for i in range(4)]
    amask = _f32in(nc, "amask", [128, 256])
    shapes0 = {"wg8": [2048, 8], "w1": [2048, 7168], "w3": [2048, 7168], "w2": [7168, 2048],
               "a1": [2048, 256], "a3": [2048, 256], "b1": [256, 7168], "b3": [256, 7168],
               "a2": [7168, 256], "b2": [256, 2048]}
    wp = {k: _precast_pair(nc, k, v) for k, v in shapes0.items()}
    wq1 = _precast_pair(nc, "wq1", [2048, 2048])
    wk1 = _precast_pair(nc, "wk1", [2048, 2048])
    wv1 = _precast_pair(nc, "wv1", [2048, 2048])
    wo = _precast_pair(nc, "wo", [2048, 2048])
    wgte = _precast_pair(nc, "wgte", [2048, 8192])
    wup = _precast_pair(nc, "wup", [2048, 8192])
    wdn = _precast_pair(nc, "wdn", [8192, 2048])
    out = nc.dram_tensor("out", [2048, 2048], F32, kind="ExternalOutput").ap()
    xn0 = nc.dram_tensor("xn0", [2048, 2048], BF16, kind="Internal").ap()
    q0 = nc.dram_tensor("q0", [2048, 2048], BF16, kind="Internal").ap()
    kv0 = nc.dram_tensor("kv0", [4096, 2048], BF16, kind="Internal").ap()
    kvf0 = nc.dram_tensor("kvf0", [8192, 2048], BF16, kind="Internal").ap()
    s0 = nc.dram_tensor("s0", [2048, 2048], F32, kind="Internal").ap()
    g0 = nc.dram_tensor("g0", [2048, 2048], F32, kind="Internal").ap()
    x1 = nc.dram_tensor("x1", [2048, 2048], F32, kind="Internal").ap()
    xn1 = nc.dram_tensor("xn1", [2048, 2048], BF16, kind="Internal").ap()
    q1 = nc.dram_tensor("q1", [2048, 2048], BF16, kind="Internal").ap()
    kv1 = nc.dram_tensor("kv1", [4096, 2048], BF16, kind="Internal").ap()
    kvf1 = nc.dram_tensor("kvf1", [8192, 2048], BF16, kind="Internal").ap()
    s1 = nc.dram_tensor("s1", [2048, 2048], F32, kind="Internal").ap()
    g1 = nc.dram_tensor("g1", [2048, 2048], F32, kind="Internal").ap()
    caps = _cin_dram(nc, cin)
    with ExitStack() as stack:
        ctx = Ctx(nc, stack)
        consts = ConstPool(ctx, caps)
        cos_sb = stack.enter_context(nc.sbuf_tensor("cos_sb", [128, 1024], F32))
        sin_sb = stack.enter_context(nc.sbuf_tensor("sin_sb", [128, 1024], F32))
        qscale_sb = stack.enter_context(nc.sbuf_tensor("qscale_sb", [128, 256], F32))
        neglam = stack.enter_context(nc.sbuf_tensor("neglam", [128, 1], F32))
        ctx.run_phase(consts)
        pc = make_precast(ctx, "all", [wq, wk, wv] + list(wp.values())
                          + [wq1, wk1, wv1, wo, wgte, wup, wdn], wait=False)
        ctx.run_phase(pc)
        ctx.run_phase(_tabs_phase(ctx, cosd, sind, cos_sb, sin_sb))
        with ctx.scope():
            ctx.run_phase(make_qscale_phase(ctx, consts, logpos, scaler, qscale_sb))
        with ctx.scope():
            ctx.run_phase(make_lambda_phase(ctx, consts, lam, neglam))
        with ctx.scope():
            ctx.run_phase(LNPhase(ctx, consts, x, xn0, "l0"))
        ctx.run_phase(Fence(ctx, "gp2", pc.targets[id(wv[1])]))
        with ctx.scope():
            jobs = [{"w": wq[1], "out": q0, "mode": "q0"},
                    {"w": wk[1], "out": kv0[0:2048, :], "mode": "k0"},
                    {"w": wv[1], "out": kv0[2048:4096, :], "mode": "v"}]
            ctx.run_phase(make_qkv_phase(ctx, consts, "a", xn0, jobs, cos_sb, sin_sb, qscale_sb))
        ag0 = make_allgather(ctx, "kv0", kv0, kvf0, wait=True)
        ctx.run_phase(ag0)
        ctx.run_phase(Fence(ctx, "gp2", pc.targets[id(wp["b2"][1])]))
        with ctx.scope():
            ctx.run_phase(make_ffn_l0(ctx, consts, "b", xn0,
                                      {k: v[1] for k, v in wp.items()}, g0))
        ctx.run_phase(Fence(ctx, "cc1", ag0.target))
        with ctx.scope():
            kblk = [kvf0[0:2048, :], kvf0[4096:6144, :]]
            vblk = [kvf0[2048:4096, :], kvf0[6144:8192, :]]
            ctx.run_phase(make_attn_l0(ctx, consts, "b", q0, kblk, vblk, amask, neglam, s0))
        with ctx.scope():
            ctx.run_phase(make_combine(ctx, consts, "b", x, s0, g0, x1))
        with ctx.scope():
            ctx.run_phase(LNPhase(ctx, consts, x1, xn1, "l1"))
        ctx.run_phase(Fence(ctx, "gp2", pc.targets[id(wv1[1])]))
        with ctx.scope():
            jobs = [{"w": wq1[1], "out": q1, "mode": "q1"},
                    {"w": wk1[1], "out": kv1[0:2048, :], "mode": "k1"},
                    {"w": wv1[1], "out": kv1[2048:4096, :], "mode": "v"}]
            ctx.run_phase(make_qkv_phase(ctx, consts, "b", xn1, jobs, cos_sb, sin_sb, None))
        ag1 = make_allgather(ctx, "kv1", kv1, kvf1, wait=True)
        ctx.run_phase(ag1)
        ctx.run_phase(Fence(ctx, "gp2", pc.targets[id(wdn[1])]))
        with ctx.scope():
            ctx.run_phase(make_ffn_l1(ctx, consts, "c", xn1, wgte[1], wup[1], wdn[1], g1))
        ctx.run_phase(Fence(ctx, "cc1", ag1.target))
        with ctx.scope():
            o_fm = ctx.stack.enter_context(nc.sbuf_tensor("o_fm", [128, 16 * 2048], BF16))
            kblk = [kvf1[0:2048, :], kvf1[4096:6144, :]]
            vblk = [kvf1[2048:4096, :], kvf1[6144:8192, :]]
            with ctx.scope():
                ctx.run_phase(make_attn_l1(ctx, consts, "c", q1, kblk, vblk, amask, o_fm))
            with ctx.scope():
                ctx.run_phase(make_wo_phase(ctx, consts, "c", o_fm, wo[1], s1))
        with ctx.scope():
            ctx.run_phase(make_combine(ctx, consts, "c", x1, s1, g1, out))
    return nc


FUSED = True


def kernel_fused(inputs, cin, cink, layouts):
    (x_core, tabs, wdevs, scaler_d, lam_d) = layouts
    if "F" not in _cache:
        _cache["F"] = _build_fused(cin)
    in_maps = []
    for c in range(8):
        h = c % 2
        im = {"x": x_core[c], "cosd": tabs[h][0], "sind": tabs[h][1], "logpos": tabs[h][2],
              "scaler": scaler_d, "amask": AMASK_H[h],
              **{f"lam{i}": lam_d[i] for i in range(4)}, **cink}
        im.update(wdevs)
        in_maps.append(im)
    res = run_bass_kernel_spmd(_cache["F"], in_maps, core_ids=list(range(8)))
    out = np.empty((4, 4096, 2048), np.float32)
    for c in range(8):
        b, h = c // 2, c % 2
        out[b][POS_H[h]] = res.results[c]["out"].T
    return out


# version 12
# speedup vs baseline: 1.0578x; 1.0578x over previous
import numpy as np
import ml_dtypes
import concourse.bass as bass
import concourse.mybir as mybir

F32 = mybir.dt.float32
BF16 = mybir.dt.bfloat16
AF = mybir.ActivationFunctionType
ALU = mybir.AluOpType
AX = mybir.AxisListType
BF = ml_dtypes.bfloat16

D = 2048
T_CORE = 2048
NH, HD = 16, 128
ENGS = ("sync", "gpsimd", "vector", "scalar", "tensor")


class Sem:
    def __init__(self, h, name):
        self.h = h; self.n = 0; self.name = name


class Ctx:
    def __init__(self, nc, stack):
        self.nc = nc
        self.stack = stack
        self.perm = stack
        self._sems = {}
        self.phase_idx = 0
        self.s_phase = self.sem("phase")

    def sem(self, name):
        if name not in self._sems:
            h = self.perm.enter_context(self.nc.semaphore(name))
            self._sems[name] = Sem(h, name)
        return self._sems[name]

    def scope(self):
        import contextlib
        @contextlib.contextmanager
        def cm():
            from contextlib import ExitStack
            old = self.stack
            with ExitStack() as st:
                self.stack = st
                try:
                    yield st
                finally:
                    self.stack = old
        return cm()

    def run_phase(self, phase):
        i = self.phase_idx
        sp = self.s_phase
        phase.begin({k: s.n for k, s in self._sems.items()})
        with self.nc.Block(name=getattr(phase, "name", f"ph{i}")) as block:
            for eng_name in ENGS:
                body = getattr(phase, eng_name, None)
                def mk(body=body):
                    def run(e):
                        if i > 0:
                            e.wait_ge(sp.h, 5 * i)
                        if body is not None:
                            body(e)
                        e.nop().then_inc(sp.h, 1)
                    return run
                getattr(block, eng_name)(mk())
        for name, total in phase.totals().items():
            self._sems[name].n += total
        sp.n += 5
        self.phase_idx += 1


class PhaseBase:
    def begin(self, bases):
        self.b = bases
    def totals(self):
        return self._totals
    def w(self, e, sem, rel):
        e.wait_ge(sem.h, self.b.get(sem.name, 0) + rel)


def host_consts():
    c = {}
    c["ones_bf"] = np.ones((128, 1), dtype=BF)
    c["ones1_f"] = np.ones((1, 128), dtype=np.float32)
    c["ident_bf"] = np.eye(128, dtype=BF)
    c["iota8"] = np.tile(np.arange(8, dtype=np.float32)[None, :], (128, 1))
    c["cmask"] = np.where(np.arange(128)[None, :] > np.arange(128)[:, None],
                          np.float32(-1e30), np.float32(0.0))
    c["ident_f"] = np.eye(128, dtype=np.float32)
    c["iota8x4"] = np.tile(np.arange(8, dtype=np.float32)[None, :], (128, 16))
    c["iotam99"] = c["iota8x4"] - 99.0
    m = np.arange(128)
    c["b32a"] = np.stack([(m // 32 == e).astype(BF) for e in range(4)] +
                         [np.zeros(128, BF) for _ in range(4)], 0)
    c["b32b"] = np.stack([np.zeros(128, BF) for _ in range(4)] +
                         [(m // 32 == (e - 4)).astype(BF) for e in range(4, 8)], 0)
    return c


class ConstPool(PhaseBase):
    name = "consts"
    def __init__(self, ctx, aps):
        nc, st = ctx.nc, ctx.stack
        self.ones_bf = st.enter_context(nc.sbuf_tensor("c_ones_bf", [128, 1], BF16))
        self.ones1_f = st.enter_context(nc.sbuf_tensor("c_ones1_f", [1, 128], F32))
        self.ident = st.enter_context(nc.sbuf_tensor("c_ident", [128, 128], BF16))
        self.iota8 = st.enter_context(nc.sbuf_tensor("c_iota8", [128, 8], F32))
        self.eps_rms = st.enter_context(nc.sbuf_tensor("c_eps_rms", [128, 1], F32))
        self.cmask = st.enter_context(nc.sbuf_tensor("c_cmask", [128, 128], F32))
        self.ident_f = st.enter_context(nc.sbuf_tensor("c_ident_f", [128, 128], F32))
        self.iota8x4 = st.enter_context(nc.sbuf_tensor("c_iota84", [128, 128], F32))
        self.iotam99 = st.enter_context(nc.sbuf_tensor("c_iom99", [128, 128], F32))
        self.b32a = st.enter_context(nc.sbuf_tensor("c_b32a", [8, 128], BF16))
        self.b32b = st.enter_context(nc.sbuf_tensor("c_b32b", [8, 128], BF16))
        self.aps = aps
        self.s = ctx.sem("dma_sync")
        self.sv = ctx.sem("v1")
        self._totals = {"dma_sync": 160, "v1": 1}
    def sync(self, e):
        e.dma_start(out=self.ones_bf[:, :], in_=self.aps["ones_bf"]).then_inc(self.s.h, 16)
        e.dma_start(out=self.ones1_f[:, :], in_=self.aps["ones1_f"]).then_inc(self.s.h, 16)
        e.dma_start(out=self.ident[:, :], in_=self.aps["ident_bf"]).then_inc(self.s.h, 16)
        e.dma_start(out=self.iota8[:, :], in_=self.aps["iota8"]).then_inc(self.s.h, 16)
        e.dma_start(out=self.cmask[:, :], in_=self.aps["cmask"]).then_inc(self.s.h, 16)
        e.dma_start(out=self.ident_f[:, :], in_=self.aps["ident_f"]).then_inc(self.s.h, 16)
        e.dma_start(out=self.iota8x4[:, :], in_=self.aps["iota8x4"]).then_inc(self.s.h, 16)
        e.dma_start(out=self.iotam99[:, :], in_=self.aps["iotam99"]).then_inc(self.s.h, 16)
        e.dma_start(out=self.b32a[:, :], in_=self.aps["b32a"]).then_inc(self.s.h, 16)
        e.dma_start(out=self.b32b[:, :], in_=self.aps["b32b"]).then_inc(self.s.h, 16)
        self.w(e, self.s, 160)
        self.w(e, self.sv, 1)

    def vector(self, e):
        e.memset(self.eps_rms[:, :], 1.1920929e-7).then_inc(self.sv.h, 1)


class LNPhase(PhaseBase):
    NCH = 4

    def __init__(self, ctx, consts, x_fm, xn_out, tag):
        self.c = consts
        self.x = x_fm; self.xn = xn_out
        self.name = f"ln_{tag}"
        nc, st = ctx.nc, ctx.stack
        t = tag
        self.xf = st.enter_context(nc.sbuf_tensor(f"ln_xf_{t}", [128, 16 * 512], F32))
        self.xb = st.enter_context(nc.sbuf_tensor(f"ln_xb_{t}", [128, 16 * 512], BF16))
        self.x2b = st.enter_context(nc.sbuf_tensor(f"ln_x2b_{t}", [128, 16 * 512], BF16))
        self.sm = st.enter_context(nc.sbuf_tensor(f"ln_sm_{t}", [1, 512 * 4], F32))
        self.xnb = st.enter_context(nc.sbuf_tensor(f"ln_xnb_{t}", [128, 16 * 512], BF16))
        self.scr = st.enter_context(nc.sbuf_tensor(f"ln_scr_{t}", [128, 512], F32))
        self.ps_s1 = st.enter_context(nc.psum_tensor(f"ln_s1_{t}", [1, 512], F32))
        self.ps_s2 = st.enter_context(nc.psum_tensor(f"ln_s2_{t}", [1, 512], F32))
        self.ps_mb = st.enter_context(nc.psum_tensor(f"ln_mb_{t}", [128, 512], F32))
        self.ps_rb = st.enter_context(nc.psum_tensor(f"ln_rb_{t}", [128, 512], F32))
        self.s_ld = ctx.sem("dma_sync")
        self.s_cst = ctx.sem("dma_gp")
        self.s_sq = ctx.sem("v1")
        self.s_stats = ctx.sem("t1")
        self.s_sm = ctx.sem("v2")
        self.s_sm2 = ctx.sem("sc1")
        self.s_bc = ctx.sem("t2")
        self.s_xn = ctx.sem("v3")
        self.s_st = ctx.sem("dma_out")
        n = self.NCH
        self._totals = {"dma_sync": 32 * n, "dma_gp": 32 * n, "v1": n, "t1": 2 * n,
                        "v2": 2 * n, "sc1": n, "t2": 2 * n, "v3": 16 * n,
                        "dma_out": 16 * n}

    def sync(self, e):
        for i in range(self.NCH):
            if i > 0:
                self.w(e, self.s_xn, 16 * i)
            src = self.x.rearrange("(kc p) t -> p kc t", p=128)
            for half in range(2):
                e.dma_start(
                    out=self.xf[:, :].rearrange("p (kc t) -> p kc t", kc=16)[:, 8*half:8*half+8, :],
                    in_=src[:, 8*half:8*half+8, 512*i:512*(i+1)],
                ).then_inc(self.s_ld.h, 16)
            if i > 0:
                self._store(e, i - 1)
        self.w(e, self.s_xn, 16 * self.NCH)
        self._store(e, self.NCH - 1)
        self.w(e, self.s_st, 16 * self.NCH)

    def _store(self, e, i):
        dst = self.xn.rearrange("(kc p) t -> p kc t", p=128)
        e.dma_start(
            out=dst[:, :, 512*i:512*(i+1)],
            in_=self.xnb[:, :].rearrange("p (kc t) -> p kc t", kc=16),
        ).then_inc(self.s_st.h, 16)

    def gpsimd(self, e):
        for i in range(self.NCH):
            if i > 0:
                self.w(e, self.s_stats, 2 * i)
            src = self.x.rearrange("(kc p) t -> p kc t", p=128)
            for half in range(2):
                e.dma_start(
                    out=self.xb[:, :].rearrange("p (kc t) -> p kc t", kc=16)[:, 8*half:8*half+8, :],
                    in_=src[:, 8*half:8*half+8, 512*i:512*(i+1)],
                ).then_inc(self.s_cst.h, 16)

    def vector(self, e):
        for i in range(self.NCH):
            self.w(e, self.s_cst, 32 * (i + 1))
            if i > 0:
                self.w(e, self.s_stats, 2 * i)
            e.tensor_mul(self.x2b[:, :], self.xb[:, :], self.xb[:, :]).then_inc(self.s_sq.h, 1)
            self.w(e, self.s_stats, 2 * (i + 1))
            negm = self.sm[:, 0:512]; e2 = self.sm[:, 512:1024]
            var = self.sm[:, 1024:1536]; rstd = self.sm[:, 1536:2048]
            e.tensor_scalar_mul(negm, self.ps_s1[:, :], -1.0 / 2048.0)
            e.tensor_scalar_mul(e2, self.ps_s2[:, :], 1.0 / 2048.0)
            t = self.scr[0:1, 0:512]
            e.tensor_mul(t, negm, negm)
            e.tensor_sub(var, e2, t)
            e.tensor_scalar_add(var, var, 1e-5).then_inc(self.s_sm.h, 1)
            self.w(e, self.s_sm2, i + 1)
            e.reciprocal(rstd, var).then_inc(self.s_sm.h, 1)
            self.w(e, self.s_bc, 2 * (i + 1))
            self.w(e, self.s_st, 16 * i)
            for kc in range(16):
                sl = slice(512 * kc, 512 * (kc + 1))
                e.tensor_add(self.scr[:, :], self.xf[:, sl], self.ps_mb[:, :])
                e.tensor_mul(self.xnb[:, sl], self.scr[:, :], self.ps_rb[:, :]).then_inc(self.s_xn.h, 1)

    def scalar(self, e):
        for i in range(self.NCH):
            self.w(e, self.s_sm, 2 * i + 1)
            var = self.sm[:, 1024:1536]
            e.activation(out=var, in_=var, func=AF.Sqrt).then_inc(self.s_sm2.h, 1)

    def tensor(self, e):
        for i in range(self.NCH):
            self.w(e, self.s_cst, 32 * (i + 1))
            if i > 0:
                self.w(e, self.s_sm, 2 * i - 1)
            for kc in range(16):
                sl = slice(512 * kc, 512 * (kc + 1))
                e.matmul(self.ps_s1[:, :], self.c.ones_bf[:, :], self.xb[:, sl],
                         start=(kc == 0), stop=(kc == 15))
            self.w(e, self.s_sq, i + 1)
            last = None
            for kc in range(16):
                sl = slice(512 * kc, 512 * (kc + 1))
                last = e.matmul(self.ps_s2[:, :], self.c.ones_bf[:, :], self.x2b[:, sl],
                                start=(kc == 0), stop=(kc == 15))
            last.then_inc(self.s_stats.h, 2)
            self.w(e, self.s_sm, 2 * (i + 1))
            if i > 0:
                self.w(e, self.s_xn, 16 * i)
            negm = self.sm[:, 0:512]; rstd = self.sm[:, 1536:2048]
            e.matmul(self.ps_mb[:, :], self.c.ones1_f[:, :], negm, start=True, stop=True)
            e.matmul(self.ps_rb[:, :], self.c.ones1_f[:, :], rstd, start=True, stop=True).then_inc(self.s_bc.h, 2)


class Plan(PhaseBase):
    def __init__(self, ctx, name):
        self.ctx = ctx
        self.name = name
        self.steps = []
        self.cnt = {}
        self._totals = self.cnt

    def sem(self, name):
        return self.ctx.sem(name)

    def step(self, engine, fn, waits=(), inc=None, manual=()):
        self.steps.append((engine, fn, tuple(waits), inc))
        if inc is not None:
            s, k = inc
            self.cnt[s.name] = self.cnt.get(s.name, 0) + k
        for (s, k) in manual:
            self.cnt[s.name] = self.cnt.get(s.name, 0) + k

    def rel(self, sem):
        return self.cnt.get(sem.name, 0)

    def _body(self, eng_name):
        def run(e):
            for (en, fn, waits, inc) in self.steps:
                if en != eng_name:
                    continue
                for (s, rv) in waits:
                    if rv > 0:
                        e.wait_ge(s.h, self.b.get(s.name, 0) + rv)
                ins = fn(e)
                if inc is not None:
                    s, k = inc
                    ins.then_inc(s.h, k)
        return run

    def __getattr__(self, name):
        if name in ENGS:
            return self._body(name)
        raise AttributeError(name)


_qs_ctr = [0]
def make_qscale_phase(ctx, consts, logpos_ap, scaler_ap, qscale_sb):
    nc, st = ctx.nc, ctx.stack
    _qs_ctr[0] += 1
    u = _qs_ctr[0]
    p = Plan(ctx, f"qscale{u}")
    lp = st.enter_context(nc.sbuf_tensor(f"qs_lp{u}", [1, 2048], F32))
    sc = st.enter_context(nc.sbuf_tensor(f"qs_sc{u}", [1, 16], F32))
    ps = st.enter_context(nc.psum_tensor(f"qs_ps{u}", [128, 256], F32))
    sd = p.sem("dma_sync"); st_ = p.sem("t1"); sv = p.sem("v1")
    p.step("sync", lambda e: e.dma_start(out=lp[:, :], in_=logpos_ap), inc=(sd, 16))
    p.step("sync", lambda e: e.dma_start(out=sc[:, :], in_=scaler_ap), inc=(sd, 16))
    def mms(e):
        last = None
        for tt in range(16):
            last = e.matmul(ps[:, 16*tt:16*(tt+1)], lp[:, 128*tt:128*(tt+1)], sc[:, :],
                            start=True, stop=True)
        return last
    p.step("tensor", mms, waits=[(sd, 32)], inc=(st_, 1))
    p.step("vector", lambda e: e.tensor_copy(qscale_sb[:, :], ps[:, :]),
           waits=[(st_, p.rel(st_))], inc=(sv, 1))
    return p


def make_lambda_phase(ctx, consts, lam_aps, neglam_sb):
    nc, st = ctx.nc, ctx.stack
    p = Plan(ctx, "lambda")
    lv = st.enter_context(nc.sbuf_tensor("lam_v", [1, 256], F32))
    scr = st.enter_context(nc.sbuf_tensor("lam_scr", [1, 128], F32))
    red = st.enter_context(nc.sbuf_tensor("lam_red", [1, 4], F32))
    p.red_dump = red
    ps = st.enter_context(nc.psum_tensor("lam_ps", [128, 1], F32))
    sd = p.sem("dma_sync"); sv = p.sem("v1"); ss = p.sem("sc1"); st_ = p.sem("t1")
    for i, ap in enumerate(lam_aps):
        p.step("sync", lambda e, ap=ap, i=i: e.dma_start(out=lv[:, 64*i:64*(i+1)], in_=ap), inc=(sd, 16))
    def v1(e):
        e.tensor_mul(scr[:, 0:64], lv[:, 0:64], lv[:, 64:128])
        e.tensor_mul(scr[:, 64:128], lv[:, 128:192], lv[:, 192:256])
        e.reduce_sum(red[:, 0:1], scr[:, 0:64], AX.X)
        return e.reduce_sum(red[:, 1:2], scr[:, 64:128], AX.X)
    p.step("vector", v1, waits=[(sd, 64)], inc=(sv, 1))
    p.step("scalar", lambda e: e.activation(out=red[:, 2:4], in_=red[:, 0:2], func=AF.Exp),
           waits=[(sv, 1)], inc=(ss, 1))
    def v2(e):
        return e.scalar_tensor_tensor(red[:, 0:1], red[:, 3:4], -0.2, red[:, 2:3],
                                      ALU.add, ALU.subtract)
    p.step("vector", v2, waits=[(ss, 1)], inc=(sv, 1))
    p.step("tensor", lambda e: e.matmul(ps[:, :], consts.ones1_f[:, :], red[:, 0:1],
                                        start=True, stop=True),
           waits=[(sv, 2)], inc=(st_, 1))
    p.step("vector", lambda e: e.tensor_copy(neglam_sb[:, :], ps[:, :]),
           waits=[(st_, 1)], inc=(sv, 1))
    return p


def make_qkv_phase(ctx, consts, tag, xn_dram, jobs, cos_sb, sin_sb, qscale_sb):
    nc, st = ctx.nc, ctx.stack
    p = Plan(ctx, f"qkv_{tag}")
    xn_sb = st.enter_context(nc.sbuf_tensor(f"qkv_xn_{tag}", [128, 16 * 2048], BF16))
    w_sb = st.enter_context(nc.sbuf_tensor(f"qkv_w_{tag}", [128, 16 * 2048], BF16))
    qf = [st.enter_context(nc.sbuf_tensor(f"qkv_qf{i}_{tag}", [128, 2048], F32)) for i in range(2)]
    q2f = st.enter_context(nc.sbuf_tensor(f"qkv_q2f_{tag}", [128, 2048], F32))
    rot = st.enter_context(nc.sbuf_tensor(f"qkv_rot_{tag}", [128, 2048], F32))
    t1 = st.enter_context(nc.sbuf_tensor(f"qkv_t1_{tag}", [128, 1024], F32))
    t2 = st.enter_context(nc.sbuf_tensor(f"qkv_t2_{tag}", [128, 1024], F32))
    ssq = st.enter_context(nc.sbuf_tensor(f"qkv_ssq_{tag}", [128, 64], F32))
    comb = st.enter_context(nc.sbuf_tensor(f"qkv_comb_{tag}", [128, 16], F32))
    ob = [st.enter_context(nc.sbuf_tensor(f"qkv_ob{i}_{tag}", [128, 2048], BF16)) for i in range(2)]
    ps = [st.enter_context(nc.psum_tensor(f"qkv_ps{i}_{tag}", [128, 2048], F32)) for i in range(2)]
    sd = p.sem("dma_sync"); so = p.sem("dma_out")
    smm = p.sem("t1"); sev = p.sem("sc1"); sssq = p.sem("v1"); srstd = p.sem("sc2")
    spp = p.sem("v2")

    xn_src = xn_dram.rearrange("(kc p) t -> p kc t", p=128)
    xn_dst = xn_sb[:, :].rearrange("p (kc t) -> p kc t", kc=16)
    for q in range(4):
        p.step("sync", lambda e, q=q: e.dma_start(out=xn_dst[:, 4*q:4*q+4, :],
                                                  in_=xn_src[:, 4*q:4*q+4, :]), inc=(sd, 16))

    for js, job in enumerate(jobs):
        mode = job["mode"]
        w_src = job["w"].rearrange("(kc p) n -> p kc n", p=128)
        w_dst = w_sb[:, :].rearrange("p (kc n) -> p kc n", kc=16)
        wld_target = None
        for q in range(4):
            waits = [(smm, p.rel(smm))] if (js > 0 and q == 0) else ()
            p.step("sync", lambda e, q=q, w_src=w_src: e.dma_start(out=w_dst[:, 4*q:4*q+4, :],
                                                      in_=w_src[:, 4*q:4*q+4, :]),
                   waits=waits, inc=(sd, 16))
        wld_target = p.rel(sd)

        for tt in range(16):
            pb = ps[tt % 2]
            def mms(e, tt=tt, pb=pb):
                last = None
                for kc in range(16):
                    lhs = xn_sb[:, 2048*kc + 128*tt: 2048*kc + 128*(tt+1)]
                    for nch in range(4):
                        last = e.matmul(pb[:, 512*nch:512*(nch+1)], lhs,
                                        w_sb[:, 2048*kc + 512*nch: 2048*kc + 512*(nch+1)],
                                        start=(kc == 0), stop=(kc == 15))
                return last
            waits = [(sd, wld_target)]
            if p.rel(smm) >= 2:
                waits.append((sev, p.rel(smm) - 1))
            p.step("tensor", mms, waits=waits, inc=(smm, 1))

            if mode == "v":
                def ev(e, tt=tt, pb=pb):
                    return e.copy(ob[tt % 2][:, :], pb[:, :])
                waits = [(smm, p.rel(smm))]
                if p.rel(so) >= 32:
                    waits.append((so, p.rel(so) - 16))
                p.step("scalar", ev, waits=waits, inc=(sev, 1))
                p.step("sync", lambda e, tt=tt, job=job: e.dma_start(
                    out=job["out"][128*tt:128*(tt+1), :], in_=ob[tt % 2][:, :]),
                    waits=[(sev, p.rel(sev))], inc=(so, 16))
                continue

            def ev(e, tt=tt, pb=pb):
                return e.copy(qf[tt % 2][:, :], pb[:, :])
            waits = [(smm, p.rel(smm))]
            if p.rel(spp) >= 2:
                waits.append((spp, p.rel(spp) - 1))
            p.step("scalar", ev, waits=waits, inc=(sev, 1))

            par = tt % 2
            qfb = qf[par]
            do_rms = mode in ("q0", "k0")
            if tt == 0 and "dbg_qf" in job:
                p.step("sync", lambda e, qfb=qfb, job=job: e.dma_start(out=job["dbg_qf"], in_=qfb[:, :]),
                       waits=[(sev, p.rel(sev))], inc=(so, 16))
            if do_rms:
                def vssq(e, qfb=qfb, par=par):
                    e.tensor_mul(q2f[:, :], qfb[:, :], qfb[:, :])
                    return e.tensor_reduce(ssq[:, 32*par:32*par+16],
                                           q2f[:, :].rearrange("p (h d) -> p h d", h=16),
                                           AX.X, ALU.add)
                p.step("vector", vssq, waits=[(sev, p.rel(sev))], inc=(sssq, 1))
                def srs(e, par=par):
                    e.activation(out=ssq[:, 32*par+16:32*par+32], in_=ssq[:, 32*par:32*par+16],
                                 func=AF.Ln, scale=1.0/128.0, bias=consts.eps_rms[:, :])
                    return e.activation(out=ssq[:, 32*par+16:32*par+32],
                                        in_=ssq[:, 32*par+16:32*par+32],
                                        func=AF.Exp, scale=-0.5)
                p.step("scalar", srs, waits=[(sssq, p.rel(sssq))], inc=(srstd, 1))

            def vrot(e, qfb=qfb, mode=mode, tt=tt):
                cosb = cos_sb[:, 64*tt:64*(tt+1)].rearrange("p (g d) -> p g d", g=1).broadcast_to([128, 16, 64])
                sinb = sin_sb[:, 64*tt:64*(tt+1)].rearrange("p (g d) -> p g d", g=1).broadcast_to([128, 16, 64])
                lo = qfb[:, :].rearrange("p (h d) -> p h d", h=16)[:, :, 0:64]
                hi = qfb[:, :].rearrange("p (h d) -> p h d", h=16)[:, :, 64:128]
                rlo = rot[:, :].rearrange("p (h d) -> p h d", h=16)[:, :, 0:64]
                rhi = rot[:, :].rearrange("p (h d) -> p h d", h=16)[:, :, 64:128]
                tv1 = t1[:, :].rearrange("p (h d) -> p h d", h=16)
                tv2 = t2[:, :].rearrange("p (h d) -> p h d", h=16)
                e.tensor_tensor(tv1, lo, cosb, ALU.mult)
                e.tensor_tensor(tv2, hi, sinb, ALU.mult)
                if mode in ("q0", "k0"):
                    e.tensor_add(rlo, tv1, tv2)
                else:
                    e.tensor_sub(rlo, tv1, tv2)
                e.tensor_tensor(tv1, lo, sinb, ALU.mult)
                e.tensor_tensor(tv2, hi, cosb, ALU.mult)
                if mode in ("q0", "k0"):
                    return e.tensor_sub(rhi, tv2, tv1)
                else:
                    return e.tensor_add(rhi, tv1, tv2)
            p.step("vector", vrot, waits=[(sev, p.rel(sev))], inc=(sssq, 1) if not do_rms else None)

            waits = []
            if do_rms:
                waits.append((srstd, p.rel(srstd)))
            if p.rel(so) >= 32:
                waits.append((so, p.rel(so) - 16))
            def vapply(e, mode=mode, par=par, tt=tt):
                last = None
                obb = ob[par]
                rstd_t = ssq[:, 32*par+16:32*par+32]
                if mode == "q0":
                    for h in range(16):
                        last = e.tensor_scalar(obb[:, 128*h:128*(h+1)],
                                               rot[:, 128*h:128*(h+1)],
                                               rstd_t[:, h:h+1],
                                               qscale_sb[:, 16*tt+h:16*tt+h+1],
                                               ALU.mult, ALU.mult)
                elif mode == "k0":
                    for h in range(16):
                        last = e.tensor_scalar_mul(obb[:, 128*h:128*(h+1)],
                                                   rot[:, 128*h:128*(h+1)],
                                                   rstd_t[:, h:h+1])
                else:
                    last = e.tensor_scalar_mul(obb[:, :], rot[:, :], 0.08838834764831845)
                return last
            p.step("vector", vapply, waits=waits, inc=(spp, 1))
            if tt == 0 and "dbg_rot" in job:
                p.step("sync", lambda e, job=job: e.dma_start(out=job["dbg_rot"], in_=rot[:, :]),
                       waits=[(spp, p.rel(spp))], inc=(so, 16))
            p.step("sync", lambda e, tt=tt, par=par, job=job: e.dma_start(
                out=job["out"][128*tt:128*(tt+1), :], in_=ob[par][:, :]),
                waits=[(spp, p.rel(spp))], inc=(so, 16))

    p.step("sync", lambda e: e.nop(), waits=[(so, p.rel(so))])
    return p




def ktiles_uniform(j, W=384):
    out = []
    base = 256 * j
    n = base // W
    for i in range(n):
        out.append((W * i, W, False))
    rem = base % W
    if rem:
        out.append((n * W, rem, False))
    out.append((base, 256, True))
    return out


def _k_rhs(kbuf_pair, k0, kw):
    out = []
    for c in range(kw // 128):
        gc = k0 // 128 + c
        blk = gc % 2
        off = (gc // 2) * 128
        out.append((kbuf_pair[blk][:, off:off + 128], 128 * c))
    return out


def make_attn_l1(ctx, consts, tag, q_tm, k_tm, v_tm, amask_ap, o_fm_sb):
    if not isinstance(k_tm, (list, tuple)):
        k_tm = [k_tm[0:2048, :], k_tm[2048:4096, :]]
        v_tm = [v_tm[0:2048, :], v_tm[2048:4096, :]]
    nc, st = ctx.nc, ctx.stack
    p = Plan(ctx, f"attn1_{tag}")
    kb = [[st.enter_context(nc.sbuf_tensor(f"at1_k{i}{blk}_{tag}", [128, 2048], BF16))
           for blk in range(2)] for i in range(2)]
    vb = [[st.enter_context(nc.sbuf_tensor(f"at1_v{i}{blk}_{tag}", [128, 16 * 129], BF16))
           for blk in range(2)] for i in range(2)]
    qbuf = [st.enter_context(nc.sbuf_tensor(f"at1_q{i}_{tag}", [128, 2048], BF16)) for i in range(2)]
    amask = st.enter_context(nc.sbuf_tensor(f"at1_am_{tag}", [128, 256], F32))
    p_sb = [st.enter_context(nc.sbuf_tensor(f"at1_p{i}_{tag}", [128, 512], BF16)) for i in range(2)]
    pt_sb = [st.enter_context(nc.sbuf_tensor(f"at1_pt{i}_{tag}", [128, 512], BF16)) for i in range(2)]
    m_sb = st.enter_context(nc.sbuf_tensor(f"at1_m_{tag}", [128, 3], F32))
    o_acc = st.enter_context(nc.sbuf_tensor(f"at1_oacc_{tag}", [128, 132], F32))
    corr = st.enter_context(nc.sbuf_tensor(f"at1_corr_{tag}", [128, 2], F32))
    rcp = st.enter_context(nc.sbuf_tensor(f"at1_rcp_{tag}", [128, 1], F32))
    o_bf = st.enter_context(nc.sbuf_tensor(f"at1_obf_{tag}", [128, 128], BF16))
    ps_s = [st.enter_context(nc.psum_tensor(f"at1_s{i}_{tag}", [128, 512], F32)) for i in range(2)]
    ps_pt = [st.enter_context(nc.psum_tensor(f"at1_ppt{i}_{tag}", [128, 512], BF16)) for i in range(2)]
    ps_o = [st.enter_context(nc.psum_tensor(f"at1_o{i}_{tag}", [128, 132], F32)) for i in range(2)]
    ps_y = st.enter_context(nc.psum_tensor(f"at1_y_{tag}", [128, 128], BF16))

    sd = p.sem("dma_sync"); gp1 = p.sem("gp1")
    t_s = p.sem("t1"); v_st = p.sem("v1"); sc_p = p.sem("sc1")
    t_tr = p.sem("t2"); v_pt = p.sem("v2"); t_o = p.sem("t3"); v_up = p.sem("v3")
    v_fin = p.sem("v4"); t_y = p.sem("t4"); v_ev = p.sem("v5"); sc_fin = p.sem("sc2")

    def ms(e):
        for i in range(2):
            for blk in range(2):
                e.memset(vb[i][blk][:, :], 1.0)
        return e.dma_start(out=amask[:, :], in_=amask_ap)
    p.step("gpsimd", ms, inc=(gp1, 16))

    kt_end = []
    gk = 0
    fin = 0
    pend = []

    def front(b, j, k0, kw, msk, kl, g, load_tgt):
        par = g % 2
        cur = g % 3; prev = (g - 1) % 3
        waits = [(sd, load_tgt)]
        if g >= 2:
            waits.append((sc_p, g - 1))
        def tsm(e, b=b, j=j, k0=k0, kw=kw, par=par):
            last = None
            for (ap, off) in _k_rhs(kb[b], k0, kw):
                last = e.matmul(ps_s[par][:, off:off+128],
                                qbuf[b][:, 128*j:128*(j+1)], ap, start=True, stop=True)
            return last
        p.step("tensor", tsm, waits=waits, inc=(t_s, 1))
        def vstats(e, par=par, kw=kw, msk=msk, kl=kl, cur=cur, prev=prev):
            if msk:
                e.tensor_add(ps_s[par][:, 0:256], ps_s[par][:, 0:256], amask[:, :])
            if kl == 0:
                return e.reduce_max(m_sb[:, cur:cur+1], ps_s[par][:, 0:kw], AX.X, negate=True)
            e.tensor_scalar_mul(ps_s[par][:, kw:kw+1], m_sb[:, prev:prev+1], -1.0)
            return e.reduce_max(m_sb[:, cur:cur+1], ps_s[par][:, 0:kw+1], AX.X, negate=True)
        p.step("vector", vstats, waits=[(t_s, g + 1)], inc=(v_st, 1))
        def sp(e, par=par, kw=kw, kl=kl, cur=cur, prev=prev, g=g):
            if kl > 0:
                e.activation(out=corr[:, g % 2:g % 2 + 1], in_=m_sb[:, prev:prev+1],
                             func=AF.Exp, scale=-1.0, bias=m_sb[:, cur:cur+1])
            return e.activation(out=p_sb[par][:, 0:kw], in_=ps_s[par][:, 0:kw],
                                func=AF.Exp, bias=m_sb[:, cur:cur+1])
        waits = [(v_st, g + 1)]
        if g >= 2:
            waits.append((t_tr, g - 1))
        p.step("scalar", sp, waits=waits, inc=(sc_p, 1))

    def back(b, k0, kw, kl, g):
        par = g % 2
        def ttr(e, kw=kw, par=par):
            last = None
            for c in range(kw // 128):
                last = e.transpose(ps_pt[par][:, 128*c:128*(c+1)], p_sb[par][:, 128*c:128*(c+1)],
                                   consts.ident[:, :])
            return last
        waits = [(sc_p, g + 1)]
        if g >= 2:
            waits.append((v_pt, g - 1))
        p.step("tensor", ttr, waits=waits, inc=(t_tr, 1))
        waits = [(t_tr, g + 1)]
        if g >= 2:
            waits.append((t_o, g - 1))
        p.step("vector", lambda e, kw=kw, par=par: e.tensor_copy(pt_sb[par][:, 0:kw], ps_pt[par][:, 0:kw]),
               waits=waits, inc=(v_pt, 1))
        def tom(e, k0=k0, kw=kw, par=par, b=b):
            last = None
            nc_ = kw // 128
            for c in range(nc_):
                gc = k0 // 128 + c
                blk = gc % 2
                off = 129 * (gc // 2)
                last = e.matmul(ps_o[par][:, 0:129], pt_sb[par][:, 128*c:128*(c+1)],
                                vb[b][blk][:, off:off+129],
                                start=(c == 0), stop=(c == nc_ - 1))
            return last
        waits = [(v_pt, g + 1)]
        if g >= 2:
            waits.append((v_up, g - 1))
        p.step("tensor", tom, waits=waits, inc=(t_o, 1))
        def vup(e, par=par, kl=kl, g=g):
            if kl == 0:
                return e.tensor_scalar(o_acc[:, 0:129], ps_o[par][:, 0:129], 1.0, None, ALU.mult)
            return e.scalar_tensor_tensor(o_acc[:, 0:129], o_acc[:, 0:129],
                                          corr[:, g % 2:g % 2 + 1], ps_o[par][:, 0:129],
                                          ALU.mult, ALU.add)
        p.step("vector", vup, waits=[(t_o, g + 1)], inc=(v_up, 1))

    for h in range(16):
        b = h % 2
        waits = [(gp1, 16)] if h < 2 else [(t_o, kt_end[h - 2])]
        p.step("sync", lambda e, b=b, h=h: e.dma_start(
            out=qbuf[b][:, :], in_=q_tm[:, 128*h:128*(h+1)], transpose=True),
            waits=waits, inc=(sd, 16))
        for blk in range(2):
            p.step("sync", lambda e, b=b, h=h, blk=blk: e.dma_start(
                out=kb[b][blk][:, :], in_=k_tm[blk][:, 128*h:128*(h+1)],
                transpose=True), inc=(sd, 16))
            vv = vb[b][blk][:, :].rearrange("p (kc n) -> p kc n", n=129)
            vsrc = v_tm[blk].rearrange("(kc p) n -> p kc n", p=128)
            p.step("sync", lambda e, b=b, h=h, blk=blk, vv=vv, vsrc=vsrc: e.dma_start(
                out=vv[:, :, 0:128], in_=vsrc[:, :, 128*h:128*(h+1)]),
                inc=(sd, 16))
        load_tgt = p.rel(sd)

        for j in range(16):
            kts = ktiles_uniform(j)
            for kl, (k0, kw, msk) in enumerate(kts):
                front(b, j, k0, kw, msk, kl, gk, load_tgt)
                for fn in pend:
                    fn()
                pend = [lambda b=b, k0=k0, kw=kw, kl=kl, g=gk: back(b, k0, kw, kl, g)]
                gk += 1
            for fn in pend:
                fn()
            pend = []
            def sfin(e):
                e.activation(out=rcp[:, :], in_=o_acc[:, 128:129], func=AF.Ln)
                return e.activation(out=rcp[:, :], in_=rcp[:, :], func=AF.Exp, scale=-1.0)
            p.step("scalar", sfin, waits=[(v_up, gk)], inc=(sc_fin, 1))
            def vfin(e):
                return e.tensor_tensor(o_bf[:, :], o_acc[:, 0:128],
                                       rcp[:, 0:1].broadcast_to([128, 128]), ALU.mult)
            p.step("vector", vfin, waits=[(sc_fin, fin + 1)], inc=(v_fin, 1))
            waits = [(v_fin, fin + 1)]
            if fin >= 1:
                waits.append((v_ev, fin))
            p.step("tensor", lambda e: e.transpose(ps_y[:, :], o_bf[:, :], consts.ident[:, :]),
                   waits=waits, inc=(t_y, 1))
            p.step("vector", lambda e, h=h, j=j: e.tensor_copy(
                o_fm_sb[:, 2048*h + 128*j: 2048*h + 128*(j+1)], ps_y[:, :]),
                waits=[(t_y, fin + 1)], inc=(v_ev, 1))
            fin += 1
        kt_end.append(gk)
    p.step("sync", lambda e: e.nop(), waits=[(v_ev, fin)])
    return p


def make_attn_l0(ctx, consts, tag, q_tm, k_tm, v_tm, amask_ap, neglam_sb, s_fm_out):
    if not isinstance(k_tm, (list, tuple)):
        k_tm = [k_tm[0:2048, :], k_tm[2048:4096, :]]
        v_tm = [v_tm[0:2048, :], v_tm[2048:4096, :]]
    nc, st = ctx.nc, ctx.stack
    p = Plan(ctx, f"attn0_{tag}")
    ka = [[st.enter_context(nc.sbuf_tensor(f"at0_ka{i}{blk}_{tag}", [128, 2048], BF16))
           for blk in range(2)] for i in range(2)]
    kbb = [[st.enter_context(nc.sbuf_tensor(f"at0_kb{i}{blk}_{tag}", [128, 2048], BF16))
            for blk in range(2)] for i in range(2)]
    qa = [st.enter_context(nc.sbuf_tensor(f"at0_qa{i}_{tag}", [128, 2048], BF16)) for i in range(2)]
    qb = [st.enter_context(nc.sbuf_tensor(f"at0_qb{i}_{tag}", [128, 2048], BF16)) for i in range(2)]
    vbf = [[st.enter_context(nc.sbuf_tensor(f"at0_v{i}{blk}_{tag}", [128, 16 * 257], BF16))
            for blk in range(2)] for i in range(2)]
    amask = st.enter_context(nc.sbuf_tensor(f"at0_am_{tag}", [128, 256], F32))
    p_sb = [st.enter_context(nc.sbuf_tensor(f"at0_p{i}_{tag}", [128, 1024], BF16)) for i in range(2)]
    pt_sb = [st.enter_context(nc.sbuf_tensor(f"at0_pt{i}_{tag}", [128, 1024], BF16)) for i in range(2)]
    m_a = st.enter_context(nc.sbuf_tensor(f"at0_ma_{tag}", [128, 3], F32))
    m_b = st.enter_context(nc.sbuf_tensor(f"at0_mb_{tag}", [128, 3], F32))
    corr_a = st.enter_context(nc.sbuf_tensor(f"at0_ca_{tag}", [128, 2], F32))
    corr_b = st.enter_context(nc.sbuf_tensor(f"at0_cb_{tag}", [128, 2], F32))
    oa = st.enter_context(nc.sbuf_tensor(f"at0_oa_{tag}", [128, 260], F32))
    ob_ = st.enter_context(nc.sbuf_tensor(f"at0_ob_{tag}", [128, 260], F32))
    rcp_a = st.enter_context(nc.sbuf_tensor(f"at0_ra_{tag}", [128, 1], F32))
    rcp_b = st.enter_context(nc.sbuf_tensor(f"at0_rb_{tag}", [128, 1], F32))
    o2n = st.enter_context(nc.sbuf_tensor(f"at0_o2n_{tag}", [128, 256], F32))
    y_tm = st.enter_context(nc.sbuf_tensor(f"at0_ytm_{tag}", [128, 256], F32))
    y_stage = st.enter_context(nc.sbuf_tensor(f"at0_yst_{tag}", [128, 2 * 2048], F32))
    ps_sa = [st.enter_context(nc.psum_tensor(f"at0_psa{i}_{tag}", [128, 512], F32)) for i in range(2)]
    ps_sb_ = [st.enter_context(nc.psum_tensor(f"at0_psb{i}_{tag}", [128, 512], F32)) for i in range(2)]
    ps_pt = st.enter_context(nc.psum_tensor(f"at0_ppt_{tag}", [128, 1024], BF16))
    ps_oa = st.enter_context(nc.psum_tensor(f"at0_poa_{tag}", [128, 260], F32))
    ps_ob = st.enter_context(nc.psum_tensor(f"at0_pob_{tag}", [128, 260], F32))
    ps_y = st.enter_context(nc.psum_tensor(f"at0_py_{tag}", [128, 256], F32))

    sd = p.sem("dma_sync"); gp1 = p.sem("gp1"); so = p.sem("dma_out")
    t_s = p.sem("t1"); v_st = p.sem("v1"); sc_p = p.sem("sc1")
    t_tr = p.sem("t2"); v_pt = p.sem("v2"); t_o = p.sem("t3"); v_up = p.sem("v3")
    v_fin = p.sem("v4"); t_y = p.sem("t4"); v_ev = p.sem("v5"); sc_fin = p.sem("sc2")

    def ms(e):
        for i in range(2):
            for blk in range(2):
                e.memset(vbf[i][blk][:, :], 1.0)
        return e.dma_start(out=amask[:, :], in_=amask_ap)
    p.step("gpsimd", ms, inc=(gp1, 16))

    kt_end = []
    gk = 0
    fin = 0
    pend = []

    def front(bpar, j, k0, kw, msk, kl, g, load_tgt):
        par = g % 2
        cur = g % 3; prev = (g - 1) % 3
        waits = [(sd, load_tgt)]
        if g >= 2:
            waits.append((sc_p, g - 1))
        def tsm(e, bpar=bpar, j=j, k0=k0, kw=kw, par=par):
            last = None
            for (ap, off) in _k_rhs(ka[bpar], k0, kw):
                last = e.matmul(ps_sa[par][:, off:off+128],
                                qa[bpar][:, 128*j:128*(j+1)], ap, start=True, stop=True)
            for (ap, off) in _k_rhs(kbb[bpar], k0, kw):
                last = e.matmul(ps_sb_[par][:, off:off+128],
                                qb[bpar][:, 128*j:128*(j+1)], ap, start=True, stop=True)
            return last
        p.step("tensor", tsm, waits=waits, inc=(t_s, 1))
        def vstats(e, par=par, kw=kw, msk=msk, kl=kl, cur=cur, prev=prev):
            last = None
            for ps, m_ in ((ps_sa[par], m_a), (ps_sb_[par], m_b)):
                if msk:
                    e.tensor_add(ps[:, 0:256], ps[:, 0:256], amask[:, :])
                if kl == 0:
                    last = e.reduce_max(m_[:, cur:cur+1], ps[:, 0:kw], AX.X, negate=True)
                else:
                    e.tensor_scalar_mul(ps[:, kw:kw+1], m_[:, prev:prev+1], -1.0)
                    last = e.reduce_max(m_[:, cur:cur+1], ps[:, 0:kw+1], AX.X, negate=True)
            return last
        p.step("vector", vstats, waits=[(t_s, g + 1)], inc=(v_st, 1))
        def sp(e, par=par, kw=kw, kl=kl, cur=cur, prev=prev, g=g):
            last = None
            for ps, m_, co, off in ((ps_sa[par], m_a, corr_a, 0), (ps_sb_[par], m_b, corr_b, 512)):
                if kl > 0:
                    e.activation(out=co[:, g % 2:g % 2 + 1], in_=m_[:, prev:prev+1],
                                 func=AF.Exp, scale=-1.0, bias=m_[:, cur:cur+1])
                last = e.activation(out=p_sb[par][:, off:off+kw], in_=ps[:, 0:kw],
                                    func=AF.Exp, bias=m_[:, cur:cur+1])
            return last
        waits = [(v_st, g + 1)]
        if g >= 2:
            waits.append((t_tr, g - 1))
        p.step("scalar", sp, waits=waits, inc=(sc_p, 1))

    def back(bpar, k0, kw, kl, g):
        par = g % 2
        def ttr(e, kw=kw, par=par):
            last = None
            for off in (0, 512):
                for c in range(kw // 128):
                    last = e.transpose(ps_pt[:, off + 128*c: off + 128*(c+1)],
                                       p_sb[par][:, off + 128*c: off + 128*(c+1)],
                                       consts.ident[:, :])
            return last
        waits = [(sc_p, g + 1)]
        if g >= 1:
            waits.append((v_pt, g))
        p.step("tensor", ttr, waits=waits, inc=(t_tr, 1))
        def vpt(e, kw=kw, par=par):
            e.tensor_copy(pt_sb[par][:, 0:kw], ps_pt[:, 0:kw])
            return e.tensor_copy(pt_sb[par][:, 512:512+kw], ps_pt[:, 512:512+kw])
        waits = [(t_tr, g + 1)]
        if g >= 2:
            waits.append((t_o, g - 1))
        p.step("vector", vpt, waits=waits, inc=(v_pt, 1))
        def tom(e, k0=k0, kw=kw, bpar=bpar, par=par):
            last = None
            nc_ = kw // 128
            for ps_o_, off in ((ps_oa, 0), (ps_ob, 512)):
                for c in range(nc_):
                    gc = k0 // 128 + c
                    blk = gc % 2
                    voff = 257 * (gc // 2)
                    last = e.matmul(ps_o_[:, 0:257], pt_sb[par][:, off + 128*c: off + 128*(c+1)],
                                    vbf[bpar][blk][:, voff:voff+257],
                                    start=(c == 0), stop=(c == nc_ - 1))
            return last
        waits = [(v_pt, g + 1)]
        if g >= 1:
            waits.append((v_up, g))
        p.step("tensor", tom, waits=waits, inc=(t_o, 1))
        def vup(e, kl=kl, g=g):
            last = None
            for ps_o_, o_, co in ((ps_oa, oa, corr_a), (ps_ob, ob_, corr_b)):
                if kl == 0:
                    last = e.tensor_scalar(o_[:, 0:257], ps_o_[:, 0:257], 1.0, None, ALU.mult)
                else:
                    last = e.scalar_tensor_tensor(o_[:, 0:257], o_[:, 0:257],
                                                  co[:, g % 2:g % 2 + 1],
                                                  ps_o_[:, 0:257], ALU.mult, ALU.add)
            return last
        p.step("vector", vup, waits=[(t_o, g + 1)], inc=(v_up, 1))

    for jv in range(8):
        bpar = jv % 2
        ha, hb = jv, jv + 8
        waits = [(gp1, 16)] if jv < 2 else [(t_o, kt_end[jv - 2])]
        p.step("sync", lambda e, bpar=bpar, ha=ha: e.dma_start(
            out=qa[bpar][:, :], in_=q_tm[:, 128*ha:128*(ha+1)], transpose=True),
            waits=waits, inc=(sd, 16))
        p.step("sync", lambda e, bpar=bpar, hb=hb: e.dma_start(
            out=qb[bpar][:, :], in_=q_tm[:, 128*hb:128*(hb+1)], transpose=True), inc=(sd, 16))
        for blk in range(2):
            p.step("sync", lambda e, bpar=bpar, ha=ha, blk=blk: e.dma_start(
                out=ka[bpar][blk][:, :], in_=k_tm[blk][:, 128*ha:128*(ha+1)],
                transpose=True), inc=(sd, 16))
            p.step("sync", lambda e, bpar=bpar, hb=hb, blk=blk: e.dma_start(
                out=kbb[bpar][blk][:, :], in_=k_tm[blk][:, 128*hb:128*(hb+1)],
                transpose=True), inc=(sd, 16))
            vv = vbf[bpar][blk][:, :].rearrange("p (kc n) -> p kc n", n=257)
            vsrc = v_tm[blk].rearrange("(kc p) n -> p kc n", p=128)
            p.step("sync", lambda e, jv=jv, blk=blk, vv=vv, vsrc=vsrc, bpar=bpar: e.dma_start(
                out=vv[:, :, 0:256], in_=vsrc[:, :, 256*jv:256*(jv+1)]),
                inc=(sd, 16))
        load_tgt = p.rel(sd)

        for j in range(16):
            kts = ktiles_uniform(j)
            for kl, (k0, kw, msk) in enumerate(kts):
                front(bpar, j, k0, kw, msk, kl, gk, load_tgt)
                for fn in pend:
                    fn()
                pend = [lambda bpar=bpar, k0=k0, kw=kw, kl=kl, g=gk: back(bpar, k0, kw, kl, g)]
                gk += 1
            for fn in pend:
                fn()
            pend = []
            def sfin(e):
                e.activation(out=rcp_a[:, :], in_=oa[:, 256:257], func=AF.Ln)
                e.activation(out=rcp_a[:, :], in_=rcp_a[:, :], func=AF.Exp, scale=-1.0)
                e.activation(out=rcp_b[:, :], in_=ob_[:, 256:257], func=AF.Ln)
                return e.activation(out=rcp_b[:, :], in_=rcp_b[:, :], func=AF.Exp, scale=-1.0)
            p.step("scalar", sfin, waits=[(v_up, gk)], inc=(sc_fin, 1))
            def vfin(e):
                e.tensor_scalar(o2n[:, :], ob_[:, 0:256], rcp_b[:, 0:1], neglam_sb[:, 0:1],
                                ALU.mult, ALU.mult)
                return e.scalar_tensor_tensor(y_tm[:, :], oa[:, 0:256], rcp_a[:, 0:1],
                                              o2n[:, :], ALU.mult, ALU.add)
            p.step("vector", vfin, waits=[(sc_fin, fin + 1)], inc=(v_fin, 1))
            def tyr(e):
                e.transpose(ps_y[:, 0:128], y_tm[:, 0:128], consts.ident_f[:, :])
                return e.transpose(ps_y[:, 128:256], y_tm[:, 128:256], consts.ident_f[:, :])
            waits = [(v_fin, fin + 1)]
            if fin >= 1:
                waits.append((v_ev, fin))
            p.step("tensor", tyr, waits=waits, inc=(t_y, 1))
            def vev(e, j=j):
                e.tensor_copy(y_stage[:, 128*j:128*(j+1)], ps_y[:, 0:128])
                return e.tensor_copy(y_stage[:, 2048 + 128*j:2048 + 128*(j+1)], ps_y[:, 128:256])
            p.step("vector", vev, waits=[(t_y, fin + 1), (so, p.rel(so))], inc=(v_ev, 1))
            fin += 1

        for rb_ in range(2):
            p.step("sync", lambda e, jv=jv, rb_=rb_: e.dma_start(
                out=s_fm_out[256*jv + 128*rb_: 256*jv + 128*(rb_+1), :],
                in_=y_stage[:, 2048*rb_:2048*(rb_+1)]),
                waits=[(v_ev, fin)], inc=(so, 16))
        kt_end.append(gk)
    p.step("sync", lambda e: e.nop(), waits=[(so, p.rel(so))])
    return p

def make_wo_phase(ctx, consts, tag, o_fm_sb, wo_dram, s_fm_out):
    nc, st = ctx.nc, ctx.stack
    p = Plan(ctx, f"wo_{tag}")
    wo_sb = st.enter_context(nc.sbuf_tensor(f"wo_w_{tag}", [128, 16 * 2048], BF16))
    sev = [st.enter_context(nc.sbuf_tensor(f"wo_ev{i}_{tag}", [128, 512], F32)) for i in range(2)]
    ps = [st.enter_context(nc.psum_tensor(f"wo_ps{i}_{tag}", [128, 512], F32)) for i in range(2)]
    sd = p.sem("dma_sync"); so = p.sem("dma_out")
    smm = p.sem("t1"); sevs = p.sem("sc1")
    wsrc = wo_dram.rearrange("(kc p) n -> p kc n", p=128)
    wdst = wo_sb[:, :].rearrange("p (kc n) -> p kc n", kc=16)
    for q in range(4):
        p.step("sync", lambda e, q=q: e.dma_start(out=wdst[:, 4*q:4*q+4, :],
                                                  in_=wsrc[:, 4*q:4*q+4, :]), inc=(sd, 16))
    wt = p.rel(sd)
    g = 0
    for mc in range(16):
        for tc in range(4):
            def mm(e, mc=mc, tc=tc, g=g):
                last = None
                for kc in range(16):
                    last = e.matmul(ps[g % 2][:, :], wo_sb[:, 2048*kc + 128*mc: 2048*kc + 128*(mc+1)],
                                    o_fm_sb[:, 2048*kc + 512*tc: 2048*kc + 512*(tc+1)],
                                    start=(kc == 0), stop=(kc == 15))
                return last
            waits = [(sd, wt)]
            if g >= 2:
                waits.append((sevs, g - 1))
            p.step("tensor", mm, waits=waits, inc=(smm, 1))
            def ev(e, g=g):
                return e.copy(sev[g % 2][:, :], ps[g % 2][:, :])
            waits = [(smm, g + 1)]
            if g >= 2:
                waits.append((so, 16 * (g - 1)))
            p.step("scalar", ev, waits=waits, inc=(sevs, 1))
            p.step("sync", lambda e, mc=mc, tc=tc, g=g: e.dma_start(
                out=s_fm_out[128*mc:128*(mc+1), 512*tc:512*(tc+1)], in_=sev[g % 2][:, :]),
                waits=[(sevs, g + 1)], inc=(so, 16))
            g += 1
    p.step("sync", lambda e: e.nop(), waits=[(so, 16 * g)])
    return p


def make_ffn_l1(ctx, consts, tag, xn_dram, wg_dram, wu_dram, wd_dram, g_fm_out):
    nc, st = ctx.nc, ctx.stack
    p = Plan(ctx, f"ffn1_{tag}")
    xn_sb = st.enter_context(nc.sbuf_tensor(f"f1_xn_{tag}", [128, 16 * 512], BF16))
    wbuf = [st.enter_context(nc.sbuf_tensor(f"f1_w{i}_{tag}", [128, 16 * 1024], BF16)) for i in range(2)]
    hs = st.enter_context(nc.sbuf_tensor(f"f1_hs_{tag}", [128, 64 * 512], BF16))
    hj = [st.enter_context(nc.sbuf_tensor(f"f1_hj{i}_{tag}", [128, 512], F32)) for i in range(4)]
    aj = [st.enter_context(nc.sbuf_tensor(f"f1_aj{i}_{tag}", [128, 512], F32)) for i in range(4)]
    hg = st.enter_context(nc.sbuf_tensor(f"f1_hg_{tag}", [128, 512], F32))
    hgj = [st.enter_context(nc.sbuf_tensor(f"f1_hgj{i}_{tag}", [128, 512], F32)) for i in range(4)]
    scr = [st.enter_context(nc.sbuf_tensor(f"f1_scr{i}_{tag}", [128, 512], F32)) for i in range(7)]
    wdbuf = wbuf
    gsb = [st.enter_context(nc.sbuf_tensor(f"f1_g{i}_{tag}", [128, 512], F32)) for i in range(2)]
    pg = [st.enter_context(nc.psum_tensor(f"f1_pg{i}_{tag}", [128, 512], F32)) for i in range(4)]
    pu = [st.enter_context(nc.psum_tensor(f"f1_pu{i}_{tag}", [128, 512], F32)) for i in range(4)]
    ps_d = [pg[0], pg[1]]
    sd = p.sem("dma_sync"); so = p.sem("dma_out")
    t_mm = p.sem("t1"); v_h = p.sem("v1"); sc_s = p.sem("sc1"); v_sp = p.sem("v2")
    t_d = p.sem("t2"); sc_d = p.sem("sc2")

    xn_src = xn_dram.rearrange("(kc p) t -> p kc t", p=128)
    wg_src = wg_dram.rearrange("(kc p) n -> p kc n", p=128)
    wu_src = wu_dram.rearrange("(kc p) n -> p kc n", p=128)
    wd_src = wd_dram.rearrange("(ic p) n -> p ic n", p=128)

    for tc in range(4):
        base_v_sp = p.rel(v_sp)
        def ldxn(e, tc=tc):
            return e.dma_start(out=xn_sb[:, :].rearrange("p (kc t) -> p kc t", kc=16),
                               in_=xn_src[:, :, 512*tc:512*(tc+1)])
        waits = []
        if tc > 0:
            waits.append((t_mm, p.rel(t_mm)))
        p.step("sync", ldxn, waits=waits, inc=(sd, 16))
        xn_tgt = p.rel(sd)

        gu = 0
        for c in range(16):
            wb = wbuf[c % 2]
            waits = [(t_mm, p.rel(t_mm) - 4)] if (tc > 0 or c >= 2) else []
            if tc > 0 and c < 2:
                waits.append((t_d, 16 * tc - (1 if c == 0 else 0)))
            wv_ = wb[:, :].rearrange("p (kc two n) -> p kc two n", two=2, n=512)
            p.step("sync", lambda e, c=c, wv_=wv_: e.dma_start(
                out=wv_[:, :, 0, :], in_=wg_src[:, :, 512*c:512*(c+1)]),
                waits=waits, inc=(sd, 16))
            p.step("sync", lambda e, c=c, wv_=wv_: e.dma_start(
                out=wv_[:, :, 1, :], in_=wu_src[:, :, 512*c:512*(c+1)]),
                inc=(sd, 16))
            w_tgt = p.rel(sd)
            for j in range(4):
                pg_ = pg[j]; pu_ = pu[j]
                def mm(e, c=c, j=j, wb=wb, pg_=pg_, pu_=pu_):
                    last = None
                    for kc in range(16):
                        last = e.matmul(pg_[:, :], wb[:, 1024*kc + 128*j: 1024*kc + 128*(j+1)],
                                        xn_sb[:, 512*kc:512*(kc+1)],
                                        start=(kc == 0), stop=(kc == 15))
                    for kc in range(16):
                        last = e.matmul(pu_[:, :], wb[:, 1024*kc + 512 + 128*j: 1024*kc + 512 + 128*(j+1)],
                                        xn_sb[:, 512*kc:512*(kc+1)],
                                        start=(kc == 0), stop=(kc == 15))
                    return last
                waits = [(sd, max(w_tgt, xn_tgt))]
                if tc > 0 or gu >= 4:
                    waits.append((v_h, p.rel(v_h) - 3))
                p.step("tensor", mm, waits=waits, inc=(t_mm, 1))
                p.step("scalar", lambda e, j=j, pg_=pg_: e.activation(
                    out=hgj[j][:, :], in_=pg_[:, :], func=AF.Silu),
                    waits=[(t_mm, p.rel(t_mm)), (v_h, p.rel(v_h) - (3 if (tc > 0 or gu >= 4) else gu))],
                    inc=(sc_s, 1))
                p.step("vector", lambda e, j=j, pu_=pu_: e.tensor_mul(hj[j][:, :], hgj[j][:, :], pu_[:, :]),
                       waits=[(sc_s, p.rel(sc_s))], inc=(v_h, 1))
                gu += 1
            def sabs(e):
                last = None
                for j in range(4):
                    last = e.activation(out=aj[j][:, :], in_=hj[j][:, :], func=AF.Abs)
                return last
            p.step("scalar", sabs, waits=[(v_h, p.rel(v_h))], inc=(sc_s, 1))
            def vsparse(e, c=c, tc=tc):
                MIN = ALU.min
                e.tensor_tensor(scr[0][:, :], aj[0][:, :], aj[1][:, :], MIN)
                e.tensor_tensor(scr[1][:, :], aj[2][:, :], aj[3][:, :], MIN)
                e.tensor_tensor(scr[2][:, :], aj[0][:, :], aj[2][:, :], MIN)
                e.tensor_tensor(scr[3][:, :], aj[1][:, :], aj[3][:, :], MIN)
                e.tensor_tensor(scr[4][:, :], aj[0][:, :], aj[3][:, :], MIN)
                e.tensor_tensor(scr[5][:, :], aj[1][:, :], aj[2][:, :], MIN)
                e.tensor_max(scr[6][:, :], scr[0][:, :], scr[1][:, :])
                e.tensor_max(scr[0][:, :], scr[2][:, :], scr[3][:, :])
                e.tensor_max(scr[3][:, :], scr[4][:, :], scr[5][:, :])
                e.tensor_max(scr[2][:, :], scr[6][:, :], scr[0][:, :])
                e.tensor_max(scr[1][:, :], scr[2][:, :], scr[3][:, :])
                gsel = scr[5]
                last = None
                for j in range(4):
                    e.tensor_tensor(gsel[:, :], aj[j][:, :], scr[1][:, :], ALU.is_ge)
                    last = e.tensor_mul(hs[:, 512*(4*c+j):512*(4*c+j+1)], hj[j][:, :], gsel[:, :])
                return last
            waits = [(sc_s, p.rel(sc_s))]
            if tc > 0:
                waits.append((t_d, 16 * tc))
            p.step("vector", vsparse, waits=waits, inc=(v_sp, 1))
        if tc == 0 and getattr(ctx, "dbg_hs", None) is not None:
            p.step("sync", lambda e: e.dma_start(out=ctx.dbg_hs, in_=hs[:, :]),
                   waits=[(v_sp, base_v_sp + 16)], inc=(so, 16))
        for oc in range(16):
            wdb = wdbuf[oc % 2]
            waits = [(v_sp, base_v_sp + 16)]
            if tc > 0 or oc >= 2:
                waits.append((t_d, p.rel(t_d) - 1))
            p.step("sync", lambda e, oc=oc, wdb=wdb: e.dma_start(
                out=wdb[:, 0:56*128].rearrange("p (ic n) -> p ic n", n=128),
                in_=wd_src[:, 0:56, 128*oc:128*(oc+1)]),
                waits=waits, inc=(sd, 16))
            p.step("sync", lambda e, oc=oc, wdb=wdb: e.dma_start(
                out=wdb[:, 56*128:64*128].rearrange("p (ic n) -> p ic n", n=128),
                in_=wd_src[:, 56:64, 128*oc:128*(oc+1)]),
                inc=(sd, 16))
            wd_tgt = p.rel(sd)
            def mmd(e, oc=oc, wdb=wdb):
                last = None
                for ic in range(64):
                    last = e.matmul(ps_d[oc % 2][:, :], wdb[:, 128*ic:128*(ic+1)],
                                    hs[:, 512*ic:512*(ic+1)],
                                    start=(ic == 0), stop=(ic == 63))
                return last
            waits = [(sd, wd_tgt)]
            if tc > 0 or oc >= 2:
                waits.append((sc_d, p.rel(sc_d) - 1))
            p.step("tensor", mmd, waits=waits, inc=(t_d, 1))
            def ev(e, oc=oc):
                return e.activation(out=gsb[oc % 2][:, :], in_=ps_d[oc % 2][:, :], func=AF.Silu)
            waits = [(t_d, p.rel(t_d))]
            if p.rel(so) >= 32:
                waits.append((so, p.rel(so) - 16))
            p.step("scalar", ev, waits=waits, inc=(sc_d, 1))
            p.step("sync", lambda e, oc=oc, tc=tc: e.dma_start(
                out=g_fm_out[128*oc:128*(oc+1), 512*tc:512*(tc+1)], in_=gsb[oc % 2][:, :]),
                waits=[(sc_d, p.rel(sc_d))], inc=(so, 16))
    p.step("sync", lambda e: e.nop(), waits=[(so, p.rel(so))])
    return p


def make_ffn_l0(ctx, consts, tag, xn_dram, w, g_fm_out, extra_casts=()):
    nc, st = ctx.nc, ctx.stack
    p = Plan(ctx, f"ffn0_{tag}")
    F = 7168; NFS = 56
    xn_sb = st.enter_context(nc.sbuf_tensor(f"f0_xn_{tag}", [128, 16 * 512], BF16))
    wg8_sb = st.enter_context(nc.sbuf_tensor(f"f0_wg8_{tag}", [128, 16 * 8], BF16))
    b2w = st.enter_context(nc.sbuf_tensor(f"f0_b2w_{tag}", [128, 2 * 2048], BF16))
    u_sb = st.enter_context(nc.sbuf_tensor(f"f0_u_{tag}", [128, 128], F32))
    u1_sb = st.enter_context(nc.sbuf_tensor(f"f0_u1_{tag}", [128, 128], F32))
    nmx = st.enter_context(nc.sbuf_tensor(f"f0_nmx_{tag}", [128, 16], F32))
    nm1_sb = st.enter_context(nc.sbuf_tensor(f"f0_nm1_{tag}", [128, 16], F32))
    nm2_sb = st.enter_context(nc.sbuf_tensor(f"f0_nm2_{tag}", [128, 16], F32))
    r_t1 = st.enter_context(nc.sbuf_tensor(f"f0_rt1_{tag}", [128, 128], F32))
    r_t2 = st.enter_context(nc.sbuf_tensor(f"f0_rt2_{tag}", [128, 128], F32))
    nidx = st.enter_context(nc.sbuf_tensor(f"f0_nidx_{tag}", [128, 32], F32))
    idx_sb = st.enter_context(nc.sbuf_tensor(f"f0_idx_{tag}", [128, 32], F32))
    m1_sb = st.enter_context(nc.sbuf_tensor(f"f0_m1_{tag}", [128, 16], F32))
    m2_sb = st.enter_context(nc.sbuf_tensor(f"f0_m2_{tag}", [128, 16], F32))
    negden = st.enter_context(nc.sbuf_tensor(f"f0_den_{tag}", [128, 16], F32))
    rcpd_sb = st.enter_context(nc.sbuf_tensor(f"f0_rcpd_{tag}", [128, 16], F32))
    rw0_sb = st.enter_context(nc.sbuf_tensor(f"f0_rw0_{tag}", [128, 16], F32))
    rw1_sb = st.enter_context(nc.sbuf_tensor(f"f0_rw1_{tag}", [128, 16], F32))
    e0_sb = st.enter_context(nc.sbuf_tensor(f"f0_e0_{tag}", [128, 128], F32))
    e1_sb = st.enter_context(nc.sbuf_tensor(f"f0_e1_{tag}", [128, 128], F32))
    e0t = st.enter_context(nc.sbuf_tensor(f"f0_e0t_{tag}", [8, 512], BF16))
    e1t = st.enter_context(nc.sbuf_tensor(f"f0_e1t_{tag}", [8, 512], BF16))
    rwt = st.enter_context(nc.sbuf_tensor(f"f0_rwt_{tag}", [1, 1024], F32))
    rwb0 = st.enter_context(nc.sbuf_tensor(f"f0_rwb0_{tag}", [128, 512], F32))
    rwb1 = st.enter_context(nc.sbuf_tensor(f"f0_rwb1_{tag}", [128, 512], F32))
    em0 = st.enter_context(nc.sbuf_tensor(f"f0_em0_{tag}", [128, 1024], BF16))
    em1 = st.enter_context(nc.sbuf_tensor(f"f0_em1_{tag}", [128, 1024], BF16))
    ma1 = [st.enter_context(nc.sbuf_tensor(f"f0_ma1{k}_{tag}", [128, 1024], BF16)) for k in range(2)]
    ma3 = [st.enter_context(nc.sbuf_tensor(f"f0_ma3{k}_{tag}", [128, 1024], BF16)) for k in range(2)]
    a2c = st.enter_context(nc.sbuf_tensor(f"f0_a2c_{tag}", [128, 1024], BF16))
    wbf = [st.enter_context(nc.sbuf_tensor(f"f0_wb{i}_{tag}", [128, 8960], BF16)) for i in range(2)]
    xx0 = st.enter_context(nc.sbuf_tensor(f"f0_xx0_{tag}", [128, 512], BF16))
    xx1 = st.enter_context(nc.sbuf_tensor(f"f0_xx1_{tag}", [128, 512], BF16))
    xz = st.enter_context(nc.sbuf_tensor(f"f0_xz_{tag}", [128, NFS * 512], BF16))
    base1s = st.enter_context(nc.sbuf_tensor(f"f0_b1s_{tag}", [128, 512], F32))
    base3s = st.enter_context(nc.sbuf_tensor(f"f0_b3s_{tag}", [128, 512], F32))
    x1s = [st.enter_context(nc.sbuf_tensor(f"f0_x1s{k}_{tag}", [128, 512], F32)) for k in range(2)]
    x3s = [st.enter_context(nc.sbuf_tensor(f"f0_x3s{k}_{tag}", [128, 512], F32)) for k in range(2)]
    sil_s = [st.enter_context(nc.sbuf_tensor(f"f0_sil{k}_{tag}", [128, 512], F32)) for k in range(2)]
    a2acc = [[st.enter_context(nc.sbuf_tensor(f"f0_a2a{k}{i}_{tag}", [128, 1024], F32))
              for i in range(2)] for k in range(2)]
    gs = [st.enter_context(nc.sbuf_tensor(f"f0_gs{i}_{tag}", [128, 512], F32)) for i in range(2)]
    pA = st.enter_context(nc.psum_tensor(f"f0_pA_{tag}", [128, 1024], F32))
    pB0 = st.enter_context(nc.psum_tensor(f"f0_pB0_{tag}", [128, 1024], F32))
    pB1 = st.enter_context(nc.psum_tensor(f"f0_pB1_{tag}", [128, 1024], F32))
    pa2 = st.enter_context(nc.psum_tensor(f"f0_pa2_{tag}", [128, 1024], F32))
    pB = pB0
    pC = pB1
    pD = pa2

    sd = p.sem("dma_sync"); so = p.sem("dma_out")
    t1 = p.sem("t1"); t2 = p.sem("t2"); t3 = p.sem("t3"); t4 = p.sem("t4")
    v1 = p.sem("v1"); v2 = p.sem("v2"); v3 = p.sem("v3"); v4 = p.sem("v4"); v5 = p.sem("v5")
    sc1 = p.sem("sc1"); sc2 = p.sem("sc2"); sc3 = p.sem("sc3")
    gp = p.sem("gp1")

    xn_src = xn_dram.rearrange("(kc p) t -> p kc t", p=128)
    gp2 = p.sem("gp2")
    for (csrc, cdst) in extra_casts:
        p.step("gpsimd", lambda e, csrc=csrc, cdst=cdst: e.dma_start(out=cdst, in_=csrc),
               inc=(gp2, 16))
    p.step("sync", lambda e: e.dma_start(out=wg8_sb[:, :].rearrange("p (kc n) -> p kc n", n=8),
                                         in_=w["wg8"].rearrange("(kc p) n -> p kc n", p=128)), inc=(sd, 16))
    p.step("sync", lambda e: e.dma_start(out=b2w[:, :].rearrange("p (h n) -> p h n", h=2),
                                         in_=w["b2"].rearrange("(h p) n -> p h n", p=128)), inc=(sd, 16))
    res_tgt = p.rel(sd)


    for tcq in range(4):
        p.step("sync", lambda e, tcq=tcq: e.dma_start(
            out=xn_sb[:, :].rearrange("p (kc t) -> p kc t", kc=16),
            in_=xn_src[:, :, 512*tcq:512*(tcq+1)]),
            waits=[(t1, p.rel(t1))], inc=(sd, 16))
        xn_tgt_r = p.rel(sd)
        def rmm(e, tcq=tcq):
            last = None
            for t4_ in range(4):
                col = 32*tcq + 8*t4_
                for kc in range(16):
                    last = e.matmul(pA[:, col:col+8],
                                    xn_sb[:, 512*kc + 128*t4_: 512*kc + 128*(t4_+1)],
                                    wg8_sb[:, 8*kc:8*(kc+1)],
                                    start=(kc == 0), stop=(kc == 15))
            return last
        p.step("tensor", rmm, waits=[(sd, max(res_tgt, xn_tgt_r))], inc=(t1, 1))
    p.step("vector", lambda e: e.tensor_reduce(
        nmx[:, :], pA[:, 0:128].rearrange("p (g n) -> p g n", g=16), AX.X, ALU.max, negate=True),
        waits=[(t1, p.rel(t1))], inc=(v1, 1))
    def sexp(e):
        last = None
        for g in range(16):
            last = e.activation(out=u_sb[:, 8*g:8*(g+1)], in_=pA[:, 8*g:8*(g+1)],
                                func=AF.Exp, bias=nmx[:, g:g+1])
        return last
    p.step("scalar", sexp, waits=[(v1, p.rel(v1))], inc=(sc1, 1))
    p.step("vector", lambda e: e.tensor_reduce(
        nm1_sb[:, :], u_sb[:, :].rearrange("p (g n) -> p g n", g=16), AX.X, ALU.max, negate=True),
        waits=[(sc1, p.rel(sc1))], inc=(v1, 1))
    p.step("scalar", lambda e: e.mul(m1_sb[:, :], nm1_sb[:, :], -1.0),
           waits=[(v1, p.rel(v1))], inc=(sc1, 1))
    def vA(e):
        u3 = u_sb[:, :].rearrange("p (g n) -> p g n", g=16)
        e.tensor_tensor(r_t1[:, :].rearrange("p (g n) -> p g n", g=16), u3,
                        m1_sb[:, :].rearrange("p (g n) -> p g n", g=16)[:, :, 0:1].broadcast_to([128, 16, 8]),
                        ALU.is_ge)
        e.tensor_mul(r_t2[:, :], r_t1[:, :], consts.iotam99[:, :])
        e.tensor_scalar_add(r_t1[:, :], r_t2[:, :], 99.0)
        return e.tensor_reduce(nidx[:, 0:16], r_t1[:, :].rearrange("p (g n) -> p g n", g=16),
                               AX.X, ALU.min)
    p.step("vector", vA, waits=[(sc1, p.rel(sc1))], inc=(v1, 1))
    p.step("scalar", lambda e: e.copy(idx_sb[:, 0:16], nidx[:, 0:16]),
           waits=[(v1, p.rel(v1))], inc=(sc1, 1))
    def vB(e):
        e.tensor_tensor(e0_sb[:, :].rearrange("p (g n) -> p g n", g=16),
                        consts.iota8x4[:, :].rearrange("p (g n) -> p g n", g=16),
                        idx_sb[:, 0:16].rearrange("p (g n) -> p g n", g=16)[:, :, 0:1].broadcast_to([128, 16, 8]),
                        ALU.is_equal)
        e.tensor_scalar_mul(r_t1[:, :], e0_sb[:, :], -1e30)
        e.tensor_add(u1_sb[:, :], u_sb[:, :], r_t1[:, :])
        return e.tensor_reduce(nm2_sb[:, :], u1_sb[:, :].rearrange("p (g n) -> p g n", g=16),
                               AX.X, ALU.max, negate=True)
    p.step("vector", vB, waits=[(sc1, p.rel(sc1))], inc=(v1, 1))
    p.step("scalar", lambda e: e.mul(m2_sb[:, :], nm2_sb[:, :], -1.0),
           waits=[(v1, p.rel(v1))], inc=(sc1, 1))
    def vC(e):
        u3 = u1_sb[:, :].rearrange("p (g n) -> p g n", g=16)
        e.tensor_tensor(r_t1[:, :].rearrange("p (g n) -> p g n", g=16), u3,
                        m2_sb[:, :].rearrange("p (g n) -> p g n", g=16)[:, :, 0:1].broadcast_to([128, 16, 8]),
                        ALU.is_ge)
        e.tensor_mul(r_t2[:, :], r_t1[:, :], consts.iotam99[:, :])
        e.tensor_scalar_add(r_t1[:, :], r_t2[:, :], 99.0)
        return e.tensor_reduce(nidx[:, 16:32], r_t1[:, :].rearrange("p (g n) -> p g n", g=16),
                               AX.X, ALU.min)
    p.step("vector", vC, waits=[(sc1, p.rel(sc1))], inc=(v1, 1))
    p.step("scalar", lambda e: e.copy(idx_sb[:, 16:32], nidx[:, 16:32]),
           waits=[(v1, p.rel(v1))], inc=(sc1, 1))
    def vD(e):
        e.tensor_tensor(e1_sb[:, :].rearrange("p (g n) -> p g n", g=16),
                        consts.iota8x4[:, :].rearrange("p (g n) -> p g n", g=16),
                        idx_sb[:, 16:32].rearrange("p (g n) -> p g n", g=16)[:, :, 0:1].broadcast_to([128, 16, 8]),
                        ALU.is_equal)
        return e.tensor_add(negden[:, :], nm1_sb[:, :], nm2_sb[:, :])
    p.step("vector", vD, waits=[(sc1, p.rel(sc1))], inc=(v1, 1))
    def srcp(e):
        e.activation(out=rcpd_sb[:, :], in_=negden[:, :], func=AF.Ln, scale=-1.0)
        return e.activation(out=rcpd_sb[:, :], in_=rcpd_sb[:, :], func=AF.Exp, scale=-1.0)
    p.step("scalar", srcp, waits=[(v1, p.rel(v1))], inc=(sc1, 1))
    def vE(e):
        e.tensor_mul(rw0_sb[:, :], m1_sb[:, :], rcpd_sb[:, :])
        return e.tensor_mul(rw1_sb[:, :], m2_sb[:, :], rcpd_sb[:, :])
    p.step("vector", vE, waits=[(sc1, p.rel(sc1))], inc=(v1, 1))

    for tc in range(4):
        waits = [(t4, p.rel(t4))] if tc > 0 else [(t1, p.rel(t1))]
        p.step("sync", lambda e, tc=tc: e.dma_start(
            out=xn_sb[:, :].rearrange("p (kc t) -> p kc t", kc=16),
            in_=xn_src[:, :, 512*tc:512*(tc+1)]), waits=waits, inc=(sd, 16))
        xn_tgt = p.rel(sd)

        def ttr(e, tc=tc):
            for t4_ in range(4):
                g = 4*tc + t4_
                e.transpose(pB[0:8, 128*t4_:128*(t4_+1)], e0_sb[:, 8*g:8*(g+1)], consts.ident_f[:, :])
            for t4_ in range(4):
                g = 4*tc + t4_
                e.transpose(pB[0:8, 512 + 128*t4_: 512 + 128*(t4_+1)], e1_sb[:, 8*g:8*(g+1)], consts.ident_f[:, :])
            for t4_ in range(4):
                g = 4*tc + t4_
                e.transpose(pC[0:1, 128*t4_:128*(t4_+1)], rw0_sb[:, g:g+1], consts.ident_f[:, :])
            last = None
            for t4_ in range(4):
                g = 4*tc + t4_
                last = e.transpose(pC[0:1, 512 + 128*t4_: 512 + 128*(t4_+1)], rw1_sb[:, g:g+1], consts.ident_f[:, :])
            return last
        waits = [(v1, p.rel(v1)), (sd, xn_tgt)]
        if tc > 0:
            waits.append((v3, p.rel(v3)))
        p.step("tensor", ttr, waits=waits, inc=(t2, 1))
        def vevt(e):
            e.tensor_copy(e0t[:, :], pB[0:8, 0:512])
            e.tensor_copy(e1t[:, :], pB[0:8, 512:1024])
            return e.tensor_copy(rwt[:, :], pC[0:1, 0:1024])
        p.step("vector", vevt, waits=[(t2, p.rel(t2))], inc=(v2, 1))
        def tbc(e):
            e.matmul(pD[:, 0:512], consts.ones1_f[:, :], rwt[:, 0:512], start=True, stop=True)
            return e.matmul(pD[:, 512:1024], consts.ones1_f[:, :], rwt[:, 512:1024], start=True, stop=True)
        p.step("tensor", tbc, waits=[(v2, p.rel(v2))], inc=(t2, 1))
        def vevb(e):
            e.tensor_copy(rwb0[:, :], pD[:, 0:512])
            return e.tensor_copy(rwb1[:, :], pD[:, 512:1024])
        p.step("vector", vevb, waits=[(t2, p.rel(t2))], inc=(v2, 1))

        a1w = wbf[0][:, 0:4096]
        a3w = wbf[1][:, 0:4096]
        waits = []
        if tc > 0:
            waits.append((t3, p.rel(t3)))
            waits.append((t4, p.rel(t4)))
        def lda13(e):
            e.dma_start(out=wbf[0][:, 0:4096].rearrange("p (kc n) -> p kc n", n=256),
                        in_=w["a1"].rearrange("(kc p) n -> p kc n", p=128)).then_inc(sd.h, 16)
            return e.dma_start(out=wbf[1][:, 0:4096].rearrange("p (kc n) -> p kc n", n=256),
                               in_=w["a3"].rearrange("(kc p) n -> p kc n", p=128))
        p.step("sync", lda13, waits=waits, inc=(sd, 16), manual=((sd, 16),))
        a13_tgt = p.rel(sd)
        def tma(e):
            last = None
            for half in range(2):
                for kc in range(16):
                    last = e.matmul(pA[:, 512*half:512*half+512],
                                    a1w[:, 256*kc + 128*half: 256*kc + 128*(half+1)],
                                    xn_sb[:, 512*kc:512*(kc+1)], start=(kc == 0), stop=(kc == 15))
            for half in range(2):
                for kc in range(16):
                    last = e.matmul(pB[:, 512*half:512*half+512],
                                    a3w[:, 256*kc + 128*half: 256*kc + 128*(half+1)],
                                    xn_sb[:, 512*kc:512*(kc+1)], start=(kc == 0), stop=(kc == 15))
            e.matmul(pC[:, 0:512], consts.b32a[:, :], e0t[:, :], start=True, stop=True)
            e.matmul(pC[:, 512:1024], consts.b32b[:, :], e0t[:, :], start=True, stop=True)
            e.matmul(pD[:, 0:512], consts.b32a[:, :], e1t[:, :], start=True, stop=True)
            last = e.matmul(pD[:, 512:1024], consts.b32b[:, :], e1t[:, :], start=True, stop=True)
            return last
        p.step("tensor", tma, waits=[(v2, p.rel(v2)), (v1, p.rel(v1)), (sd, a13_tgt)], inc=(t3, 1))
        def vem(e):
            e.tensor_copy(em0[:, :], pC[:, :])
            e.tensor_copy(em1[:, :], pD[:, :])
            e.tensor_mul(ma1[0][:, :], pA[:, :], em0[:, :])
            e.tensor_mul(ma1[1][:, :], pA[:, :], em1[:, :])
            e.tensor_mul(ma3[0][:, :], pB[:, :], em0[:, :])
            return e.tensor_mul(ma3[1][:, :], pB[:, :], em1[:, :])
        p.step("vector", vem, waits=[(t3, p.rel(t3))], inc=(v3, 1))

        w1_src = w["w1"].rearrange("(kc p) f -> p kc f", p=128)
        w3_src = w["w3"].rearrange("(kc p) f -> p kc f", p=128)
        b1_src = w["b1"].rearrange("(h p) f -> p h f", p=128)
        b3_src = w["b3"].rearrange("(h p) f -> p h f", p=128)
        a2_src = w["a2"].rearrange("(fs p) n -> p fs n", p=128)
        fs_base_t1 = p.rel(t1); fs_base_sc1 = p.rel(sc1); fs_base_v4 = p.rel(v4)
        fs_base_t4 = p.rel(t4); fs_base_sc3 = p.rel(sc3); fs_base_v5 = p.rel(v5)
        fs_base_t2 = p.rel(t2); fs_base_t3 = p.rel(t3); fs_base_v2 = p.rel(v2)
        fs_base_v3 = p.rel(v3)
        for fs in range(NFS):
            wb = wbf[fs % 2]
            waits = []
            if fs >= 2:
                waits.append((t3, fs_base_t3 + fs - 1))
            else:
                waits.append((t3, fs_base_t3))
            def ld(e, fs=fs, wb=wb):
                e.dma_start(out=wb[:, 0:2048].rearrange("p (kc n) -> p kc n", kc=16),
                            in_=w1_src[:, :, 128*fs:128*(fs+1)]).then_inc(sd.h, 16)
                e.dma_start(out=wb[:, 2048:4096].rearrange("p (kc n) -> p kc n", kc=16),
                            in_=w3_src[:, :, 128*fs:128*(fs+1)]).then_inc(sd.h, 16)
                e.dma_start(out=wb[:, 4096:4352].rearrange("p (h n) -> p h n", h=2),
                            in_=b1_src[:, :, 128*fs:128*(fs+1)]).then_inc(sd.h, 16)
                e.dma_start(out=wb[:, 4352:4608].rearrange("p (h n) -> p h n", h=2),
                            in_=b3_src[:, :, 128*fs:128*(fs+1)]).then_inc(sd.h, 16)
                return e.dma_start(out=wb[:, 4608:4864], in_=a2_src[:, fs, :])
            p.step("sync", ld, waits=waits, inc=(sd, 16), manual=((sd, 64),))
            ld_tgt = p.rel(sd)
            def s1(e, fs=fs, wb=wb):
                last = None
                for kc in range(16):
                    e.matmul(pA[:, 0:512], wb[:, 128*kc:128*(kc+1)],
                             xn_sb[:, 512*kc:512*(kc+1)], start=(kc == 0), stop=(kc == 15))
                for kc in range(16):
                    e.matmul(pA[:, 512:1024], wb[:, 2048 + 128*kc: 2048 + 128*(kc+1)],
                             xn_sb[:, 512*kc:512*(kc+1)], start=(kc == 0), stop=(kc == 15))
                for k, pBk, ma1k, ma3k in ((0, pB0, ma1[0], ma3[0]), (1, pB1, ma1[1], ma3[1])):
                    for h in range(2):
                        e.matmul(pBk[:, 0:512], wb[:, 4096 + 128*h: 4096 + 128*(h+1)],
                                 ma1k[:, 512*h:512*(h+1)], start=(h == 0), stop=(h == 1))
                    for h in range(2):
                        last = e.matmul(pBk[:, 512:1024], wb[:, 4352 + 128*h: 4352 + 128*(h+1)],
                                        ma3k[:, 512*h:512*(h+1)], start=(h == 0), stop=(h == 1))
                return last
            waits = [(sd, ld_tgt), (v3, fs_base_v3)]
            if fs >= 1:
                waits.append((sc1, fs_base_sc1 + fs))
                waits.append((v4, fs_base_v4 + fs))
            p.step("tensor", s1, waits=waits, inc=(t1, 1))
            def s2(e):
                e.copy(base1s[:, :], pA[:, 0:512])
                return e.copy(base3s[:, :], pA[:, 512:1024])
            waits = [(t1, fs_base_t1 + fs + 1)]
            if fs >= 1:
                waits.append((v4, fs_base_v4 + fs))
            p.step("scalar", s2, waits=waits, inc=(sc1, 1))
            def s3(e):
                e.tensor_add(x1s[0][:, :], base1s[:, :], pB0[:, 0:512])
                e.tensor_add(x3s[0][:, :], base3s[:, :], pB0[:, 512:1024])
                e.tensor_add(x1s[1][:, :], base1s[:, :], pB1[:, 0:512])
                return e.tensor_add(x3s[1][:, :], base3s[:, :], pB1[:, 512:1024])
            waits = [(sc1, fs_base_sc1 + fs + 1)]
            if fs >= 1:
                waits.append((sc3, fs_base_sc3 + fs))
            p.step("vector", s3, waits=waits, inc=(v4, 1))
            def s4(e):
                e.activation(out=sil_s[0][:, :], in_=x1s[0][:, :], func=AF.Silu)
                return e.activation(out=sil_s[1][:, :], in_=x1s[1][:, :], func=AF.Silu)
            p.step("scalar", s4, waits=[(v4, fs_base_v4 + fs + 1)], inc=(sc3, 1))
            def s5(e, fs=fs):
                e.tensor_mul(x1s[0][:, :], sil_s[0][:, :], x3s[0][:, :])
                e.tensor_mul(xx0[:, :], x1s[0][:, :], rwb0[:, :])
                e.tensor_mul(x1s[1][:, :], sil_s[1][:, :], x3s[1][:, :])
                e.tensor_mul(xx1[:, :], x1s[1][:, :], rwb1[:, :])
                return e.tensor_add(xz[:, 512*fs:512*(fs+1)], xx0[:, :], xx1[:, :])
            waits = [(sc3, fs_base_sc3 + fs + 1)]
            if fs >= 1:
                waits.append((t3, fs_base_t3 + fs))
            p.step("vector", s5, waits=waits, inc=(v5, 1))
            def s6(e, fs=fs, wb=wb):
                last = None
                for h in range(2):
                    last = e.matmul(pa2[:, 512*h:512*h+512], wb[:, 4608 + 128*h: 4608 + 128*(h+1)],
                                    xx0[:, :], start=True, stop=True)
                return last
            waits = [(v5, fs_base_v5 + fs + 1)]
            if fs >= 1:
                waits.append((v3, fs_base_v3 + fs))
            p.step("tensor", s6, waits=waits, inc=(t2, 1))
            def s7(e, fs=fs):
                dst = a2acc[0][fs % 2]
                if fs == 0:
                    return e.tensor_scalar(dst[:, :], pa2[:, :], 1.0, None, ALU.mult)
                return e.tensor_add(dst[:, :], a2acc[0][1 - fs % 2][:, :], pa2[:, :])
            p.step("vector", s7, waits=[(t2, fs_base_t2 + fs + 1)], inc=(v2, 1))
            def s8(e, fs=fs, wb=wb):
                last = None
                for h in range(2):
                    last = e.matmul(pa2[:, 512*h:512*h+512], wb[:, 4608 + 128*h: 4608 + 128*(h+1)],
                                    xx1[:, :], start=True, stop=True)
                return last
            p.step("tensor", s8, waits=[(v2, fs_base_v2 + fs + 1)], inc=(t3, 1))
            def s9(e, fs=fs):
                dst = a2acc[1][fs % 2]
                if fs == 0:
                    return e.tensor_scalar(dst[:, :], pa2[:, :], 1.0, None, ALU.mult)
                return e.tensor_add(dst[:, :], a2acc[1][1 - fs % 2][:, :], pa2[:, :])
            p.step("vector", s9, waits=[(t3, fs_base_t3 + fs + 1)], inc=(v3, 1))

        def va2(e):
            f0 = a2acc[0][(NFS - 1) % 2]; f1 = a2acc[1][(NFS - 1) % 2]
            e.tensor_mul(x1s[0][:, :], f0[:, 0:512], em0[:, 0:512])
            e.tensor_mul(x1s[1][:, :], f1[:, 0:512], em1[:, 0:512])
            e.tensor_add(a2c[:, 0:512], x1s[0][:, :], x1s[1][:, :])
            e.tensor_mul(x3s[0][:, :], f0[:, 512:1024], em0[:, 512:1024])
            e.tensor_mul(x3s[1][:, :], f1[:, 512:1024], em1[:, 512:1024])
            return e.tensor_add(a2c[:, 512:1024], x3s[0][:, :], x3s[1][:, :])
        p.step("vector", va2, waits=[(v3, p.rel(v3))], inc=(v3, 1))

        w2_src = w["w2"].rearrange("(fs p) n -> p fs n", p=128)
        for oc in range(16):
            wb = wbf[oc % 2]
            waits = [(v5, p.rel(v5))]
            if oc < 2:
                waits.append((t3, p.rel(t3)))
            else:
                waits.append((t4, p.rel(t4) - 1))
            p.step("sync", lambda e, oc=oc, wb=wb: e.dma_start(
                out=wb[:, 0:NFS*128].rearrange("p (fs n) -> p fs n", n=128),
                in_=w2_src[:, :, 128*oc:128*(oc+1)]), waits=waits, inc=(sd, 16))
            wd_tgt = p.rel(sd)
            def dmm2(e, oc=oc, wb=wb):
                par = oc % 2
                last = None
                for fs in range(NFS):
                    last = e.matmul(pA[:, 512*par:512*par+512], wb[:, 128*fs:128*(fs+1)],
                                    xz[:, 512*fs:512*(fs+1)], start=(fs == 0), stop=False)
                last = e.matmul(pA[:, 512*par:512*par+512],
                                b2w[:, 2048*0 + 128*oc: 2048*0 + 128*(oc+1)],
                                a2c[:, 0:512], start=False, stop=False)
                last = e.matmul(pA[:, 512*par:512*par+512],
                                b2w[:, 2048*1 + 128*oc: 2048*1 + 128*(oc+1)],
                                a2c[:, 512:1024], start=False, stop=True)
                return last
            waits = [(sd, wd_tgt), (v3, p.rel(v3)), (sc1, p.rel(sc1))]
            if oc >= 2:
                waits.append((sc2, p.rel(sc2) - 1))
            p.step("tensor", dmm2, waits=waits, inc=(t4, 1))
            def dev(e, oc=oc):
                return e.activation(out=gs[oc % 2][:, :], in_=pA[:, 512*(oc % 2):512*(oc % 2)+512],
                                    func=AF.Silu)
            waits = [(t4, p.rel(t4))]
            if p.rel(so) >= 32:
                waits.append((so, p.rel(so) - 16))
            p.step("scalar", dev, waits=waits, inc=(sc2, 1))
            p.step("sync", lambda e, oc=oc, tc=tc: e.dma_start(
                out=g_fm_out[128*oc:128*(oc+1), 512*tc:512*(tc+1)], in_=gs[oc % 2][:, :]),
                waits=[(sc2, p.rel(sc2))], inc=(so, 16))
    p.step("sync", lambda e: e.nop(), waits=[(so, p.rel(so))])
    return p


def make_combine(ctx, consts, tag, x_fm, s_fm, g_fm, out_fm):
    nc, st = ctx.nc, ctx.stack
    p = Plan(ctx, f"comb_{tag}")
    xb = [st.enter_context(nc.sbuf_tensor(f"cb_x{i}_{tag}", [128, 2048], F32)) for i in range(2)]
    sb = [st.enter_context(nc.sbuf_tensor(f"cb_s{i}_{tag}", [128, 2048], F32)) for i in range(2)]
    gb = [st.enter_context(nc.sbuf_tensor(f"cb_g{i}_{tag}", [128, 2048], F32)) for i in range(2)]
    ob = [st.enter_context(nc.sbuf_tensor(f"cb_o{i}_{tag}", [128, 2048], F32)) for i in range(2)]
    tb = [st.enter_context(nc.sbuf_tensor(f"cb_t{i}_{tag}", [128, 2048], F32)) for i in range(2)]
    sd = p.sem("dma_sync"); sg = p.sem("dma_gp"); so = p.sem("dma_out"); v1 = p.sem("v1")
    for kc in range(16):
        par = kc % 2
        waits = []
        if kc >= 2:
            waits.append((v1, p.rel(v1) - 1))
        def ld(e, kc=kc, par=par):
            e.dma_start(out=xb[par][:, :], in_=x_fm[128*kc:128*(kc+1), :]).then_inc(sd.h, 16)
            return e.dma_start(out=sb[par][:, :], in_=s_fm[128*kc:128*(kc+1), :])
        p.step("sync", ld, waits=waits, inc=(sd, 16), manual=((sd, 16),))
        p.step("gpsimd", lambda e, kc=kc, par=par: e.dma_start(
            out=gb[par][:, :], in_=g_fm[128*kc:128*(kc+1), :]), waits=waits, inc=(sg, 16))
        def cmb(e, par=par):
            e.tensor_mul(tb[par][:, :], sb[par][:, :], gb[par][:, :])
            return e.tensor_add(ob[par][:, :], tb[par][:, :], xb[par][:, :])
        waits2 = [(sd, p.rel(sd)), (sg, p.rel(sg))]
        if kc >= 2:
            waits2.append((so, p.rel(so) - 16))
        p.step("vector", cmb, waits=waits2, inc=(v1, 1))
        p.step("sync", lambda e, kc=kc, par=par: e.dma_start(
            out=out_fm[128*kc:128*(kc+1), :], in_=ob[par][:, :]),
            waits=[(v1, p.rel(v1))], inc=(so, 16))
    p.step("sync", lambda e: e.nop(), waits=[(so, p.rel(so))])
    return p


def make_precast(ctx, tag, pairs, wait=True):
    p = Plan(ctx, f"precast_{tag}")
    sg = p.sem("gp2")
    p.targets = {}
    for (src, dst) in pairs:
        p.step("gpsimd", lambda e, src=src, dst=dst: e.dma_start(out=dst, in_=src), inc=(sg, 16))
        p.targets[id(dst)] = ctx._sems["gp2"].n + p.rel(sg)
    if wait:
        p.step("gpsimd", lambda e: e.nop(), waits=[(sg, p.rel(sg))])
    return p


class Fence(PhaseBase):
    def __init__(self, ctx, sem_name, absolute):
        self.name = f"fence_{sem_name}_{absolute}"
        self.s = ctx.sem(sem_name)
        self.absolute = absolute
        self._totals = {}
    def sync(self, e):
        e.wait_ge(self.s.h, self.absolute)


def make_allgather(ctx, tag, in_ap, out_ap, wait=True):
    p = Plan(ctx, f"ag_{tag}")
    cc = p.sem("cc1")
    p.step("gpsimd", lambda e: e.collective_compute(
        "AllGather", mybir.AluOpType.bypass,
        replica_groups=[[0, 1], [2, 3], [4, 5], [6, 7]],
        ins=[in_ap], outs=[out_ap]), inc=(cc, 1))
    p.target = ctx._sems["cc1"].n + 1
    if wait:
        p.step("gpsimd", lambda e: e.nop(), waits=[(cc, p.rel(cc))])
    return p
from contextlib import ExitStack
from concourse.bass_utils import run_bass_kernel_spmd

POS_H = [
    np.concatenate([np.arange(256*i, 256*i+128) for i in range(16)]),
    np.concatenate([np.arange(256*i+128, 256*i+256) for i in range(16)]),
]
_CM = np.where(np.arange(128)[None, :] > np.arange(128)[:, None],
               np.float32(-1e30), np.float32(0.0))
AMASK_H = [np.concatenate([_CM, np.full((128, 128), -1e30, np.float32)], 1),
           np.concatenate([np.zeros((128, 128), np.float32), _CM], 1)]


def _tabs(pos):
    freqs = 1.0 / 10000.0 ** (np.arange(64, dtype=np.float64) / 64)
    ang = pos[:, None].astype(np.float64) * freqs[None, :]
    c = np.cos(ang).astype(np.float32)
    s = np.sin(ang).astype(np.float32)
    cd = np.ascontiguousarray(c.reshape(16, 128, 64).transpose(1, 0, 2).reshape(128, 1024))
    sd_ = np.ascontiguousarray(s.reshape(16, 128, 64).transpose(1, 0, 2).reshape(128, 1024))
    lp = np.ascontiguousarray((np.log(pos + 1.0) / np.sqrt(128.0)).astype(np.float32)[None, :])
    return cd, sd_, lp


def _cin_dram(nc, cin):
    return {k: nc.dram_tensor("cc_" + k, list(v.shape), BF16 if v.dtype == BF else F32,
                              kind="ExternalInput").ap() for k, v in cin.items()}


def _f32in(nc, name, shape):
    return nc.dram_tensor(name, list(shape), F32, kind="ExternalInput").ap()


def _precast_pair(nc, name, shape):
    s = nc.dram_tensor(name + "_f", list(shape), F32, kind="ExternalInput").ap()
    d = nc.dram_tensor(name + "_b", list(shape), BF16, kind="Internal").ap()
    return s, d


def _tabs_phase(ctx, cosd, sind, cos_sb, sin_sb):
    pl = Plan(ctx, "tabs")
    sd = pl.sem("dma_sync")
    pl.step("sync", lambda e: e.dma_start(out=cos_sb[:, :], in_=cosd), inc=(sd, 16))
    pl.step("sync", lambda e: e.dma_start(out=sin_sb[:, :], in_=sind), inc=(sd, 16))
    pl.step("sync", lambda e: e.nop(), waits=[(sd, pl.rel(sd))])
    return pl


def _build_launch_a(cin):
    nc = bass.Bass("TRN2", target_bir_lowering=False, debug=False)
    x = _f32in(nc, "x", [2048, 2048])
    wq = _precast_pair(nc, "wq", [2048, 2048])
    wk = _precast_pair(nc, "wk", [2048, 2048])
    wv = _precast_pair(nc, "wv", [2048, 2048])
    cosd = _f32in(nc, "cosd", [128, 1024]); sind = _f32in(nc, "sind", [128, 1024])
    logpos = _f32in(nc, "logpos", [1, 2048]); scaler = _f32in(nc, "scaler", [1, 16])
    xn0 = nc.dram_tensor("xn0", [2048, 2048], BF16, kind="ExternalOutput").ap()
    q0 = nc.dram_tensor("q0", [2048, 2048], BF16, kind="ExternalOutput").ap()
    kv0 = nc.dram_tensor("kv0", [4096, 2048], BF16, kind="ExternalOutput").ap()
    caps = _cin_dram(nc, cin)
    with ExitStack() as stack:
        ctx = Ctx(nc, stack)
        consts = ConstPool(ctx, caps)
        cos_sb = stack.enter_context(nc.sbuf_tensor("cos_sb", [128, 1024], F32))
        sin_sb = stack.enter_context(nc.sbuf_tensor("sin_sb", [128, 1024], F32))
        qscale_sb = stack.enter_context(nc.sbuf_tensor("qscale_sb", [128, 256], F32))
        ctx.run_phase(consts)
        ctx.run_phase(make_precast(ctx, "a", [wq, wk, wv]))
        ctx.run_phase(_tabs_phase(ctx, cosd, sind, cos_sb, sin_sb))
        with ctx.scope():
            ctx.run_phase(make_qscale_phase(ctx, consts, logpos, scaler, qscale_sb))
        with ctx.scope():
            ctx.run_phase(LNPhase(ctx, consts, x, xn0, "l0"))
        with ctx.scope():
            jobs = [{"w": wq[1], "out": q0, "mode": "q0"},
                    {"w": wk[1], "out": kv0[0:2048, :], "mode": "k0"},
                    {"w": wv[1], "out": kv0[2048:4096, :], "mode": "v"}]
            ctx.run_phase(make_qkv_phase(ctx, consts, "a", xn0, jobs, cos_sb, sin_sb, qscale_sb))
    return nc


def _build_launch_b(cin):
    nc = bass.Bass("TRN2", target_bir_lowering=False, debug=False)
    x = _f32in(nc, "x", [2048, 2048])
    xn0 = nc.dram_tensor("xn0", [2048, 2048], BF16, kind="ExternalInput").ap()
    q0 = nc.dram_tensor("q0", [2048, 2048], BF16, kind="ExternalInput").ap()
    ktm0 = nc.dram_tensor("ktm0", [4096, 2048], BF16, kind="ExternalInput").ap()
    vtm0 = nc.dram_tensor("vtm0", [4096, 2048], BF16, kind="ExternalInput").ap()
    lam = [_f32in(nc, f"lam{i}", [1, 64]) for i in range(4)]
    amask = _f32in(nc, "amask", [128, 256])
    cosd = _f32in(nc, "cosd", [128, 1024]); sind = _f32in(nc, "sind", [128, 1024])
    shapes0 = {"wg8": [2048, 8], "w1": [2048, 7168], "w3": [2048, 7168], "w2": [7168, 2048],
               "a1": [2048, 256], "a3": [2048, 256], "b1": [256, 7168], "b3": [256, 7168],
               "a2": [7168, 256], "b2": [256, 2048]}
    wp = {k: _precast_pair(nc, k, v) for k, v in shapes0.items()}
    wq1 = _precast_pair(nc, "wq1", [2048, 2048])
    wk1 = _precast_pair(nc, "wk1", [2048, 2048])
    wv1 = _precast_pair(nc, "wv1", [2048, 2048])
    x1 = nc.dram_tensor("x1", [2048, 2048], F32, kind="ExternalOutput").ap()
    xn1 = nc.dram_tensor("xn1", [2048, 2048], BF16, kind="ExternalOutput").ap()
    q1 = nc.dram_tensor("q1", [2048, 2048], BF16, kind="ExternalOutput").ap()
    kv1 = nc.dram_tensor("kv1", [4096, 2048], BF16, kind="ExternalOutput").ap()
    s0 = nc.dram_tensor("s0", [2048, 2048], F32, kind="Internal").ap()
    g0 = nc.dram_tensor("g0", [2048, 2048], F32, kind="Internal").ap()
    caps = _cin_dram(nc, cin)
    with ExitStack() as stack:
        ctx = Ctx(nc, stack)
        consts = ConstPool(ctx, caps)
        cos_sb = stack.enter_context(nc.sbuf_tensor("cos_sb", [128, 1024], F32))
        sin_sb = stack.enter_context(nc.sbuf_tensor("sin_sb", [128, 1024], F32))
        neglam = stack.enter_context(nc.sbuf_tensor("neglam", [128, 1], F32))
        ctx.run_phase(consts)
        ctx.run_phase(make_precast(ctx, "b", list(wp.values()) + [wq1, wk1, wv1]))
        ctx.run_phase(_tabs_phase(ctx, cosd, sind, cos_sb, sin_sb))
        with ctx.scope():
            ctx.run_phase(make_lambda_phase(ctx, consts, lam, neglam))
        with ctx.scope():
            ctx.run_phase(make_attn_l0(ctx, consts, "b", q0, ktm0, vtm0, amask, neglam, s0))
        with ctx.scope():
            ctx.run_phase(make_ffn_l0(ctx, consts, "b", xn0,
                                      {k: v[1] for k, v in wp.items()}, g0,
                                      extra_casts=l1_casts))
        with ctx.scope():
            ctx.run_phase(make_combine(ctx, consts, "b", x, s0, g0, x1))
        with ctx.scope():
            ctx.run_phase(LNPhase(ctx, consts, x1, xn1, "l1"))
        with ctx.scope():
            jobs = [{"w": wq1[1], "out": q1, "mode": "q1"},
                    {"w": wk1[1], "out": kv1[0:2048, :], "mode": "k1"},
                    {"w": wv1[1], "out": kv1[2048:4096, :], "mode": "v"}]
            ctx.run_phase(make_qkv_phase(ctx, consts, "b", xn1, jobs, cos_sb, sin_sb, None))
    return nc


def _build_launch_c(cin):
    nc = bass.Bass("TRN2", target_bir_lowering=False, debug=False)
    x1 = _f32in(nc, "x1", [2048, 2048])
    xn1 = nc.dram_tensor("xn1", [2048, 2048], BF16, kind="ExternalInput").ap()
    q1 = nc.dram_tensor("q1", [2048, 2048], BF16, kind="ExternalInput").ap()
    ktm1 = nc.dram_tensor("ktm1", [4096, 2048], BF16, kind="ExternalInput").ap()
    vtm1 = nc.dram_tensor("vtm1", [4096, 2048], BF16, kind="ExternalInput").ap()
    amask = _f32in(nc, "amask", [128, 256])
    wo = _precast_pair(nc, "wo", [2048, 2048])
    wgte = _precast_pair(nc, "wgte", [2048, 8192])
    wup = _precast_pair(nc, "wup", [2048, 8192])
    wdn = _precast_pair(nc, "wdn", [8192, 2048])
    out = nc.dram_tensor("out", [2048, 2048], F32, kind="ExternalOutput").ap()
    s1 = nc.dram_tensor("s1", [2048, 2048], F32, kind="Internal").ap()
    g1 = nc.dram_tensor("g1", [2048, 2048], F32, kind="Internal").ap()
    caps = _cin_dram(nc, cin)
    with ExitStack() as stack:
        ctx = Ctx(nc, stack)
        consts = ConstPool(ctx, caps)
        ctx.run_phase(consts)
        ctx.run_phase(make_precast(ctx, "c", [wo, wgte, wup, wdn]))
        with ctx.scope():
            o_fm = ctx.stack.enter_context(nc.sbuf_tensor("o_fm", [128, 16 * 2048], BF16))
            with ctx.scope():
                ctx.run_phase(make_attn_l1(ctx, consts, "c", q1, ktm1, vtm1, amask, o_fm))
            with ctx.scope():
                ctx.run_phase(make_wo_phase(ctx, consts, "c", o_fm, wo[1], s1))
        with ctx.scope():
            ctx.run_phase(make_ffn_l1(ctx, consts, "c", xn1, wgte[1], wup[1], wdn[1], g1))
        with ctx.scope():
            ctx.run_phase(make_combine(ctx, consts, "c", x1, s1, g1, out))
    return nc


_cache = {}


def kernel(**inputs):
    cin = host_consts()
    cink = {"cc_" + k: v for k, v in cin.items()}
    C = np.ascontiguousarray

    x = np.asarray(inputs["L0_attn_x"], np.float32)
    wq_d = C(np.asarray(inputs["L0_attn_Wq"], np.float32))
    wk_d = C(np.asarray(inputs["L0_attn_Wk"], np.float32))
    wv_d = C(np.asarray(inputs["L0_attn_Wv"], np.float32))
    scaler_d = C(np.asarray(inputs["L0_attn_softmax_scaler"], np.float32).reshape(1, 16))
    lam_d = [C(np.asarray(inputs[k], np.float32).reshape(1, 64)) for k in
             ("L0_attn_lambda_q1", "L0_attn_lambda_k1", "L0_attn_lambda_q2", "L0_attn_lambda_k2")]
    wg8_d = C(np.asarray(inputs["L0_ffn_Wg"], np.float32).T)
    w1_d = C(np.asarray(inputs["L0_ffn_W1"], np.float32).T)
    w3_d = C(np.asarray(inputs["L0_ffn_W3"], np.float32).T)
    w2_d = C(np.asarray(inputs["L0_ffn_W2"], np.float32).T)
    a1_d = C(np.asarray(inputs["L0_ffn_A1"], np.float32).transpose(2, 0, 1).reshape(2048, 256))
    a3_d = C(np.asarray(inputs["L0_ffn_A3"], np.float32).transpose(2, 0, 1).reshape(2048, 256))
    b1_d = C(np.asarray(inputs["L0_ffn_B1"], np.float32).transpose(0, 2, 1).reshape(256, 7168))
    b3_d = C(np.asarray(inputs["L0_ffn_B3"], np.float32).transpose(0, 2, 1).reshape(256, 7168))
    a2_d = C(np.asarray(inputs["L0_ffn_A2"], np.float32).transpose(2, 0, 1).reshape(7168, 256))
    b2_d = C(np.asarray(inputs["L0_ffn_B2"], np.float32).transpose(0, 2, 1).reshape(256, 2048))
    wq1_d = C(np.asarray(inputs["L1_attn_wq"], np.float32).reshape(2048, 2048))
    wk1_d = C(np.asarray(inputs["L1_attn_wk"], np.float32).reshape(2048, 2048))
    wv1_d = C(np.asarray(inputs["L1_attn_wv"], np.float32).reshape(2048, 2048))
    wo_d = C(np.asarray(inputs["L1_attn_wo"], np.float32).reshape(2048, 2048))
    wg_d = C(np.asarray(inputs["L1_ffn_w_gate"], np.float32).reshape(2048, 16, 128, 4)
             .transpose(0, 1, 3, 2).reshape(2048, 8192))
    wu_d = C(np.asarray(inputs["L1_ffn_w_up"], np.float32).reshape(2048, 16, 128, 4)
             .transpose(0, 1, 3, 2).reshape(2048, 8192))
    wd_d = C(np.asarray(inputs["L1_ffn_w_down"], np.float32).reshape(16, 128, 4, 2048)
             .transpose(0, 2, 1, 3).reshape(8192, 2048))

    tabs = [_tabs(POS_H[0]), _tabs(POS_H[1])]
    x_core = [C(x[c // 2][POS_H[c % 2]].T) for c in range(8)]

    if FUSED:
        wdevs = {"wq_f": wq_d, "wk_f": wk_d, "wv_f": wv_d,
                 "wg8_f": wg8_d, "w1_f": w1_d, "w3_f": w3_d, "w2_f": w2_d,
                 "a1_f": a1_d, "a3_f": a3_d, "b1_f": b1_d, "b3_f": b3_d,
                 "a2_f": a2_d, "b2_f": b2_d,
                 "wq1_f": wq1_d, "wk1_f": wk1_d, "wv1_f": wv1_d,
                 "wo_f": wo_d, "wgte_f": wg_d, "wup_f": wu_d, "wdn_f": wd_d}
        return kernel_fused(inputs, cin, cink, (x_core, tabs, wdevs, scaler_d, lam_d))

    if "A" not in _cache:
        _cache["A"] = _build_launch_a(cin)
    in_maps = []
    for c in range(8):
        h = c % 2
        in_maps.append({"x": x_core[c], "wq_f": wq_d, "wk_f": wk_d, "wv_f": wv_d,
                        "cosd": tabs[h][0], "sind": tabs[h][1], "logpos": tabs[h][2],
                        "scaler": scaler_d, **cink})
    ra = run_bass_kernel_spmd(_cache["A"], in_maps, core_ids=list(range(8)))
    kt0, vt0 = [], []
    for b in range(4):
        k_full = np.concatenate([ra.results[2*b]["kv0"][0:2048], ra.results[2*b+1]["kv0"][0:2048]], 0)
        v_full = np.concatenate([ra.results[2*b]["kv0"][2048:4096], ra.results[2*b+1]["kv0"][2048:4096]], 0)
        kt0.append(k_full); vt0.append(v_full)

    if "B" not in _cache:
        _cache["B"] = _build_launch_b(cin)
    in_maps = []
    for c in range(8):
        b, h = c // 2, c % 2
        in_maps.append({"x": x_core[c], "xn0": ra.results[c]["xn0"], "q0": ra.results[c]["q0"],
                        "ktm0": kt0[b], "vtm0": vt0[b], "amask": AMASK_H[h],
                        **{f"lam{i}": lam_d[i] for i in range(4)},
                        "cosd": tabs[h][0], "sind": tabs[h][1],
                        "wg8_f": wg8_d, "w1_f": w1_d, "w3_f": w3_d, "w2_f": w2_d,
                        "a1_f": a1_d, "a3_f": a3_d, "b1_f": b1_d, "b3_f": b3_d,
                        "a2_f": a2_d, "b2_f": b2_d,
                        "wq1_f": wq1_d, "wk1_f": wk1_d, "wv1_f": wv1_d, **cink})
    resb = run_bass_kernel_spmd(_cache["B"], in_maps, core_ids=list(range(8)))
    rb = list(resb.results)

    kt1, vt1 = [], []
    for b in range(4):
        kt1.append(np.concatenate([rb[2*b]["kv1"][0:2048], rb[2*b+1]["kv1"][0:2048]], 0))
        vt1.append(np.concatenate([rb[2*b]["kv1"][2048:4096], rb[2*b+1]["kv1"][2048:4096]], 0))

    if "C" not in _cache:
        _cache["C"] = _build_launch_c(cin)
    ims = []
    for c in range(8):
        b, h = c // 2, c % 2
        ims.append({"x1": rb[c]["x1"], "xn1": rb[c]["xn1"], "q1": rb[c]["q1"],
                    "ktm1": kt1[b], "vtm1": vt1[b], "amask": AMASK_H[h],
                    "wo_f": wo_d, "wgte_f": wg_d, "wup_f": wu_d, "wdn_f": wd_d, **cink})
    resc = run_bass_kernel_spmd(_cache["C"], ims, core_ids=list(range(8)))
    rc = list(resc.results)

    out = np.empty((4, 4096, 2048), np.float32)
    for c in range(8):
        b, h = c // 2, c % 2
        out[b][POS_H[h]] = rc[c]["out"].T
    return out


def _build_fused(cin):
    nc = bass.Bass("TRN2", target_bir_lowering=False, debug=False)
    x = _f32in(nc, "x", [2048, 2048])
    wq = _precast_pair(nc, "wq", [2048, 2048])
    wk = _precast_pair(nc, "wk", [2048, 2048])
    wv = _precast_pair(nc, "wv", [2048, 2048])
    cosd = _f32in(nc, "cosd", [128, 1024]); sind = _f32in(nc, "sind", [128, 1024])
    logpos = _f32in(nc, "logpos", [1, 2048]); scaler = _f32in(nc, "scaler", [1, 16])
    lam = [_f32in(nc, f"lam{i}", [1, 64]) for i in range(4)]
    amask = _f32in(nc, "amask", [128, 256])
    shapes0 = {"wg8": [2048, 8], "w1": [2048, 7168], "w3": [2048, 7168], "w2": [7168, 2048],
               "a1": [2048, 256], "a3": [2048, 256], "b1": [256, 7168], "b3": [256, 7168],
               "a2": [7168, 256], "b2": [256, 2048]}
    wp = {k: _precast_pair(nc, k, v) for k, v in shapes0.items()}
    wq1 = _precast_pair(nc, "wq1", [2048, 2048])
    wk1 = _precast_pair(nc, "wk1", [2048, 2048])
    wv1 = _precast_pair(nc, "wv1", [2048, 2048])
    wo = _precast_pair(nc, "wo", [2048, 2048])
    wgte = _precast_pair(nc, "wgte", [2048, 8192])
    wup = _precast_pair(nc, "wup", [2048, 8192])
    wdn = _precast_pair(nc, "wdn", [8192, 2048])
    out = nc.dram_tensor("out", [2048, 2048], F32, kind="ExternalOutput").ap()
    xn0 = nc.dram_tensor("xn0", [2048, 2048], BF16, kind="Internal").ap()
    q0 = nc.dram_tensor("q0", [2048, 2048], BF16, kind="Internal").ap()
    kv0 = nc.dram_tensor("kv0", [4096, 2048], BF16, kind="Internal").ap()
    kvf0 = nc.dram_tensor("kvf0", [8192, 2048], BF16, kind="Internal").ap()
    s0 = nc.dram_tensor("s0", [2048, 2048], F32, kind="Internal").ap()
    g0 = nc.dram_tensor("g0", [2048, 2048], F32, kind="Internal").ap()
    x1 = nc.dram_tensor("x1", [2048, 2048], F32, kind="Internal").ap()
    xn1 = nc.dram_tensor("xn1", [2048, 2048], BF16, kind="Internal").ap()
    q1 = nc.dram_tensor("q1", [2048, 2048], BF16, kind="Internal").ap()
    kv1 = nc.dram_tensor("kv1", [4096, 2048], BF16, kind="Internal").ap()
    kvf1 = nc.dram_tensor("kvf1", [8192, 2048], BF16, kind="Internal").ap()
    s1 = nc.dram_tensor("s1", [2048, 2048], F32, kind="Internal").ap()
    g1 = nc.dram_tensor("g1", [2048, 2048], F32, kind="Internal").ap()
    caps = _cin_dram(nc, cin)
    with ExitStack() as stack:
        ctx = Ctx(nc, stack)
        consts = ConstPool(ctx, caps)
        cos_sb = stack.enter_context(nc.sbuf_tensor("cos_sb", [128, 1024], F32))
        sin_sb = stack.enter_context(nc.sbuf_tensor("sin_sb", [128, 1024], F32))
        qscale_sb = stack.enter_context(nc.sbuf_tensor("qscale_sb", [128, 256], F32))
        neglam = stack.enter_context(nc.sbuf_tensor("neglam", [128, 1], F32))
        ctx.run_phase(consts)
        pc = make_precast(ctx, "all", [wq, wk, wv] + list(wp.values()), wait=False)
        ctx.run_phase(pc)
        l1_casts = [wq1, wk1, wv1, wo, wgte, wup, wdn]
        l1_tgt = {id(pair[1]): ctx._sems["gp2"].n + 16 * (i + 1)
                  for i, pair in enumerate(l1_casts)}
        ctx.run_phase(_tabs_phase(ctx, cosd, sind, cos_sb, sin_sb))
        with ctx.scope():
            ctx.run_phase(make_qscale_phase(ctx, consts, logpos, scaler, qscale_sb))
        with ctx.scope():
            ctx.run_phase(make_lambda_phase(ctx, consts, lam, neglam))
        with ctx.scope():
            ctx.run_phase(LNPhase(ctx, consts, x, xn0, "l0"))
        ctx.run_phase(Fence(ctx, "gp2", pc.targets[id(wv[1])]))
        with ctx.scope():
            jobs = [{"w": wq[1], "out": q0, "mode": "q0"},
                    {"w": wk[1], "out": kv0[0:2048, :], "mode": "k0"},
                    {"w": wv[1], "out": kv0[2048:4096, :], "mode": "v"}]
            ctx.run_phase(make_qkv_phase(ctx, consts, "a", xn0, jobs, cos_sb, sin_sb, qscale_sb))
        ag0 = make_allgather(ctx, "kv0", kv0, kvf0, wait=True)
        ctx.run_phase(ag0)
        ctx.run_phase(Fence(ctx, "gp2", pc.targets[id(wp["b2"][1])]))
        with ctx.scope():
            ctx.run_phase(make_ffn_l0(ctx, consts, "b", xn0,
                                      {k: v[1] for k, v in wp.items()}, g0,
                                      extra_casts=l1_casts))
        ctx.run_phase(Fence(ctx, "cc1", ag0.target))
        with ctx.scope():
            kblk = [kvf0[0:2048, :], kvf0[4096:6144, :]]
            vblk = [kvf0[2048:4096, :], kvf0[6144:8192, :]]
            ctx.run_phase(make_attn_l0(ctx, consts, "b", q0, kblk, vblk, amask, neglam, s0))
        with ctx.scope():
            ctx.run_phase(make_combine(ctx, consts, "b", x, s0, g0, x1))
        with ctx.scope():
            ctx.run_phase(LNPhase(ctx, consts, x1, xn1, "l1"))
        ctx.run_phase(Fence(ctx, "gp2", l1_tgt[id(wv1[1])]))
        with ctx.scope():
            jobs = [{"w": wq1[1], "out": q1, "mode": "q1"},
                    {"w": wk1[1], "out": kv1[0:2048, :], "mode": "k1"},
                    {"w": wv1[1], "out": kv1[2048:4096, :], "mode": "v"}]
            ctx.run_phase(make_qkv_phase(ctx, consts, "b", xn1, jobs, cos_sb, sin_sb, None))
        ag1 = make_allgather(ctx, "kv1", kv1, kvf1, wait=True)
        ctx.run_phase(ag1)
        ctx.run_phase(Fence(ctx, "gp2", l1_tgt[id(wdn[1])]))
        with ctx.scope():
            ctx.run_phase(make_ffn_l1(ctx, consts, "c", xn1, wgte[1], wup[1], wdn[1], g1))
        ctx.run_phase(Fence(ctx, "cc1", ag1.target))
        with ctx.scope():
            o_fm = ctx.stack.enter_context(nc.sbuf_tensor("o_fm", [128, 16 * 2048], BF16))
            kblk = [kvf1[0:2048, :], kvf1[4096:6144, :]]
            vblk = [kvf1[2048:4096, :], kvf1[6144:8192, :]]
            with ctx.scope():
                ctx.run_phase(make_attn_l1(ctx, consts, "c", q1, kblk, vblk, amask, o_fm))
            with ctx.scope():
                ctx.run_phase(make_wo_phase(ctx, consts, "c", o_fm, wo[1], s1))
        with ctx.scope():
            ctx.run_phase(make_combine(ctx, consts, "c", x1, s1, g1, out))
    return nc


FUSED = True


def kernel_fused(inputs, cin, cink, layouts):
    (x_core, tabs, wdevs, scaler_d, lam_d) = layouts
    if "F" not in _cache:
        _cache["F"] = _build_fused(cin)
    in_maps = []
    for c in range(8):
        h = c % 2
        im = {"x": x_core[c], "cosd": tabs[h][0], "sind": tabs[h][1], "logpos": tabs[h][2],
              "scaler": scaler_d, "amask": AMASK_H[h],
              **{f"lam{i}": lam_d[i] for i in range(4)}, **cink}
        im.update(wdevs)
        in_maps.append(im)
    res = run_bass_kernel_spmd(_cache["F"], in_maps, core_ids=list(range(8)))
    out = np.empty((4, 4096, 2048), np.float32)
    for c in range(8):
        b, h = c // 2, c % 2
        out[b][POS_H[h]] = res.results[c]["out"].T
    return out


# version 13 (final)
# speedup vs baseline: 1.0803x; 1.0213x over previous
import numpy as np
import ml_dtypes
import concourse.bass as bass
import concourse.mybir as mybir

F32 = mybir.dt.float32
BF16 = mybir.dt.bfloat16
AF = mybir.ActivationFunctionType
ALU = mybir.AluOpType
AX = mybir.AxisListType
BF = ml_dtypes.bfloat16

D = 2048
T_CORE = 2048
NH, HD = 16, 128
ENGS = ("sync", "gpsimd", "vector", "scalar", "tensor")


class Sem:
    def __init__(self, h, name):
        self.h = h; self.n = 0; self.name = name


class Ctx:
    def __init__(self, nc, stack):
        self.nc = nc
        self.stack = stack
        self.perm = stack
        self._sems = {}
        self.phase_idx = 0
        self.s_phase = self.sem("phase")

    def sem(self, name):
        if name not in self._sems:
            h = self.perm.enter_context(self.nc.semaphore(name))
            self._sems[name] = Sem(h, name)
        return self._sems[name]

    def scope(self):
        import contextlib
        @contextlib.contextmanager
        def cm():
            from contextlib import ExitStack
            old = self.stack
            with ExitStack() as st:
                self.stack = st
                try:
                    yield st
                finally:
                    self.stack = old
        return cm()

    def run_phase(self, phase):
        i = self.phase_idx
        sp = self.s_phase
        phase.begin({k: s.n for k, s in self._sems.items()})
        with self.nc.Block(name=getattr(phase, "name", f"ph{i}")) as block:
            for eng_name in ENGS:
                body = getattr(phase, eng_name, None)
                def mk(body=body):
                    def run(e):
                        if i > 0:
                            e.wait_ge(sp.h, 5 * i)
                        if body is not None:
                            body(e)
                        e.nop().then_inc(sp.h, 1)
                    return run
                getattr(block, eng_name)(mk())
        for name, total in phase.totals().items():
            self._sems[name].n += total
        sp.n += 5
        self.phase_idx += 1


class PhaseBase:
    def begin(self, bases):
        self.b = bases
    def totals(self):
        return self._totals
    def w(self, e, sem, rel):
        e.wait_ge(sem.h, self.b.get(sem.name, 0) + rel)


def host_consts():
    c = {}
    c["ones_bf"] = np.ones((128, 1), dtype=BF)
    c["ones1_f"] = np.ones((1, 128), dtype=np.float32)
    c["ident_bf"] = np.eye(128, dtype=BF)
    c["iota8"] = np.tile(np.arange(8, dtype=np.float32)[None, :], (128, 1))
    c["cmask"] = np.where(np.arange(128)[None, :] > np.arange(128)[:, None],
                          np.float32(-1e30), np.float32(0.0))
    c["ident_f"] = np.eye(128, dtype=np.float32)
    c["iota8x4"] = np.tile(np.arange(8, dtype=np.float32)[None, :], (128, 16))
    c["iotam99"] = c["iota8x4"] - 99.0
    m = np.arange(128)
    c["b32a"] = np.stack([(m // 32 == e).astype(BF) for e in range(4)] +
                         [np.zeros(128, BF) for _ in range(4)], 0)
    c["b32b"] = np.stack([np.zeros(128, BF) for _ in range(4)] +
                         [(m // 32 == (e - 4)).astype(BF) for e in range(4, 8)], 0)
    return c


class ConstPool(PhaseBase):
    name = "consts"
    def __init__(self, ctx, aps):
        nc, st = ctx.nc, ctx.stack
        self.ones_bf = st.enter_context(nc.sbuf_tensor("c_ones_bf", [128, 1], BF16))
        self.ones1_f = st.enter_context(nc.sbuf_tensor("c_ones1_f", [1, 128], F32))
        self.ident = st.enter_context(nc.sbuf_tensor("c_ident", [128, 128], BF16))
        self.iota8 = st.enter_context(nc.sbuf_tensor("c_iota8", [128, 8], F32))
        self.eps_rms = st.enter_context(nc.sbuf_tensor("c_eps_rms", [128, 1], F32))
        self.cmask = st.enter_context(nc.sbuf_tensor("c_cmask", [128, 128], F32))
        self.ident_f = st.enter_context(nc.sbuf_tensor("c_ident_f", [128, 128], F32))
        self.iota8x4 = st.enter_context(nc.sbuf_tensor("c_iota84", [128, 128], F32))
        self.iotam99 = st.enter_context(nc.sbuf_tensor("c_iom99", [128, 128], F32))
        self.b32a = st.enter_context(nc.sbuf_tensor("c_b32a", [8, 128], BF16))
        self.b32b = st.enter_context(nc.sbuf_tensor("c_b32b", [8, 128], BF16))
        self.aps = aps
        self.s = ctx.sem("dma_sync")
        self.sv = ctx.sem("v1")
        self._totals = {"dma_sync": 160, "v1": 1}
    def sync(self, e):
        e.dma_start(out=self.ones_bf[:, :], in_=self.aps["ones_bf"]).then_inc(self.s.h, 16)
        e.dma_start(out=self.ones1_f[:, :], in_=self.aps["ones1_f"]).then_inc(self.s.h, 16)
        e.dma_start(out=self.ident[:, :], in_=self.aps["ident_bf"]).then_inc(self.s.h, 16)
        e.dma_start(out=self.iota8[:, :], in_=self.aps["iota8"]).then_inc(self.s.h, 16)
        e.dma_start(out=self.cmask[:, :], in_=self.aps["cmask"]).then_inc(self.s.h, 16)
        e.dma_start(out=self.ident_f[:, :], in_=self.aps["ident_f"]).then_inc(self.s.h, 16)
        e.dma_start(out=self.iota8x4[:, :], in_=self.aps["iota8x4"]).then_inc(self.s.h, 16)
        e.dma_start(out=self.iotam99[:, :], in_=self.aps["iotam99"]).then_inc(self.s.h, 16)
        e.dma_start(out=self.b32a[:, :], in_=self.aps["b32a"]).then_inc(self.s.h, 16)
        e.dma_start(out=self.b32b[:, :], in_=self.aps["b32b"]).then_inc(self.s.h, 16)
        self.w(e, self.s, 160)
        self.w(e, self.sv, 1)

    def vector(self, e):
        e.memset(self.eps_rms[:, :], 1.1920929e-7).then_inc(self.sv.h, 1)


class LNPhase(PhaseBase):
    NCH = 4

    def __init__(self, ctx, consts, x_fm, xn_out, tag):
        self.c = consts
        self.x = x_fm; self.xn = xn_out
        self.name = f"ln_{tag}"
        nc, st = ctx.nc, ctx.stack
        t = tag
        self.xf = st.enter_context(nc.sbuf_tensor(f"ln_xf_{t}", [128, 16 * 512], F32))
        self.xb = st.enter_context(nc.sbuf_tensor(f"ln_xb_{t}", [128, 16 * 512], BF16))
        self.x2b = st.enter_context(nc.sbuf_tensor(f"ln_x2b_{t}", [128, 16 * 512], BF16))
        self.sm = st.enter_context(nc.sbuf_tensor(f"ln_sm_{t}", [1, 512 * 4], F32))
        self.xnb = st.enter_context(nc.sbuf_tensor(f"ln_xnb_{t}", [128, 16 * 512], BF16))
        self.scr = st.enter_context(nc.sbuf_tensor(f"ln_scr_{t}", [128, 512], F32))
        self.ps_s1 = st.enter_context(nc.psum_tensor(f"ln_s1_{t}", [1, 512], F32))
        self.ps_s2 = st.enter_context(nc.psum_tensor(f"ln_s2_{t}", [1, 512], F32))
        self.ps_mb = st.enter_context(nc.psum_tensor(f"ln_mb_{t}", [128, 512], F32))
        self.ps_rb = st.enter_context(nc.psum_tensor(f"ln_rb_{t}", [128, 512], F32))
        self.s_ld = ctx.sem("dma_sync")
        self.s_cst = ctx.sem("dma_gp")
        self.s_sq = ctx.sem("v1")
        self.s_stats = ctx.sem("t1")
        self.s_sm = ctx.sem("v2")
        self.s_sm2 = ctx.sem("sc1")
        self.s_bc = ctx.sem("t2")
        self.s_xn = ctx.sem("v3")
        self.s_st = ctx.sem("dma_out")
        n = self.NCH
        self._totals = {"dma_sync": 32 * n, "dma_gp": 32 * n, "v1": n, "t1": 2 * n,
                        "v2": 2 * n, "sc1": n, "t2": 2 * n, "v3": 16 * n,
                        "dma_out": 16 * n}

    def sync(self, e):
        for i in range(self.NCH):
            if i > 0:
                self.w(e, self.s_xn, 16 * i)
            src = self.x.rearrange("(kc p) t -> p kc t", p=128)
            for half in range(2):
                e.dma_start(
                    out=self.xf[:, :].rearrange("p (kc t) -> p kc t", kc=16)[:, 8*half:8*half+8, :],
                    in_=src[:, 8*half:8*half+8, 512*i:512*(i+1)],
                ).then_inc(self.s_ld.h, 16)
            if i > 0:
                self._store(e, i - 1)
        self.w(e, self.s_xn, 16 * self.NCH)
        self._store(e, self.NCH - 1)
        self.w(e, self.s_st, 16 * self.NCH)

    def _store(self, e, i):
        dst = self.xn.rearrange("(kc p) t -> p kc t", p=128)
        e.dma_start(
            out=dst[:, :, 512*i:512*(i+1)],
            in_=self.xnb[:, :].rearrange("p (kc t) -> p kc t", kc=16),
        ).then_inc(self.s_st.h, 16)

    def gpsimd(self, e):
        for i in range(self.NCH):
            if i > 0:
                self.w(e, self.s_stats, 2 * i)
            src = self.x.rearrange("(kc p) t -> p kc t", p=128)
            for half in range(2):
                e.dma_start(
                    out=self.xb[:, :].rearrange("p (kc t) -> p kc t", kc=16)[:, 8*half:8*half+8, :],
                    in_=src[:, 8*half:8*half+8, 512*i:512*(i+1)],
                ).then_inc(self.s_cst.h, 16)

    def vector(self, e):
        for i in range(self.NCH):
            self.w(e, self.s_cst, 32 * (i + 1))
            if i > 0:
                self.w(e, self.s_stats, 2 * i)
            e.tensor_mul(self.x2b[:, :], self.xb[:, :], self.xb[:, :]).then_inc(self.s_sq.h, 1)
            self.w(e, self.s_stats, 2 * (i + 1))
            negm = self.sm[:, 0:512]; e2 = self.sm[:, 512:1024]
            var = self.sm[:, 1024:1536]; rstd = self.sm[:, 1536:2048]
            e.tensor_scalar_mul(negm, self.ps_s1[:, :], -1.0 / 2048.0)
            e.tensor_scalar_mul(e2, self.ps_s2[:, :], 1.0 / 2048.0)
            t = self.scr[0:1, 0:512]
            e.tensor_mul(t, negm, negm)
            e.tensor_sub(var, e2, t)
            e.tensor_scalar_add(var, var, 1e-5).then_inc(self.s_sm.h, 1)
            self.w(e, self.s_sm2, i + 1)
            e.reciprocal(rstd, var).then_inc(self.s_sm.h, 1)
            self.w(e, self.s_bc, 2 * (i + 1))
            self.w(e, self.s_st, 16 * i)
            for kc in range(16):
                sl = slice(512 * kc, 512 * (kc + 1))
                e.tensor_add(self.scr[:, :], self.xf[:, sl], self.ps_mb[:, :])
                e.tensor_mul(self.xnb[:, sl], self.scr[:, :], self.ps_rb[:, :]).then_inc(self.s_xn.h, 1)

    def scalar(self, e):
        for i in range(self.NCH):
            self.w(e, self.s_sm, 2 * i + 1)
            var = self.sm[:, 1024:1536]
            e.activation(out=var, in_=var, func=AF.Sqrt).then_inc(self.s_sm2.h, 1)

    def tensor(self, e):
        for i in range(self.NCH):
            self.w(e, self.s_cst, 32 * (i + 1))
            if i > 0:
                self.w(e, self.s_sm, 2 * i - 1)
            for kc in range(16):
                sl = slice(512 * kc, 512 * (kc + 1))
                e.matmul(self.ps_s1[:, :], self.c.ones_bf[:, :], self.xb[:, sl],
                         start=(kc == 0), stop=(kc == 15))
            self.w(e, self.s_sq, i + 1)
            last = None
            for kc in range(16):
                sl = slice(512 * kc, 512 * (kc + 1))
                last = e.matmul(self.ps_s2[:, :], self.c.ones_bf[:, :], self.x2b[:, sl],
                                start=(kc == 0), stop=(kc == 15))
            last.then_inc(self.s_stats.h, 2)
            self.w(e, self.s_sm, 2 * (i + 1))
            if i > 0:
                self.w(e, self.s_xn, 16 * i)
            negm = self.sm[:, 0:512]; rstd = self.sm[:, 1536:2048]
            e.matmul(self.ps_mb[:, :], self.c.ones1_f[:, :], negm, start=True, stop=True)
            e.matmul(self.ps_rb[:, :], self.c.ones1_f[:, :], rstd, start=True, stop=True).then_inc(self.s_bc.h, 2)


class Plan(PhaseBase):
    def __init__(self, ctx, name):
        self.ctx = ctx
        self.name = name
        self.steps = []
        self.cnt = {}
        self._totals = self.cnt

    def sem(self, name):
        return self.ctx.sem(name)

    def step(self, engine, fn, waits=(), inc=None, manual=()):
        self.steps.append((engine, fn, tuple(waits), inc))
        if inc is not None:
            s, k = inc
            self.cnt[s.name] = self.cnt.get(s.name, 0) + k
        for (s, k) in manual:
            self.cnt[s.name] = self.cnt.get(s.name, 0) + k

    def rel(self, sem):
        return self.cnt.get(sem.name, 0)

    def _body(self, eng_name):
        def run(e):
            for (en, fn, waits, inc) in self.steps:
                if en != eng_name:
                    continue
                for (s, rv) in waits:
                    if rv > 0:
                        e.wait_ge(s.h, self.b.get(s.name, 0) + rv)
                ins = fn(e)
                if inc is not None:
                    s, k = inc
                    ins.then_inc(s.h, k)
        return run

    def __getattr__(self, name):
        if name in ENGS:
            return self._body(name)
        raise AttributeError(name)


_qs_ctr = [0]
def make_qscale_phase(ctx, consts, logpos_ap, scaler_ap, qscale_sb):
    nc, st = ctx.nc, ctx.stack
    _qs_ctr[0] += 1
    u = _qs_ctr[0]
    p = Plan(ctx, f"qscale{u}")
    lp = st.enter_context(nc.sbuf_tensor(f"qs_lp{u}", [1, 2048], F32))
    sc = st.enter_context(nc.sbuf_tensor(f"qs_sc{u}", [1, 16], F32))
    ps = st.enter_context(nc.psum_tensor(f"qs_ps{u}", [128, 256], F32))
    sd = p.sem("dma_sync"); st_ = p.sem("t1"); sv = p.sem("v1")
    p.step("sync", lambda e: e.dma_start(out=lp[:, :], in_=logpos_ap), inc=(sd, 16))
    p.step("sync", lambda e: e.dma_start(out=sc[:, :], in_=scaler_ap), inc=(sd, 16))
    def mms(e):
        last = None
        for tt in range(16):
            last = e.matmul(ps[:, 16*tt:16*(tt+1)], lp[:, 128*tt:128*(tt+1)], sc[:, :],
                            start=True, stop=True)
        return last
    p.step("tensor", mms, waits=[(sd, 32)], inc=(st_, 1))
    p.step("vector", lambda e: e.tensor_copy(qscale_sb[:, :], ps[:, :]),
           waits=[(st_, p.rel(st_))], inc=(sv, 1))
    return p


def make_lambda_phase(ctx, consts, lam_aps, neglam_sb):
    nc, st = ctx.nc, ctx.stack
    p = Plan(ctx, "lambda")
    lv = st.enter_context(nc.sbuf_tensor("lam_v", [1, 256], F32))
    scr = st.enter_context(nc.sbuf_tensor("lam_scr", [1, 128], F32))
    red = st.enter_context(nc.sbuf_tensor("lam_red", [1, 4], F32))
    p.red_dump = red
    ps = st.enter_context(nc.psum_tensor("lam_ps", [128, 1], F32))
    sd = p.sem("dma_sync"); sv = p.sem("v1"); ss = p.sem("sc1"); st_ = p.sem("t1")
    for i, ap in enumerate(lam_aps):
        p.step("sync", lambda e, ap=ap, i=i: e.dma_start(out=lv[:, 64*i:64*(i+1)], in_=ap), inc=(sd, 16))
    def v1(e):
        e.tensor_mul(scr[:, 0:64], lv[:, 0:64], lv[:, 64:128])
        e.tensor_mul(scr[:, 64:128], lv[:, 128:192], lv[:, 192:256])
        e.reduce_sum(red[:, 0:1], scr[:, 0:64], AX.X)
        return e.reduce_sum(red[:, 1:2], scr[:, 64:128], AX.X)
    p.step("vector", v1, waits=[(sd, 64)], inc=(sv, 1))
    p.step("scalar", lambda e: e.activation(out=red[:, 2:4], in_=red[:, 0:2], func=AF.Exp),
           waits=[(sv, 1)], inc=(ss, 1))
    def v2(e):
        return e.scalar_tensor_tensor(red[:, 0:1], red[:, 3:4], -0.2, red[:, 2:3],
                                      ALU.add, ALU.subtract)
    p.step("vector", v2, waits=[(ss, 1)], inc=(sv, 1))
    p.step("tensor", lambda e: e.matmul(ps[:, :], consts.ones1_f[:, :], red[:, 0:1],
                                        start=True, stop=True),
           waits=[(sv, 2)], inc=(st_, 1))
    p.step("vector", lambda e: e.tensor_copy(neglam_sb[:, :], ps[:, :]),
           waits=[(st_, 1)], inc=(sv, 1))
    return p


def make_qkv_phase(ctx, consts, tag, xn_dram, jobs, cos_sb, sin_sb, qscale_sb):
    nc, st = ctx.nc, ctx.stack
    p = Plan(ctx, f"qkv_{tag}")
    xn_sb = st.enter_context(nc.sbuf_tensor(f"qkv_xn_{tag}", [128, 16 * 2048], BF16))
    w_sb = st.enter_context(nc.sbuf_tensor(f"qkv_w_{tag}", [128, 16 * 2048], BF16))
    qf = [st.enter_context(nc.sbuf_tensor(f"qkv_qf{i}_{tag}", [128, 2048], F32)) for i in range(2)]
    q2f = st.enter_context(nc.sbuf_tensor(f"qkv_q2f_{tag}", [128, 2048], F32))
    rot = st.enter_context(nc.sbuf_tensor(f"qkv_rot_{tag}", [128, 2048], F32))
    t1 = st.enter_context(nc.sbuf_tensor(f"qkv_t1_{tag}", [128, 1024], F32))
    t2 = st.enter_context(nc.sbuf_tensor(f"qkv_t2_{tag}", [128, 1024], F32))
    ssq = st.enter_context(nc.sbuf_tensor(f"qkv_ssq_{tag}", [128, 64], F32))
    comb = st.enter_context(nc.sbuf_tensor(f"qkv_comb_{tag}", [128, 16], F32))
    ob = [st.enter_context(nc.sbuf_tensor(f"qkv_ob{i}_{tag}", [128, 2048], BF16)) for i in range(2)]
    ps = [st.enter_context(nc.psum_tensor(f"qkv_ps{i}_{tag}", [128, 2048], F32)) for i in range(2)]
    sd = p.sem("dma_sync"); so = p.sem("dma_out")
    smm = p.sem("t1"); sev = p.sem("sc1"); sssq = p.sem("v1"); srstd = p.sem("sc2")
    spp = p.sem("v2")

    xn_src = xn_dram.rearrange("(kc p) t -> p kc t", p=128)
    xn_dst = xn_sb[:, :].rearrange("p (kc t) -> p kc t", kc=16)
    for q in range(4):
        p.step("sync", lambda e, q=q: e.dma_start(out=xn_dst[:, 4*q:4*q+4, :],
                                                  in_=xn_src[:, 4*q:4*q+4, :]), inc=(sd, 16))

    for js, job in enumerate(jobs):
        mode = job["mode"]
        w_src = job["w"].rearrange("(kc p) n -> p kc n", p=128)
        w_dst = w_sb[:, :].rearrange("p (kc n) -> p kc n", kc=16)
        wld_target = None
        for q in range(4):
            waits = [(smm, p.rel(smm))] if (js > 0 and q == 0) else ()
            p.step("sync", lambda e, q=q, w_src=w_src: e.dma_start(out=w_dst[:, 4*q:4*q+4, :],
                                                      in_=w_src[:, 4*q:4*q+4, :]),
                   waits=waits, inc=(sd, 16))
        wld_target = p.rel(sd)

        for tt in range(16):
            pb = ps[tt % 2]
            def mms(e, tt=tt, pb=pb):
                last = None
                for kc in range(16):
                    lhs = xn_sb[:, 2048*kc + 128*tt: 2048*kc + 128*(tt+1)]
                    for nch in range(4):
                        last = e.matmul(pb[:, 512*nch:512*(nch+1)], lhs,
                                        w_sb[:, 2048*kc + 512*nch: 2048*kc + 512*(nch+1)],
                                        start=(kc == 0), stop=(kc == 15))
                return last
            waits = [(sd, wld_target)]
            if p.rel(smm) >= 2:
                waits.append((sev, p.rel(smm) - 1))
            p.step("tensor", mms, waits=waits, inc=(smm, 1))

            if mode == "v":
                def ev(e, tt=tt, pb=pb):
                    return e.copy(ob[tt % 2][:, :], pb[:, :])
                waits = [(smm, p.rel(smm))]
                if p.rel(so) >= 32:
                    waits.append((so, p.rel(so) - 16))
                p.step("scalar", ev, waits=waits, inc=(sev, 1))
                p.step("sync", lambda e, tt=tt, job=job: e.dma_start(
                    out=job["out"][128*tt:128*(tt+1), :], in_=ob[tt % 2][:, :]),
                    waits=[(sev, p.rel(sev))], inc=(so, 16))
                continue

            def ev(e, tt=tt, pb=pb):
                return e.copy(qf[tt % 2][:, :], pb[:, :])
            waits = [(smm, p.rel(smm))]
            if p.rel(spp) >= 2:
                waits.append((spp, p.rel(spp) - 1))
            p.step("scalar", ev, waits=waits, inc=(sev, 1))

            par = tt % 2
            qfb = qf[par]
            do_rms = mode in ("q0", "k0")
            if tt == 0 and "dbg_qf" in job:
                p.step("sync", lambda e, qfb=qfb, job=job: e.dma_start(out=job["dbg_qf"], in_=qfb[:, :]),
                       waits=[(sev, p.rel(sev))], inc=(so, 16))
            if do_rms:
                def vssq(e, qfb=qfb, par=par):
                    e.tensor_mul(q2f[:, :], qfb[:, :], qfb[:, :])
                    return e.tensor_reduce(ssq[:, 32*par:32*par+16],
                                           q2f[:, :].rearrange("p (h d) -> p h d", h=16),
                                           AX.X, ALU.add)
                p.step("vector", vssq, waits=[(sev, p.rel(sev))], inc=(sssq, 1))
                def srs(e, par=par):
                    e.activation(out=ssq[:, 32*par+16:32*par+32], in_=ssq[:, 32*par:32*par+16],
                                 func=AF.Ln, scale=1.0/128.0, bias=consts.eps_rms[:, :])
                    return e.activation(out=ssq[:, 32*par+16:32*par+32],
                                        in_=ssq[:, 32*par+16:32*par+32],
                                        func=AF.Exp, scale=-0.5)
                p.step("scalar", srs, waits=[(sssq, p.rel(sssq))], inc=(srstd, 1))

            def vrot(e, qfb=qfb, mode=mode, tt=tt):
                cosb = cos_sb[:, 64*tt:64*(tt+1)].rearrange("p (g d) -> p g d", g=1).broadcast_to([128, 16, 64])
                sinb = sin_sb[:, 64*tt:64*(tt+1)].rearrange("p (g d) -> p g d", g=1).broadcast_to([128, 16, 64])
                lo = qfb[:, :].rearrange("p (h d) -> p h d", h=16)[:, :, 0:64]
                hi = qfb[:, :].rearrange("p (h d) -> p h d", h=16)[:, :, 64:128]
                rlo = rot[:, :].rearrange("p (h d) -> p h d", h=16)[:, :, 0:64]
                rhi = rot[:, :].rearrange("p (h d) -> p h d", h=16)[:, :, 64:128]
                tv1 = t1[:, :].rearrange("p (h d) -> p h d", h=16)
                tv2 = t2[:, :].rearrange("p (h d) -> p h d", h=16)
                e.tensor_tensor(tv1, lo, cosb, ALU.mult)
                e.tensor_tensor(tv2, hi, sinb, ALU.mult)
                if mode in ("q0", "k0"):
                    e.tensor_add(rlo, tv1, tv2)
                else:
                    e.tensor_sub(rlo, tv1, tv2)
                e.tensor_tensor(tv1, lo, sinb, ALU.mult)
                e.tensor_tensor(tv2, hi, cosb, ALU.mult)
                if mode in ("q0", "k0"):
                    return e.tensor_sub(rhi, tv2, tv1)
                else:
                    return e.tensor_add(rhi, tv1, tv2)
            p.step("vector", vrot, waits=[(sev, p.rel(sev))], inc=(sssq, 1) if not do_rms else None)

            waits = []
            if do_rms:
                waits.append((srstd, p.rel(srstd)))
            if p.rel(so) >= 32:
                waits.append((so, p.rel(so) - 16))
            def vapply(e, mode=mode, par=par, tt=tt):
                last = None
                obb = ob[par]
                rstd_t = ssq[:, 32*par+16:32*par+32]
                if mode == "q0":
                    for h in range(16):
                        last = e.tensor_scalar(obb[:, 128*h:128*(h+1)],
                                               rot[:, 128*h:128*(h+1)],
                                               rstd_t[:, h:h+1],
                                               qscale_sb[:, 16*tt+h:16*tt+h+1],
                                               ALU.mult, ALU.mult)
                elif mode == "k0":
                    for h in range(16):
                        last = e.tensor_scalar_mul(obb[:, 128*h:128*(h+1)],
                                                   rot[:, 128*h:128*(h+1)],
                                                   rstd_t[:, h:h+1])
                else:
                    last = e.tensor_scalar_mul(obb[:, :], rot[:, :], 0.08838834764831845)
                return last
            p.step("vector", vapply, waits=waits, inc=(spp, 1))
            if tt == 0 and "dbg_rot" in job:
                p.step("sync", lambda e, job=job: e.dma_start(out=job["dbg_rot"], in_=rot[:, :]),
                       waits=[(spp, p.rel(spp))], inc=(so, 16))
            p.step("sync", lambda e, tt=tt, par=par, job=job: e.dma_start(
                out=job["out"][128*tt:128*(tt+1), :], in_=ob[par][:, :]),
                waits=[(spp, p.rel(spp))], inc=(so, 16))

    p.step("sync", lambda e: e.nop(), waits=[(so, p.rel(so))])
    return p




def ktiles_uniform(j, W=384):
    out = []
    base = 256 * j
    n = base // W
    for i in range(n):
        out.append((W * i, W, False))
    rem = base % W
    if rem:
        out.append((n * W, rem, False))
    out.append((base, 256, True))
    return out


def _k_rhs(kbuf_pair, k0, kw):
    out = []
    for c in range(kw // 128):
        gc = k0 // 128 + c
        blk = gc % 2
        off = (gc // 2) * 128
        out.append((kbuf_pair[blk][:, off:off + 128], 128 * c))
    return out


def make_attn_l1(ctx, consts, tag, q_tm, k_tm, v_tm, amask_ap, o_fm_sb):
    if not isinstance(k_tm, (list, tuple)):
        k_tm = [k_tm[0:2048, :], k_tm[2048:4096, :]]
        v_tm = [v_tm[0:2048, :], v_tm[2048:4096, :]]
    nc, st = ctx.nc, ctx.stack
    p = Plan(ctx, f"attn1_{tag}")
    kb = [[st.enter_context(nc.sbuf_tensor(f"at1_k{i}{blk}_{tag}", [128, 2048], BF16))
           for blk in range(2)] for i in range(2)]
    vb = [[st.enter_context(nc.sbuf_tensor(f"at1_v{i}{blk}_{tag}", [128, 16 * 129], BF16))
           for blk in range(2)] for i in range(2)]
    qbuf = [st.enter_context(nc.sbuf_tensor(f"at1_q{i}_{tag}", [128, 2048], BF16)) for i in range(2)]
    amask = st.enter_context(nc.sbuf_tensor(f"at1_am_{tag}", [128, 256], F32))
    p_sb = [st.enter_context(nc.sbuf_tensor(f"at1_p{i}_{tag}", [128, 512], BF16)) for i in range(2)]
    pt_sb = [st.enter_context(nc.sbuf_tensor(f"at1_pt{i}_{tag}", [128, 512], BF16)) for i in range(2)]
    m_sb = st.enter_context(nc.sbuf_tensor(f"at1_m_{tag}", [128, 3], F32))
    o_acc = [st.enter_context(nc.sbuf_tensor(f"at1_oacc{i}_{tag}", [128, 132], F32)) for i in range(2)]
    corr = st.enter_context(nc.sbuf_tensor(f"at1_corr_{tag}", [128, 2], F32))
    rcp = [st.enter_context(nc.sbuf_tensor(f"at1_rcp{i}_{tag}", [128, 1], F32)) for i in range(2)]
    o_bf = [st.enter_context(nc.sbuf_tensor(f"at1_obf{i}_{tag}", [128, 128], BF16)) for i in range(2)]
    ps_s = [st.enter_context(nc.psum_tensor(f"at1_s{i}_{tag}", [128, 512], F32)) for i in range(2)]
    ps_pt = [st.enter_context(nc.psum_tensor(f"at1_ppt{i}_{tag}", [128, 512], BF16)) for i in range(2)]
    ps_o = [st.enter_context(nc.psum_tensor(f"at1_o{i}_{tag}", [128, 132], F32)) for i in range(2)]
    ps_y = st.enter_context(nc.psum_tensor(f"at1_y_{tag}", [128, 128], BF16))

    sd = p.sem("dma_sync"); gp1 = p.sem("gp1")
    t_s = p.sem("t1"); v_st = p.sem("v1"); sc_p = p.sem("sc1")
    t_tr = p.sem("t2"); v_pt = p.sem("v2"); t_o = p.sem("t3"); v_up = p.sem("v3")
    v_fin = p.sem("v4"); t_y = p.sem("t4"); v_ev = p.sem("v5"); sc_fin = p.sem("sc2")

    def ms(e):
        for i in range(2):
            for blk in range(2):
                e.memset(vb[i][blk][:, :], 1.0)
        return e.dma_start(out=amask[:, :], in_=amask_ap)
    p.step("gpsimd", ms, inc=(gp1, 16))

    kt_end = []
    gk = 0
    fin = 0
    pend = []

    def front(b, j, k0, kw, msk, kl, g, load_tgt):
        par = g % 2
        cur = g % 3; prev = (g - 1) % 3
        waits = [(sd, load_tgt)]
        if g >= 2:
            waits.append((sc_p, g - 1))
        def tsm(e, b=b, j=j, k0=k0, kw=kw, par=par):
            last = None
            for (ap, off) in _k_rhs(kb[b], k0, kw):
                last = e.matmul(ps_s[par][:, off:off+128],
                                qbuf[b][:, 128*j:128*(j+1)], ap, start=True, stop=True)
            return last
        p.step("tensor", tsm, waits=waits, inc=(t_s, 1))
        def vstats(e, par=par, kw=kw, msk=msk, kl=kl, cur=cur, prev=prev):
            if msk:
                e.tensor_add(ps_s[par][:, 0:256], ps_s[par][:, 0:256], amask[:, :])
            if kl == 0:
                return e.reduce_max(m_sb[:, cur:cur+1], ps_s[par][:, 0:kw], AX.X, negate=True)
            e.tensor_scalar_mul(ps_s[par][:, kw:kw+1], m_sb[:, prev:prev+1], -1.0)
            return e.reduce_max(m_sb[:, cur:cur+1], ps_s[par][:, 0:kw+1], AX.X, negate=True)
        p.step("vector", vstats, waits=[(t_s, g + 1)], inc=(v_st, 1))
        def sp(e, par=par, kw=kw, kl=kl, cur=cur, prev=prev, g=g):
            if kl > 0:
                e.activation(out=corr[:, g % 2:g % 2 + 1], in_=m_sb[:, prev:prev+1],
                             func=AF.Exp, scale=-1.0, bias=m_sb[:, cur:cur+1])
            return e.activation(out=p_sb[par][:, 0:kw], in_=ps_s[par][:, 0:kw],
                                func=AF.Exp, bias=m_sb[:, cur:cur+1])
        waits = [(v_st, g + 1)]
        if g >= 2:
            waits.append((t_tr, g - 1))
        p.step("scalar", sp, waits=waits, inc=(sc_p, 1))

    def back(b, k0, kw, kl, g, jp):
        par = g % 2
        def ttr(e, kw=kw, par=par):
            last = None
            for c in range(kw // 128):
                last = e.transpose(ps_pt[par][:, 128*c:128*(c+1)], p_sb[par][:, 128*c:128*(c+1)],
                                   consts.ident[:, :])
            return last
        waits = [(sc_p, g + 1)]
        if g >= 2:
            waits.append((v_pt, g - 1))
        p.step("tensor", ttr, waits=waits, inc=(t_tr, 1))
        waits = [(t_tr, g + 1)]
        if g >= 2:
            waits.append((t_o, g - 1))
        p.step("vector", lambda e, kw=kw, par=par: e.tensor_copy(pt_sb[par][:, 0:kw], ps_pt[par][:, 0:kw]),
               waits=waits, inc=(v_pt, 1))
        def tom(e, k0=k0, kw=kw, par=par, b=b):
            last = None
            nc_ = kw // 128
            for c in range(nc_):
                gc = k0 // 128 + c
                blk = gc % 2
                off = 129 * (gc // 2)
                last = e.matmul(ps_o[par][:, 0:129], pt_sb[par][:, 128*c:128*(c+1)],
                                vb[b][blk][:, off:off+129],
                                start=(c == 0), stop=(c == nc_ - 1))
            return last
        waits = [(v_pt, g + 1)]
        if g >= 2:
            waits.append((v_up, g - 1))
        p.step("tensor", tom, waits=waits, inc=(t_o, 1))
        def vup(e, par=par, kl=kl, g=g, jp=jp):
            if kl == 0:
                return e.tensor_scalar(o_acc[jp][:, 0:129], ps_o[par][:, 0:129], 1.0, None, ALU.mult)
            return e.scalar_tensor_tensor(o_acc[jp][:, 0:129], o_acc[jp][:, 0:129],
                                          corr[:, g % 2:g % 2 + 1], ps_o[par][:, 0:129],
                                          ALU.mult, ALU.add)
        p.step("vector", vup, waits=[(t_o, g + 1)], inc=(v_up, 1))

    def fin_emit(h, j, gk_j, fin_j):
        jp = j % 2
        def sfin(e, jp=jp):
            e.activation(out=rcp[jp][:, :], in_=o_acc[jp][:, 128:129], func=AF.Ln)
            return e.activation(out=rcp[jp][:, :], in_=rcp[jp][:, :], func=AF.Exp, scale=-1.0)
        p.step("scalar", sfin, waits=[(v_up, gk_j)], inc=(sc_fin, 1))
        def vfin(e, jp=jp):
            return e.tensor_tensor(o_bf[jp][:, :], o_acc[jp][:, 0:128],
                                   rcp[jp][:, 0:1].broadcast_to([128, 128]), ALU.mult)
        p.step("vector", vfin, waits=[(sc_fin, fin_j + 1)], inc=(v_fin, 1))
        waits = [(v_fin, fin_j + 1)]
        if fin_j >= 1:
            waits.append((v_ev, fin_j))
        p.step("tensor", lambda e, jp=jp: e.transpose(ps_y[:, :], o_bf[jp][:, :], consts.ident[:, :]),
               waits=waits, inc=(t_y, 1))
        p.step("vector", lambda e, h=h, j=j: e.tensor_copy(
            o_fm_sb[:, 2048*h + 128*j: 2048*h + 128*(j+1)], ps_y[:, :]),
            waits=[(t_y, fin_j + 1)], inc=(v_ev, 1))

    for h in range(16):
        b = h % 2
        waits = [(gp1, 16)] if h < 2 else [(t_o, kt_end[h - 2])]
        p.step("sync", lambda e, b=b, h=h: e.dma_start(
            out=qbuf[b][:, :], in_=q_tm[:, 128*h:128*(h+1)], transpose=True),
            waits=waits, inc=(sd, 16))
        for blk in range(2):
            p.step("sync", lambda e, b=b, h=h, blk=blk: e.dma_start(
                out=kb[b][blk][:, :], in_=k_tm[blk][:, 128*h:128*(h+1)],
                transpose=True), inc=(sd, 16))
            vv = vb[b][blk][:, :].rearrange("p (kc n) -> p kc n", n=129)
            vsrc = v_tm[blk].rearrange("(kc p) n -> p kc n", p=128)
            p.step("sync", lambda e, b=b, h=h, blk=blk, vv=vv, vsrc=vsrc: e.dma_start(
                out=vv[:, :, 0:128], in_=vsrc[:, :, 128*h:128*(h+1)]),
                inc=(sd, 16))
        load_tgt = p.rel(sd)

        for j in range(16):
            kts = ktiles_uniform(j)
            for kl, (k0, kw, msk) in enumerate(kts):
                front(b, j, k0, kw, msk, kl, gk, load_tgt)
                for fn in pend:
                    fn()
                pend = [lambda b=b, k0=k0, kw=kw, kl=kl, g=gk, j=j: back(b, k0, kw, kl, g, j % 2)]
                gk += 1
            pend.append(lambda h=h, j=j, gk_j=gk, fin_j=fin: fin_emit(h, j, gk_j, fin_j))
            fin += 1
        kt_end.append(gk)
    for fn in pend:
        fn()
    p.step("sync", lambda e: e.nop(), waits=[(v_ev, fin)])
    return p


def make_attn_l0(ctx, consts, tag, q_tm, k_tm, v_tm, amask_ap, neglam_sb, s_fm_out):
    if not isinstance(k_tm, (list, tuple)):
        k_tm = [k_tm[0:2048, :], k_tm[2048:4096, :]]
        v_tm = [v_tm[0:2048, :], v_tm[2048:4096, :]]
    nc, st = ctx.nc, ctx.stack
    p = Plan(ctx, f"attn0_{tag}")
    ka = [[st.enter_context(nc.sbuf_tensor(f"at0_ka{i}{blk}_{tag}", [128, 2048], BF16))
           for blk in range(2)] for i in range(2)]
    kbb = [[st.enter_context(nc.sbuf_tensor(f"at0_kb{i}{blk}_{tag}", [128, 2048], BF16))
            for blk in range(2)] for i in range(2)]
    qa = [st.enter_context(nc.sbuf_tensor(f"at0_qa{i}_{tag}", [128, 2048], BF16)) for i in range(2)]
    qb = [st.enter_context(nc.sbuf_tensor(f"at0_qb{i}_{tag}", [128, 2048], BF16)) for i in range(2)]
    vbf = [[st.enter_context(nc.sbuf_tensor(f"at0_v{i}{blk}_{tag}", [128, 16 * 257], BF16))
            for blk in range(2)] for i in range(2)]
    amask = st.enter_context(nc.sbuf_tensor(f"at0_am_{tag}", [128, 256], F32))
    p_sb = [st.enter_context(nc.sbuf_tensor(f"at0_p{i}_{tag}", [128, 1024], BF16)) for i in range(2)]
    pt_sb = [st.enter_context(nc.sbuf_tensor(f"at0_pt{i}_{tag}", [128, 1024], BF16)) for i in range(2)]
    m_a = st.enter_context(nc.sbuf_tensor(f"at0_ma_{tag}", [128, 3], F32))
    m_b = st.enter_context(nc.sbuf_tensor(f"at0_mb_{tag}", [128, 3], F32))
    corr_a = st.enter_context(nc.sbuf_tensor(f"at0_ca_{tag}", [128, 2], F32))
    corr_b = st.enter_context(nc.sbuf_tensor(f"at0_cb_{tag}", [128, 2], F32))
    oa = [st.enter_context(nc.sbuf_tensor(f"at0_oa{i}_{tag}", [128, 260], F32)) for i in range(2)]
    ob_ = [st.enter_context(nc.sbuf_tensor(f"at0_ob{i}_{tag}", [128, 260], F32)) for i in range(2)]
    rcp_a = [st.enter_context(nc.sbuf_tensor(f"at0_ra{i}_{tag}", [128, 1], F32)) for i in range(2)]
    rcp_b = [st.enter_context(nc.sbuf_tensor(f"at0_rb{i}_{tag}", [128, 1], F32)) for i in range(2)]
    o2n = [st.enter_context(nc.sbuf_tensor(f"at0_o2n{i}_{tag}", [128, 256], F32)) for i in range(2)]
    y_tm = [st.enter_context(nc.sbuf_tensor(f"at0_ytm{i}_{tag}", [128, 256], F32)) for i in range(2)]
    y_stage = st.enter_context(nc.sbuf_tensor(f"at0_yst_{tag}", [128, 2 * 2048], F32))
    ps_sa = [st.enter_context(nc.psum_tensor(f"at0_psa{i}_{tag}", [128, 512], F32)) for i in range(2)]
    ps_sb_ = [st.enter_context(nc.psum_tensor(f"at0_psb{i}_{tag}", [128, 512], F32)) for i in range(2)]
    ps_pt = st.enter_context(nc.psum_tensor(f"at0_ppt_{tag}", [128, 1024], BF16))
    ps_oa = st.enter_context(nc.psum_tensor(f"at0_poa_{tag}", [128, 260], F32))
    ps_ob = st.enter_context(nc.psum_tensor(f"at0_pob_{tag}", [128, 260], F32))
    ps_y = st.enter_context(nc.psum_tensor(f"at0_py_{tag}", [128, 256], F32))

    sd = p.sem("dma_sync"); gp1 = p.sem("gp1"); so = p.sem("dma_out")
    t_s = p.sem("t1"); v_st = p.sem("v1"); sc_p = p.sem("sc1")
    t_tr = p.sem("t2"); v_pt = p.sem("v2"); t_o = p.sem("t3"); v_up = p.sem("v3")
    v_fin = p.sem("v4"); t_y = p.sem("t4"); v_ev = p.sem("v5"); sc_fin = p.sem("sc2")

    def ms(e):
        for i in range(2):
            for blk in range(2):
                e.memset(vbf[i][blk][:, :], 1.0)
        return e.dma_start(out=amask[:, :], in_=amask_ap)
    p.step("gpsimd", ms, inc=(gp1, 16))

    kt_end = []
    gk = 0
    fin = 0
    pend = []

    def front(bpar, j, k0, kw, msk, kl, g, load_tgt):
        par = g % 2
        cur = g % 3; prev = (g - 1) % 3
        waits = [(sd, load_tgt)]
        if g >= 2:
            waits.append((sc_p, g - 1))
        def tsm(e, bpar=bpar, j=j, k0=k0, kw=kw, par=par):
            last = None
            for (ap, off) in _k_rhs(ka[bpar], k0, kw):
                last = e.matmul(ps_sa[par][:, off:off+128],
                                qa[bpar][:, 128*j:128*(j+1)], ap, start=True, stop=True)
            for (ap, off) in _k_rhs(kbb[bpar], k0, kw):
                last = e.matmul(ps_sb_[par][:, off:off+128],
                                qb[bpar][:, 128*j:128*(j+1)], ap, start=True, stop=True)
            return last
        p.step("tensor", tsm, waits=waits, inc=(t_s, 1))
        def vstats(e, par=par, kw=kw, msk=msk, kl=kl, cur=cur, prev=prev):
            last = None
            for ps, m_ in ((ps_sa[par], m_a), (ps_sb_[par], m_b)):
                if msk:
                    e.tensor_add(ps[:, 0:256], ps[:, 0:256], amask[:, :])
                if kl == 0:
                    last = e.reduce_max(m_[:, cur:cur+1], ps[:, 0:kw], AX.X, negate=True)
                else:
                    e.tensor_scalar_mul(ps[:, kw:kw+1], m_[:, prev:prev+1], -1.0)
                    last = e.reduce_max(m_[:, cur:cur+1], ps[:, 0:kw+1], AX.X, negate=True)
            return last
        p.step("vector", vstats, waits=[(t_s, g + 1)], inc=(v_st, 1))
        def sp(e, par=par, kw=kw, kl=kl, cur=cur, prev=prev, g=g):
            last = None
            for ps, m_, co, off in ((ps_sa[par], m_a, corr_a, 0), (ps_sb_[par], m_b, corr_b, 512)):
                if kl > 0:
                    e.activation(out=co[:, g % 2:g % 2 + 1], in_=m_[:, prev:prev+1],
                                 func=AF.Exp, scale=-1.0, bias=m_[:, cur:cur+1])
                last = e.activation(out=p_sb[par][:, off:off+kw], in_=ps[:, 0:kw],
                                    func=AF.Exp, bias=m_[:, cur:cur+1])
            return last
        waits = [(v_st, g + 1)]
        if g >= 2:
            waits.append((t_tr, g - 1))
        p.step("scalar", sp, waits=waits, inc=(sc_p, 1))

    def back(bpar, k0, kw, kl, g, jp):
        par = g % 2
        def ttr(e, kw=kw, par=par):
            last = None
            for off in (0, 512):
                for c in range(kw // 128):
                    last = e.transpose(ps_pt[:, off + 128*c: off + 128*(c+1)],
                                       p_sb[par][:, off + 128*c: off + 128*(c+1)],
                                       consts.ident[:, :])
            return last
        waits = [(sc_p, g + 1)]
        if g >= 1:
            waits.append((v_pt, g))
        p.step("tensor", ttr, waits=waits, inc=(t_tr, 1))
        def vpt(e, kw=kw, par=par):
            e.tensor_copy(pt_sb[par][:, 0:kw], ps_pt[:, 0:kw])
            return e.tensor_copy(pt_sb[par][:, 512:512+kw], ps_pt[:, 512:512+kw])
        waits = [(t_tr, g + 1)]
        if g >= 2:
            waits.append((t_o, g - 1))
        p.step("vector", vpt, waits=waits, inc=(v_pt, 1))
        def tom(e, k0=k0, kw=kw, bpar=bpar, par=par):
            last = None
            nc_ = kw // 128
            for ps_o_, off in ((ps_oa, 0), (ps_ob, 512)):
                for c in range(nc_):
                    gc = k0 // 128 + c
                    blk = gc % 2
                    voff = 257 * (gc // 2)
                    last = e.matmul(ps_o_[:, 0:257], pt_sb[par][:, off + 128*c: off + 128*(c+1)],
                                    vbf[bpar][blk][:, voff:voff+257],
                                    start=(c == 0), stop=(c == nc_ - 1))
            return last
        waits = [(v_pt, g + 1)]
        if g >= 1:
            waits.append((v_up, g))
        p.step("tensor", tom, waits=waits, inc=(t_o, 1))
        def vup(e, kl=kl, g=g, jp=jp):
            last = None
            for ps_o_, o_, co in ((ps_oa, oa[jp], corr_a), (ps_ob, ob_[jp], corr_b)):
                if kl == 0:
                    last = e.tensor_scalar(o_[:, 0:257], ps_o_[:, 0:257], 1.0, None, ALU.mult)
                else:
                    last = e.scalar_tensor_tensor(o_[:, 0:257], o_[:, 0:257],
                                                  co[:, g % 2:g % 2 + 1],
                                                  ps_o_[:, 0:257], ALU.mult, ALU.add)
            return last
        p.step("vector", vup, waits=[(t_o, g + 1)], inc=(v_up, 1))

    def fin_emit(jv, j, gk_j, fin_j):
        jp = j % 2
        def sfin(e, jp=jp):
            e.activation(out=rcp_a[jp][:, :], in_=oa[jp][:, 256:257], func=AF.Ln)
            e.activation(out=rcp_a[jp][:, :], in_=rcp_a[jp][:, :], func=AF.Exp, scale=-1.0)
            e.activation(out=rcp_b[jp][:, :], in_=ob_[jp][:, 256:257], func=AF.Ln)
            return e.activation(out=rcp_b[jp][:, :], in_=rcp_b[jp][:, :], func=AF.Exp, scale=-1.0)
        p.step("scalar", sfin, waits=[(v_up, gk_j)], inc=(sc_fin, 1))
        def vfin(e, jp=jp):
            e.tensor_scalar(o2n[jp][:, :], ob_[jp][:, 0:256], rcp_b[jp][:, 0:1], neglam_sb[:, 0:1],
                            ALU.mult, ALU.mult)
            return e.scalar_tensor_tensor(y_tm[jp][:, :], oa[jp][:, 0:256], rcp_a[jp][:, 0:1],
                                          o2n[jp][:, :], ALU.mult, ALU.add)
        p.step("vector", vfin, waits=[(sc_fin, fin_j + 1)], inc=(v_fin, 1))
        def tyr(e, jp=jp):
            e.transpose(ps_y[:, 0:128], y_tm[jp][:, 0:128], consts.ident_f[:, :])
            return e.transpose(ps_y[:, 128:256], y_tm[jp][:, 128:256], consts.ident_f[:, :])
        waits = [(v_fin, fin_j + 1)]
        if fin_j >= 1:
            waits.append((v_ev, fin_j))
        p.step("tensor", tyr, waits=waits, inc=(t_y, 1))
        def vev(e, j=j):
            e.tensor_copy(y_stage[:, 128*j:128*(j+1)], ps_y[:, 0:128])
            return e.tensor_copy(y_stage[:, 2048 + 128*j:2048 + 128*(j+1)], ps_y[:, 128:256])
        p.step("vector", vev, waits=[(t_y, fin_j + 1), (so, p.rel(so))], inc=(v_ev, 1))

    for jv in range(8):
        bpar = jv % 2
        ha, hb = jv, jv + 8
        waits = [(gp1, 16)] if jv < 2 else [(t_o, kt_end[jv - 2])]
        p.step("sync", lambda e, bpar=bpar, ha=ha: e.dma_start(
            out=qa[bpar][:, :], in_=q_tm[:, 128*ha:128*(ha+1)], transpose=True),
            waits=waits, inc=(sd, 16))
        p.step("sync", lambda e, bpar=bpar, hb=hb: e.dma_start(
            out=qb[bpar][:, :], in_=q_tm[:, 128*hb:128*(hb+1)], transpose=True), inc=(sd, 16))
        for blk in range(2):
            p.step("sync", lambda e, bpar=bpar, ha=ha, blk=blk: e.dma_start(
                out=ka[bpar][blk][:, :], in_=k_tm[blk][:, 128*ha:128*(ha+1)],
                transpose=True), inc=(sd, 16))
            p.step("sync", lambda e, bpar=bpar, hb=hb, blk=blk: e.dma_start(
                out=kbb[bpar][blk][:, :], in_=k_tm[blk][:, 128*hb:128*(hb+1)],
                transpose=True), inc=(sd, 16))
            vv = vbf[bpar][blk][:, :].rearrange("p (kc n) -> p kc n", n=257)
            vsrc = v_tm[blk].rearrange("(kc p) n -> p kc n", p=128)
            p.step("sync", lambda e, jv=jv, blk=blk, vv=vv, vsrc=vsrc, bpar=bpar: e.dma_start(
                out=vv[:, :, 0:256], in_=vsrc[:, :, 256*jv:256*(jv+1)]),
                inc=(sd, 16))
        load_tgt = p.rel(sd)

        for j in range(16):
            kts = ktiles_uniform(j)
            for kl, (k0, kw, msk) in enumerate(kts):
                front(bpar, j, k0, kw, msk, kl, gk, load_tgt)
                for fn in pend:
                    fn()
                pend = [lambda bpar=bpar, k0=k0, kw=kw, kl=kl, g=gk, j=j:
                        back(bpar, k0, kw, kl, g, j % 2)]
                gk += 1
            pend.append(lambda jv=jv, j=j, gk_j=gk, fin_j=fin: fin_emit(jv, j, gk_j, fin_j))
            fin += 1

        def dma_out(jv=jv, fin_jv=fin):
            for rb_ in range(2):
                p.step("sync", lambda e, jv=jv, rb_=rb_: e.dma_start(
                    out=s_fm_out[256*jv + 128*rb_: 256*jv + 128*(rb_+1), :],
                    in_=y_stage[:, 2048*rb_:2048*(rb_+1)]),
                    waits=[(v_ev, fin_jv)], inc=(so, 16))
        pend.append(dma_out)
        kt_end.append(gk)
    for fn in pend:
        fn()
    p.step("sync", lambda e: e.nop(), waits=[(so, p.rel(so))])
    return p

def make_wo_phase(ctx, consts, tag, o_fm_sb, wo_dram, s_fm_out):
    nc, st = ctx.nc, ctx.stack
    p = Plan(ctx, f"wo_{tag}")
    wo_sb = st.enter_context(nc.sbuf_tensor(f"wo_w_{tag}", [128, 16 * 2048], BF16))
    sev = [st.enter_context(nc.sbuf_tensor(f"wo_ev{i}_{tag}", [128, 512], F32)) for i in range(2)]
    ps = [st.enter_context(nc.psum_tensor(f"wo_ps{i}_{tag}", [128, 512], F32)) for i in range(2)]
    sd = p.sem("dma_sync"); so = p.sem("dma_out")
    smm = p.sem("t1"); sevs = p.sem("sc1")
    wsrc = wo_dram.rearrange("(kc p) n -> p kc n", p=128)
    wdst = wo_sb[:, :].rearrange("p (kc n) -> p kc n", kc=16)
    for q in range(4):
        p.step("sync", lambda e, q=q: e.dma_start(out=wdst[:, 4*q:4*q+4, :],
                                                  in_=wsrc[:, 4*q:4*q+4, :]), inc=(sd, 16))
    wt = p.rel(sd)
    g = 0
    for mc in range(16):
        for tc in range(4):
            def mm(e, mc=mc, tc=tc, g=g):
                last = None
                for kc in range(16):
                    last = e.matmul(ps[g % 2][:, :], wo_sb[:, 2048*kc + 128*mc: 2048*kc + 128*(mc+1)],
                                    o_fm_sb[:, 2048*kc + 512*tc: 2048*kc + 512*(tc+1)],
                                    start=(kc == 0), stop=(kc == 15))
                return last
            waits = [(sd, wt)]
            if g >= 2:
                waits.append((sevs, g - 1))
            p.step("tensor", mm, waits=waits, inc=(smm, 1))
            def ev(e, g=g):
                return e.copy(sev[g % 2][:, :], ps[g % 2][:, :])
            waits = [(smm, g + 1)]
            if g >= 2:
                waits.append((so, 16 * (g - 1)))
            p.step("scalar", ev, waits=waits, inc=(sevs, 1))
            p.step("sync", lambda e, mc=mc, tc=tc, g=g: e.dma_start(
                out=s_fm_out[128*mc:128*(mc+1), 512*tc:512*(tc+1)], in_=sev[g % 2][:, :]),
                waits=[(sevs, g + 1)], inc=(so, 16))
            g += 1
    p.step("sync", lambda e: e.nop(), waits=[(so, 16 * g)])
    return p


def make_ffn_l1(ctx, consts, tag, xn_dram, wg_dram, wu_dram, wd_dram, g_fm_out):
    nc, st = ctx.nc, ctx.stack
    p = Plan(ctx, f"ffn1_{tag}")
    xn_sb = st.enter_context(nc.sbuf_tensor(f"f1_xn_{tag}", [128, 16 * 512], BF16))
    wbuf = [st.enter_context(nc.sbuf_tensor(f"f1_w{i}_{tag}", [128, 16 * 1024], BF16)) for i in range(2)]
    hs = st.enter_context(nc.sbuf_tensor(f"f1_hs_{tag}", [128, 64 * 512], BF16))
    hj = [st.enter_context(nc.sbuf_tensor(f"f1_hj{i}_{tag}", [128, 512], F32)) for i in range(4)]
    aj = [st.enter_context(nc.sbuf_tensor(f"f1_aj{i}_{tag}", [128, 512], F32)) for i in range(4)]
    hg = st.enter_context(nc.sbuf_tensor(f"f1_hg_{tag}", [128, 512], F32))
    hgj = [st.enter_context(nc.sbuf_tensor(f"f1_hgj{i}_{tag}", [128, 512], F32)) for i in range(4)]
    scr = [st.enter_context(nc.sbuf_tensor(f"f1_scr{i}_{tag}", [128, 512], F32)) for i in range(7)]
    wdbuf = wbuf
    gsb = [st.enter_context(nc.sbuf_tensor(f"f1_g{i}_{tag}", [128, 512], F32)) for i in range(2)]
    pg = [st.enter_context(nc.psum_tensor(f"f1_pg{i}_{tag}", [128, 512], F32)) for i in range(4)]
    pu = [st.enter_context(nc.psum_tensor(f"f1_pu{i}_{tag}", [128, 512], F32)) for i in range(4)]
    ps_d = [pg[0], pg[1]]
    sd = p.sem("dma_sync"); so = p.sem("dma_out")
    t_mm = p.sem("t1"); v_h = p.sem("v1"); sc_s = p.sem("sc1"); v_sp = p.sem("v2")
    t_d = p.sem("t2"); sc_d = p.sem("sc2")

    xn_src = xn_dram.rearrange("(kc p) t -> p kc t", p=128)
    wg_src = wg_dram.rearrange("(kc p) n -> p kc n", p=128)
    wu_src = wu_dram.rearrange("(kc p) n -> p kc n", p=128)
    wd_src = wd_dram.rearrange("(ic p) n -> p ic n", p=128)

    for tc in range(4):
        base_v_sp = p.rel(v_sp)
        def ldxn(e, tc=tc):
            return e.dma_start(out=xn_sb[:, :].rearrange("p (kc t) -> p kc t", kc=16),
                               in_=xn_src[:, :, 512*tc:512*(tc+1)])
        waits = []
        if tc > 0:
            waits.append((t_mm, p.rel(t_mm)))
        p.step("sync", ldxn, waits=waits, inc=(sd, 16))
        xn_tgt = p.rel(sd)

        gu = 0
        for c in range(16):
            wb = wbuf[c % 2]
            waits = [(t_mm, p.rel(t_mm) - 4)] if (tc > 0 or c >= 2) else []
            if tc > 0 and c < 2:
                waits.append((t_d, 16 * tc - (1 if c == 0 else 0)))
            wv_ = wb[:, :].rearrange("p (kc two n) -> p kc two n", two=2, n=512)
            p.step("sync", lambda e, c=c, wv_=wv_: e.dma_start(
                out=wv_[:, :, 0, :], in_=wg_src[:, :, 512*c:512*(c+1)]),
                waits=waits, inc=(sd, 16))
            p.step("sync", lambda e, c=c, wv_=wv_: e.dma_start(
                out=wv_[:, :, 1, :], in_=wu_src[:, :, 512*c:512*(c+1)]),
                inc=(sd, 16))
            w_tgt = p.rel(sd)
            for j in range(4):
                pg_ = pg[j]; pu_ = pu[j]
                def mm(e, c=c, j=j, wb=wb, pg_=pg_, pu_=pu_):
                    last = None
                    for kc in range(16):
                        last = e.matmul(pg_[:, :], wb[:, 1024*kc + 128*j: 1024*kc + 128*(j+1)],
                                        xn_sb[:, 512*kc:512*(kc+1)],
                                        start=(kc == 0), stop=(kc == 15))
                    for kc in range(16):
                        last = e.matmul(pu_[:, :], wb[:, 1024*kc + 512 + 128*j: 1024*kc + 512 + 128*(j+1)],
                                        xn_sb[:, 512*kc:512*(kc+1)],
                                        start=(kc == 0), stop=(kc == 15))
                    return last
                waits = [(sd, max(w_tgt, xn_tgt))]
                if tc > 0 or gu >= 4:
                    waits.append((v_h, p.rel(v_h) - 3))
                p.step("tensor", mm, waits=waits, inc=(t_mm, 1))
                p.step("scalar", lambda e, j=j, pg_=pg_: e.activation(
                    out=hgj[j][:, :], in_=pg_[:, :], func=AF.Silu),
                    waits=[(t_mm, p.rel(t_mm)), (v_h, p.rel(v_h) - (3 if (tc > 0 or gu >= 4) else gu))],
                    inc=(sc_s, 1))
                p.step("vector", lambda e, j=j, pu_=pu_: e.tensor_mul(hj[j][:, :], hgj[j][:, :], pu_[:, :]),
                       waits=[(sc_s, p.rel(sc_s))], inc=(v_h, 1))
                gu += 1
            def sabs(e):
                last = None
                for j in range(4):
                    last = e.activation(out=aj[j][:, :], in_=hj[j][:, :], func=AF.Abs)
                return last
            p.step("scalar", sabs, waits=[(v_h, p.rel(v_h))], inc=(sc_s, 1))
            def vsparse(e, c=c, tc=tc):
                MIN = ALU.min
                e.tensor_tensor(scr[0][:, :], aj[0][:, :], aj[1][:, :], MIN)
                e.tensor_tensor(scr[1][:, :], aj[2][:, :], aj[3][:, :], MIN)
                e.tensor_tensor(scr[2][:, :], aj[0][:, :], aj[2][:, :], MIN)
                e.tensor_tensor(scr[3][:, :], aj[1][:, :], aj[3][:, :], MIN)
                e.tensor_tensor(scr[4][:, :], aj[0][:, :], aj[3][:, :], MIN)
                e.tensor_tensor(scr[5][:, :], aj[1][:, :], aj[2][:, :], MIN)
                e.tensor_max(scr[6][:, :], scr[0][:, :], scr[1][:, :])
                e.tensor_max(scr[0][:, :], scr[2][:, :], scr[3][:, :])
                e.tensor_max(scr[3][:, :], scr[4][:, :], scr[5][:, :])
                e.tensor_max(scr[2][:, :], scr[6][:, :], scr[0][:, :])
                e.tensor_max(scr[1][:, :], scr[2][:, :], scr[3][:, :])
                gsel = scr[5]
                last = None
                for j in range(4):
                    e.tensor_tensor(gsel[:, :], aj[j][:, :], scr[1][:, :], ALU.is_ge)
                    last = e.tensor_mul(hs[:, 512*(4*c+j):512*(4*c+j+1)], hj[j][:, :], gsel[:, :])
                return last
            waits = [(sc_s, p.rel(sc_s))]
            if tc > 0:
                waits.append((t_d, 16 * tc))
            p.step("vector", vsparse, waits=waits, inc=(v_sp, 1))
        if tc == 0 and getattr(ctx, "dbg_hs", None) is not None:
            p.step("sync", lambda e: e.dma_start(out=ctx.dbg_hs, in_=hs[:, :]),
                   waits=[(v_sp, base_v_sp + 16)], inc=(so, 16))
        for oc in range(16):
            wdb = wdbuf[oc % 2]
            waits = [(v_sp, base_v_sp + 16)]
            if tc > 0 or oc >= 2:
                waits.append((t_d, p.rel(t_d) - 1))
            p.step("sync", lambda e, oc=oc, wdb=wdb: e.dma_start(
                out=wdb[:, 0:56*128].rearrange("p (ic n) -> p ic n", n=128),
                in_=wd_src[:, 0:56, 128*oc:128*(oc+1)]),
                waits=waits, inc=(sd, 16))
            p.step("sync", lambda e, oc=oc, wdb=wdb: e.dma_start(
                out=wdb[:, 56*128:64*128].rearrange("p (ic n) -> p ic n", n=128),
                in_=wd_src[:, 56:64, 128*oc:128*(oc+1)]),
                inc=(sd, 16))
            wd_tgt = p.rel(sd)
            def mmd(e, oc=oc, wdb=wdb):
                last = None
                for ic in range(64):
                    last = e.matmul(ps_d[oc % 2][:, :], wdb[:, 128*ic:128*(ic+1)],
                                    hs[:, 512*ic:512*(ic+1)],
                                    start=(ic == 0), stop=(ic == 63))
                return last
            waits = [(sd, wd_tgt)]
            if tc > 0 or oc >= 2:
                waits.append((sc_d, p.rel(sc_d) - 1))
            p.step("tensor", mmd, waits=waits, inc=(t_d, 1))
            def ev(e, oc=oc):
                return e.activation(out=gsb[oc % 2][:, :], in_=ps_d[oc % 2][:, :], func=AF.Silu)
            waits = [(t_d, p.rel(t_d))]
            if p.rel(so) >= 32:
                waits.append((so, p.rel(so) - 16))
            p.step("scalar", ev, waits=waits, inc=(sc_d, 1))
            p.step("sync", lambda e, oc=oc, tc=tc: e.dma_start(
                out=g_fm_out[128*oc:128*(oc+1), 512*tc:512*(tc+1)], in_=gsb[oc % 2][:, :]),
                waits=[(sc_d, p.rel(sc_d))], inc=(so, 16))
    p.step("sync", lambda e: e.nop(), waits=[(so, p.rel(so))])
    return p


def make_ffn_l0(ctx, consts, tag, xn_dram, w, g_fm_out):
    nc, st = ctx.nc, ctx.stack
    p = Plan(ctx, f"ffn0_{tag}")
    F = 7168; NFS = 56
    xn_sb = st.enter_context(nc.sbuf_tensor(f"f0_xn_{tag}", [128, 16 * 512], BF16))
    wg8_sb = st.enter_context(nc.sbuf_tensor(f"f0_wg8_{tag}", [128, 16 * 8], BF16))
    b2w = st.enter_context(nc.sbuf_tensor(f"f0_b2w_{tag}", [128, 2 * 2048], BF16))
    u_sb = st.enter_context(nc.sbuf_tensor(f"f0_u_{tag}", [128, 128], F32))
    u1_sb = st.enter_context(nc.sbuf_tensor(f"f0_u1_{tag}", [128, 128], F32))
    nmx = st.enter_context(nc.sbuf_tensor(f"f0_nmx_{tag}", [128, 16], F32))
    nm1_sb = st.enter_context(nc.sbuf_tensor(f"f0_nm1_{tag}", [128, 16], F32))
    nm2_sb = st.enter_context(nc.sbuf_tensor(f"f0_nm2_{tag}", [128, 16], F32))
    r_t1 = st.enter_context(nc.sbuf_tensor(f"f0_rt1_{tag}", [128, 128], F32))
    r_t2 = st.enter_context(nc.sbuf_tensor(f"f0_rt2_{tag}", [128, 128], F32))
    nidx = st.enter_context(nc.sbuf_tensor(f"f0_nidx_{tag}", [128, 32], F32))
    idx_sb = st.enter_context(nc.sbuf_tensor(f"f0_idx_{tag}", [128, 32], F32))
    m1_sb = st.enter_context(nc.sbuf_tensor(f"f0_m1_{tag}", [128, 16], F32))
    m2_sb = st.enter_context(nc.sbuf_tensor(f"f0_m2_{tag}", [128, 16], F32))
    negden = st.enter_context(nc.sbuf_tensor(f"f0_den_{tag}", [128, 16], F32))
    rcpd_sb = st.enter_context(nc.sbuf_tensor(f"f0_rcpd_{tag}", [128, 16], F32))
    rw0_sb = st.enter_context(nc.sbuf_tensor(f"f0_rw0_{tag}", [128, 16], F32))
    rw1_sb = st.enter_context(nc.sbuf_tensor(f"f0_rw1_{tag}", [128, 16], F32))
    e0_sb = st.enter_context(nc.sbuf_tensor(f"f0_e0_{tag}", [128, 128], F32))
    e1_sb = st.enter_context(nc.sbuf_tensor(f"f0_e1_{tag}", [128, 128], F32))
    e0t = st.enter_context(nc.sbuf_tensor(f"f0_e0t_{tag}", [8, 512], BF16))
    e1t = st.enter_context(nc.sbuf_tensor(f"f0_e1t_{tag}", [8, 512], BF16))
    rwt = st.enter_context(nc.sbuf_tensor(f"f0_rwt_{tag}", [1, 1024], F32))
    rwb0 = st.enter_context(nc.sbuf_tensor(f"f0_rwb0_{tag}", [128, 512], F32))
    rwb1 = st.enter_context(nc.sbuf_tensor(f"f0_rwb1_{tag}", [128, 512], F32))
    em0 = st.enter_context(nc.sbuf_tensor(f"f0_em0_{tag}", [128, 1024], BF16))
    em1 = st.enter_context(nc.sbuf_tensor(f"f0_em1_{tag}", [128, 1024], BF16))
    ma1 = [st.enter_context(nc.sbuf_tensor(f"f0_ma1{k}_{tag}", [128, 1024], BF16)) for k in range(2)]
    ma3 = [st.enter_context(nc.sbuf_tensor(f"f0_ma3{k}_{tag}", [128, 1024], BF16)) for k in range(2)]
    a2c = st.enter_context(nc.sbuf_tensor(f"f0_a2c_{tag}", [128, 1024], BF16))
    wbf = [st.enter_context(nc.sbuf_tensor(f"f0_wb{i}_{tag}", [128, 8960], BF16)) for i in range(2)]
    xx0 = st.enter_context(nc.sbuf_tensor(f"f0_xx0_{tag}", [128, 512], BF16))
    xx1 = st.enter_context(nc.sbuf_tensor(f"f0_xx1_{tag}", [128, 512], BF16))
    xz = st.enter_context(nc.sbuf_tensor(f"f0_xz_{tag}", [128, NFS * 512], BF16))
    base1s = st.enter_context(nc.sbuf_tensor(f"f0_b1s_{tag}", [128, 512], F32))
    base3s = st.enter_context(nc.sbuf_tensor(f"f0_b3s_{tag}", [128, 512], F32))
    x1s = [st.enter_context(nc.sbuf_tensor(f"f0_x1s{k}_{tag}", [128, 512], F32)) for k in range(2)]
    x3s = [st.enter_context(nc.sbuf_tensor(f"f0_x3s{k}_{tag}", [128, 512], F32)) for k in range(2)]
    sil_s = [st.enter_context(nc.sbuf_tensor(f"f0_sil{k}_{tag}", [128, 512], F32)) for k in range(2)]
    a2acc = [[st.enter_context(nc.sbuf_tensor(f"f0_a2a{k}{i}_{tag}", [128, 1024], F32))
              for i in range(2)] for k in range(2)]
    gs = [st.enter_context(nc.sbuf_tensor(f"f0_gs{i}_{tag}", [128, 512], F32)) for i in range(2)]
    pA = st.enter_context(nc.psum_tensor(f"f0_pA_{tag}", [128, 1024], F32))
    pB0 = st.enter_context(nc.psum_tensor(f"f0_pB0_{tag}", [128, 1024], F32))
    pB1 = st.enter_context(nc.psum_tensor(f"f0_pB1_{tag}", [128, 1024], F32))
    pa2 = st.enter_context(nc.psum_tensor(f"f0_pa2_{tag}", [128, 1024], F32))
    pB = pB0
    pC = pB1
    pD = pa2

    sd = p.sem("dma_sync"); so = p.sem("dma_out")
    t1 = p.sem("t1"); t2 = p.sem("t2"); t3 = p.sem("t3"); t4 = p.sem("t4")
    v1 = p.sem("v1"); v2 = p.sem("v2"); v3 = p.sem("v3"); v4 = p.sem("v4"); v5 = p.sem("v5")
    sc1 = p.sem("sc1"); sc2 = p.sem("sc2"); sc3 = p.sem("sc3")
    gp = p.sem("gp1")

    xn_src = xn_dram.rearrange("(kc p) t -> p kc t", p=128)
    p.step("sync", lambda e: e.dma_start(out=wg8_sb[:, :].rearrange("p (kc n) -> p kc n", n=8),
                                         in_=w["wg8"].rearrange("(kc p) n -> p kc n", p=128)), inc=(sd, 16))
    p.step("sync", lambda e: e.dma_start(out=b2w[:, :].rearrange("p (h n) -> p h n", h=2),
                                         in_=w["b2"].rearrange("(h p) n -> p h n", p=128)), inc=(sd, 16))
    res_tgt = p.rel(sd)


    for tcq in range(4):
        p.step("sync", lambda e, tcq=tcq: e.dma_start(
            out=xn_sb[:, :].rearrange("p (kc t) -> p kc t", kc=16),
            in_=xn_src[:, :, 512*tcq:512*(tcq+1)]),
            waits=[(t1, p.rel(t1))], inc=(sd, 16))
        xn_tgt_r = p.rel(sd)
        def rmm(e, tcq=tcq):
            last = None
            for t4_ in range(4):
                col = 32*tcq + 8*t4_
                for kc in range(16):
                    last = e.matmul(pA[:, col:col+8],
                                    xn_sb[:, 512*kc + 128*t4_: 512*kc + 128*(t4_+1)],
                                    wg8_sb[:, 8*kc:8*(kc+1)],
                                    start=(kc == 0), stop=(kc == 15))
            return last
        p.step("tensor", rmm, waits=[(sd, max(res_tgt, xn_tgt_r))], inc=(t1, 1))
    p.step("vector", lambda e: e.tensor_reduce(
        nmx[:, :], pA[:, 0:128].rearrange("p (g n) -> p g n", g=16), AX.X, ALU.max, negate=True),
        waits=[(t1, p.rel(t1))], inc=(v1, 1))
    def sexp(e):
        last = None
        for g in range(16):
            last = e.activation(out=u_sb[:, 8*g:8*(g+1)], in_=pA[:, 8*g:8*(g+1)],
                                func=AF.Exp, bias=nmx[:, g:g+1])
        return last
    p.step("scalar", sexp, waits=[(v1, p.rel(v1))], inc=(sc1, 1))
    p.step("vector", lambda e: e.tensor_reduce(
        nm1_sb[:, :], u_sb[:, :].rearrange("p (g n) -> p g n", g=16), AX.X, ALU.max, negate=True),
        waits=[(sc1, p.rel(sc1))], inc=(v1, 1))
    p.step("scalar", lambda e: e.mul(m1_sb[:, :], nm1_sb[:, :], -1.0),
           waits=[(v1, p.rel(v1))], inc=(sc1, 1))
    def vA(e):
        u3 = u_sb[:, :].rearrange("p (g n) -> p g n", g=16)
        e.tensor_tensor(r_t1[:, :].rearrange("p (g n) -> p g n", g=16), u3,
                        m1_sb[:, :].rearrange("p (g n) -> p g n", g=16)[:, :, 0:1].broadcast_to([128, 16, 8]),
                        ALU.is_ge)
        e.tensor_mul(r_t2[:, :], r_t1[:, :], consts.iotam99[:, :])
        e.tensor_scalar_add(r_t1[:, :], r_t2[:, :], 99.0)
        return e.tensor_reduce(nidx[:, 0:16], r_t1[:, :].rearrange("p (g n) -> p g n", g=16),
                               AX.X, ALU.min)
    p.step("vector", vA, waits=[(sc1, p.rel(sc1))], inc=(v1, 1))
    p.step("scalar", lambda e: e.copy(idx_sb[:, 0:16], nidx[:, 0:16]),
           waits=[(v1, p.rel(v1))], inc=(sc1, 1))
    def vB(e):
        e.tensor_tensor(e0_sb[:, :].rearrange("p (g n) -> p g n", g=16),
                        consts.iota8x4[:, :].rearrange("p (g n) -> p g n", g=16),
                        idx_sb[:, 0:16].rearrange("p (g n) -> p g n", g=16)[:, :, 0:1].broadcast_to([128, 16, 8]),
                        ALU.is_equal)
        e.tensor_scalar_mul(r_t1[:, :], e0_sb[:, :], -1e30)
        e.tensor_add(u1_sb[:, :], u_sb[:, :], r_t1[:, :])
        return e.tensor_reduce(nm2_sb[:, :], u1_sb[:, :].rearrange("p (g n) -> p g n", g=16),
                               AX.X, ALU.max, negate=True)
    p.step("vector", vB, waits=[(sc1, p.rel(sc1))], inc=(v1, 1))
    p.step("scalar", lambda e: e.mul(m2_sb[:, :], nm2_sb[:, :], -1.0),
           waits=[(v1, p.rel(v1))], inc=(sc1, 1))
    def vC(e):
        u3 = u1_sb[:, :].rearrange("p (g n) -> p g n", g=16)
        e.tensor_tensor(r_t1[:, :].rearrange("p (g n) -> p g n", g=16), u3,
                        m2_sb[:, :].rearrange("p (g n) -> p g n", g=16)[:, :, 0:1].broadcast_to([128, 16, 8]),
                        ALU.is_ge)
        e.tensor_mul(r_t2[:, :], r_t1[:, :], consts.iotam99[:, :])
        e.tensor_scalar_add(r_t1[:, :], r_t2[:, :], 99.0)
        return e.tensor_reduce(nidx[:, 16:32], r_t1[:, :].rearrange("p (g n) -> p g n", g=16),
                               AX.X, ALU.min)
    p.step("vector", vC, waits=[(sc1, p.rel(sc1))], inc=(v1, 1))
    p.step("scalar", lambda e: e.copy(idx_sb[:, 16:32], nidx[:, 16:32]),
           waits=[(v1, p.rel(v1))], inc=(sc1, 1))
    def vD(e):
        e.tensor_tensor(e1_sb[:, :].rearrange("p (g n) -> p g n", g=16),
                        consts.iota8x4[:, :].rearrange("p (g n) -> p g n", g=16),
                        idx_sb[:, 16:32].rearrange("p (g n) -> p g n", g=16)[:, :, 0:1].broadcast_to([128, 16, 8]),
                        ALU.is_equal)
        return e.tensor_add(negden[:, :], nm1_sb[:, :], nm2_sb[:, :])
    p.step("vector", vD, waits=[(sc1, p.rel(sc1))], inc=(v1, 1))
    def srcp(e):
        e.activation(out=rcpd_sb[:, :], in_=negden[:, :], func=AF.Ln, scale=-1.0)
        return e.activation(out=rcpd_sb[:, :], in_=rcpd_sb[:, :], func=AF.Exp, scale=-1.0)
    p.step("scalar", srcp, waits=[(v1, p.rel(v1))], inc=(sc1, 1))
    def vE(e):
        e.tensor_mul(rw0_sb[:, :], m1_sb[:, :], rcpd_sb[:, :])
        return e.tensor_mul(rw1_sb[:, :], m2_sb[:, :], rcpd_sb[:, :])
    p.step("vector", vE, waits=[(sc1, p.rel(sc1))], inc=(v1, 1))

    for tc in range(4):
        waits = [(t4, p.rel(t4))] if tc > 0 else [(t1, p.rel(t1))]
        p.step("sync", lambda e, tc=tc: e.dma_start(
            out=xn_sb[:, :].rearrange("p (kc t) -> p kc t", kc=16),
            in_=xn_src[:, :, 512*tc:512*(tc+1)]), waits=waits, inc=(sd, 16))
        xn_tgt = p.rel(sd)

        def ttr(e, tc=tc):
            for t4_ in range(4):
                g = 4*tc + t4_
                e.transpose(pB[0:8, 128*t4_:128*(t4_+1)], e0_sb[:, 8*g:8*(g+1)], consts.ident_f[:, :])
            for t4_ in range(4):
                g = 4*tc + t4_
                e.transpose(pB[0:8, 512 + 128*t4_: 512 + 128*(t4_+1)], e1_sb[:, 8*g:8*(g+1)], consts.ident_f[:, :])
            for t4_ in range(4):
                g = 4*tc + t4_
                e.transpose(pC[0:1, 128*t4_:128*(t4_+1)], rw0_sb[:, g:g+1], consts.ident_f[:, :])
            last = None
            for t4_ in range(4):
                g = 4*tc + t4_
                last = e.transpose(pC[0:1, 512 + 128*t4_: 512 + 128*(t4_+1)], rw1_sb[:, g:g+1], consts.ident_f[:, :])
            return last
        waits = [(v1, p.rel(v1)), (sd, xn_tgt)]
        if tc > 0:
            waits.append((v3, p.rel(v3)))
        p.step("tensor", ttr, waits=waits, inc=(t2, 1))
        def vevt(e):
            e.tensor_copy(e0t[:, :], pB[0:8, 0:512])
            e.tensor_copy(e1t[:, :], pB[0:8, 512:1024])
            return e.tensor_copy(rwt[:, :], pC[0:1, 0:1024])
        p.step("vector", vevt, waits=[(t2, p.rel(t2))], inc=(v2, 1))
        def tbc(e):
            e.matmul(pD[:, 0:512], consts.ones1_f[:, :], rwt[:, 0:512], start=True, stop=True)
            return e.matmul(pD[:, 512:1024], consts.ones1_f[:, :], rwt[:, 512:1024], start=True, stop=True)
        p.step("tensor", tbc, waits=[(v2, p.rel(v2))], inc=(t2, 1))
        def vevb(e):
            e.tensor_copy(rwb0[:, :], pD[:, 0:512])
            return e.tensor_copy(rwb1[:, :], pD[:, 512:1024])
        p.step("vector", vevb, waits=[(t2, p.rel(t2))], inc=(v2, 1))

        a1w = wbf[0][:, 0:4096]
        a3w = wbf[1][:, 0:4096]
        waits = []
        if tc > 0:
            waits.append((t3, p.rel(t3)))
            waits.append((t4, p.rel(t4)))
        def lda13(e):
            e.dma_start(out=wbf[0][:, 0:4096].rearrange("p (kc n) -> p kc n", n=256),
                        in_=w["a1"].rearrange("(kc p) n -> p kc n", p=128)).then_inc(sd.h, 16)
            return e.dma_start(out=wbf[1][:, 0:4096].rearrange("p (kc n) -> p kc n", n=256),
                               in_=w["a3"].rearrange("(kc p) n -> p kc n", p=128))
        p.step("sync", lda13, waits=waits, inc=(sd, 16), manual=((sd, 16),))
        a13_tgt = p.rel(sd)
        def tma(e):
            last = None
            for half in range(2):
                for kc in range(16):
                    last = e.matmul(pA[:, 512*half:512*half+512],
                                    a1w[:, 256*kc + 128*half: 256*kc + 128*(half+1)],
                                    xn_sb[:, 512*kc:512*(kc+1)], start=(kc == 0), stop=(kc == 15))
            for half in range(2):
                for kc in range(16):
                    last = e.matmul(pB[:, 512*half:512*half+512],
                                    a3w[:, 256*kc + 128*half: 256*kc + 128*(half+1)],
                                    xn_sb[:, 512*kc:512*(kc+1)], start=(kc == 0), stop=(kc == 15))
            e.matmul(pC[:, 0:512], consts.b32a[:, :], e0t[:, :], start=True, stop=True)
            e.matmul(pC[:, 512:1024], consts.b32b[:, :], e0t[:, :], start=True, stop=True)
            e.matmul(pD[:, 0:512], consts.b32a[:, :], e1t[:, :], start=True, stop=True)
            last = e.matmul(pD[:, 512:1024], consts.b32b[:, :], e1t[:, :], start=True, stop=True)
            return last
        p.step("tensor", tma, waits=[(v2, p.rel(v2)), (v1, p.rel(v1)), (sd, a13_tgt)], inc=(t3, 1))
        def vem(e):
            e.tensor_copy(em0[:, :], pC[:, :])
            e.tensor_copy(em1[:, :], pD[:, :])
            e.tensor_mul(ma1[0][:, :], pA[:, :], em0[:, :])
            e.tensor_mul(ma1[1][:, :], pA[:, :], em1[:, :])
            e.tensor_mul(ma3[0][:, :], pB[:, :], em0[:, :])
            return e.tensor_mul(ma3[1][:, :], pB[:, :], em1[:, :])
        p.step("vector", vem, waits=[(t3, p.rel(t3))], inc=(v3, 1))

        w1_src = w["w1"].rearrange("(kc p) f -> p kc f", p=128)
        w3_src = w["w3"].rearrange("(kc p) f -> p kc f", p=128)
        b1_src = w["b1"].rearrange("(h p) f -> p h f", p=128)
        b3_src = w["b3"].rearrange("(h p) f -> p h f", p=128)
        a2_src = w["a2"].rearrange("(fs p) n -> p fs n", p=128)
        fs_base_t1 = p.rel(t1); fs_base_sc1 = p.rel(sc1); fs_base_v4 = p.rel(v4)
        fs_base_t4 = p.rel(t4); fs_base_sc3 = p.rel(sc3); fs_base_v5 = p.rel(v5)
        fs_base_t2 = p.rel(t2); fs_base_t3 = p.rel(t3); fs_base_v2 = p.rel(v2)
        fs_base_v3 = p.rel(v3)
        for fs in range(NFS):
            wb = wbf[fs % 2]
            waits = []
            if fs >= 2:
                waits.append((t3, fs_base_t3 + fs - 1))
            else:
                waits.append((t3, fs_base_t3))
            def ld(e, fs=fs, wb=wb):
                e.dma_start(out=wb[:, 0:2048].rearrange("p (kc n) -> p kc n", kc=16),
                            in_=w1_src[:, :, 128*fs:128*(fs+1)]).then_inc(sd.h, 16)
                e.dma_start(out=wb[:, 2048:4096].rearrange("p (kc n) -> p kc n", kc=16),
                            in_=w3_src[:, :, 128*fs:128*(fs+1)]).then_inc(sd.h, 16)
                e.dma_start(out=wb[:, 4096:4352].rearrange("p (h n) -> p h n", h=2),
                            in_=b1_src[:, :, 128*fs:128*(fs+1)]).then_inc(sd.h, 16)
                e.dma_start(out=wb[:, 4352:4608].rearrange("p (h n) -> p h n", h=2),
                            in_=b3_src[:, :, 128*fs:128*(fs+1)]).then_inc(sd.h, 16)
                return e.dma_start(out=wb[:, 4608:4864], in_=a2_src[:, fs, :])
            p.step("sync", ld, waits=waits, inc=(sd, 16), manual=((sd, 64),))
            ld_tgt = p.rel(sd)
            def s1(e, fs=fs, wb=wb):
                last = None
                for kc in range(16):
                    e.matmul(pA[:, 0:512], wb[:, 128*kc:128*(kc+1)],
                             xn_sb[:, 512*kc:512*(kc+1)], start=(kc == 0), stop=(kc == 15))
                for kc in range(16):
                    e.matmul(pA[:, 512:1024], wb[:, 2048 + 128*kc: 2048 + 128*(kc+1)],
                             xn_sb[:, 512*kc:512*(kc+1)], start=(kc == 0), stop=(kc == 15))
                for k, pBk, ma1k, ma3k in ((0, pB0, ma1[0], ma3[0]), (1, pB1, ma1[1], ma3[1])):
                    for h in range(2):
                        e.matmul(pBk[:, 0:512], wb[:, 4096 + 128*h: 4096 + 128*(h+1)],
                                 ma1k[:, 512*h:512*(h+1)], start=(h == 0), stop=(h == 1))
                    for h in range(2):
                        last = e.matmul(pBk[:, 512:1024], wb[:, 4352 + 128*h: 4352 + 128*(h+1)],
                                        ma3k[:, 512*h:512*(h+1)], start=(h == 0), stop=(h == 1))
                return last
            waits = [(sd, ld_tgt), (v3, fs_base_v3)]
            if fs >= 1:
                waits.append((sc1, fs_base_sc1 + fs))
                waits.append((v4, fs_base_v4 + fs))
            p.step("tensor", s1, waits=waits, inc=(t1, 1))
            def s2(e):
                e.copy(base1s[:, :], pA[:, 0:512])
                return e.copy(base3s[:, :], pA[:, 512:1024])
            waits = [(t1, fs_base_t1 + fs + 1)]
            if fs >= 1:
                waits.append((v4, fs_base_v4 + fs))
            p.step("scalar", s2, waits=waits, inc=(sc1, 1))
            def s3(e):
                e.tensor_add(x1s[0][:, :], base1s[:, :], pB0[:, 0:512])
                e.tensor_add(x3s[0][:, :], base3s[:, :], pB0[:, 512:1024])
                e.tensor_add(x1s[1][:, :], base1s[:, :], pB1[:, 0:512])
                return e.tensor_add(x3s[1][:, :], base3s[:, :], pB1[:, 512:1024])
            waits = [(sc1, fs_base_sc1 + fs + 1)]
            if fs >= 1:
                waits.append((sc3, fs_base_sc3 + fs))
            p.step("vector", s3, waits=waits, inc=(v4, 1))
            def s4(e):
                e.activation(out=sil_s[0][:, :], in_=x1s[0][:, :], func=AF.Silu)
                return e.activation(out=sil_s[1][:, :], in_=x1s[1][:, :], func=AF.Silu)
            p.step("scalar", s4, waits=[(v4, fs_base_v4 + fs + 1)], inc=(sc3, 1))
            def s5(e, fs=fs):
                e.tensor_mul(x1s[0][:, :], sil_s[0][:, :], x3s[0][:, :])
                e.tensor_mul(xx0[:, :], x1s[0][:, :], rwb0[:, :])
                e.tensor_mul(x1s[1][:, :], sil_s[1][:, :], x3s[1][:, :])
                e.tensor_mul(xx1[:, :], x1s[1][:, :], rwb1[:, :])
                return e.tensor_add(xz[:, 512*fs:512*(fs+1)], xx0[:, :], xx1[:, :])
            waits = [(sc3, fs_base_sc3 + fs + 1)]
            if fs >= 1:
                waits.append((t3, fs_base_t3 + fs))
            p.step("vector", s5, waits=waits, inc=(v5, 1))
            def s6(e, fs=fs, wb=wb):
                last = None
                for h in range(2):
                    last = e.matmul(pa2[:, 512*h:512*h+512], wb[:, 4608 + 128*h: 4608 + 128*(h+1)],
                                    xx0[:, :], start=True, stop=True)
                return last
            waits = [(v5, fs_base_v5 + fs + 1)]
            if fs >= 1:
                waits.append((v3, fs_base_v3 + fs))
            p.step("tensor", s6, waits=waits, inc=(t2, 1))
            def s7(e, fs=fs):
                dst = a2acc[0][fs % 2]
                if fs == 0:
                    return e.tensor_scalar(dst[:, :], pa2[:, :], 1.0, None, ALU.mult)
                return e.tensor_add(dst[:, :], a2acc[0][1 - fs % 2][:, :], pa2[:, :])
            p.step("vector", s7, waits=[(t2, fs_base_t2 + fs + 1)], inc=(v2, 1))
            def s8(e, fs=fs, wb=wb):
                last = None
                for h in range(2):
                    last = e.matmul(pa2[:, 512*h:512*h+512], wb[:, 4608 + 128*h: 4608 + 128*(h+1)],
                                    xx1[:, :], start=True, stop=True)
                return last
            p.step("tensor", s8, waits=[(v2, fs_base_v2 + fs + 1)], inc=(t3, 1))
            def s9(e, fs=fs):
                dst = a2acc[1][fs % 2]
                if fs == 0:
                    return e.tensor_scalar(dst[:, :], pa2[:, :], 1.0, None, ALU.mult)
                return e.tensor_add(dst[:, :], a2acc[1][1 - fs % 2][:, :], pa2[:, :])
            p.step("vector", s9, waits=[(t3, fs_base_t3 + fs + 1)], inc=(v3, 1))

        def va2(e):
            f0 = a2acc[0][(NFS - 1) % 2]; f1 = a2acc[1][(NFS - 1) % 2]
            e.tensor_mul(x1s[0][:, :], f0[:, 0:512], em0[:, 0:512])
            e.tensor_mul(x1s[1][:, :], f1[:, 0:512], em1[:, 0:512])
            e.tensor_add(a2c[:, 0:512], x1s[0][:, :], x1s[1][:, :])
            e.tensor_mul(x3s[0][:, :], f0[:, 512:1024], em0[:, 512:1024])
            e.tensor_mul(x3s[1][:, :], f1[:, 512:1024], em1[:, 512:1024])
            return e.tensor_add(a2c[:, 512:1024], x3s[0][:, :], x3s[1][:, :])
        p.step("vector", va2, waits=[(v3, p.rel(v3))], inc=(v3, 1))

        w2_src = w["w2"].rearrange("(fs p) n -> p fs n", p=128)
        for oc in range(16):
            wb = wbf[oc % 2]
            waits = [(v5, p.rel(v5))]
            if oc < 2:
                waits.append((t3, p.rel(t3)))
            else:
                waits.append((t4, p.rel(t4) - 1))
            p.step("sync", lambda e, oc=oc, wb=wb: e.dma_start(
                out=wb[:, 0:NFS*128].rearrange("p (fs n) -> p fs n", n=128),
                in_=w2_src[:, :, 128*oc:128*(oc+1)]), waits=waits, inc=(sd, 16))
            wd_tgt = p.rel(sd)
            def dmm2(e, oc=oc, wb=wb):
                par = oc % 2
                last = None
                for fs in range(NFS):
                    last = e.matmul(pA[:, 512*par:512*par+512], wb[:, 128*fs:128*(fs+1)],
                                    xz[:, 512*fs:512*(fs+1)], start=(fs == 0), stop=False)
                last = e.matmul(pA[:, 512*par:512*par+512],
                                b2w[:, 2048*0 + 128*oc: 2048*0 + 128*(oc+1)],
                                a2c[:, 0:512], start=False, stop=False)
                last = e.matmul(pA[:, 512*par:512*par+512],
                                b2w[:, 2048*1 + 128*oc: 2048*1 + 128*(oc+1)],
                                a2c[:, 512:1024], start=False, stop=True)
                return last
            waits = [(sd, wd_tgt), (v3, p.rel(v3)), (sc1, p.rel(sc1))]
            if oc >= 2:
                waits.append((sc2, p.rel(sc2) - 1))
            p.step("tensor", dmm2, waits=waits, inc=(t4, 1))
            def dev(e, oc=oc):
                return e.activation(out=gs[oc % 2][:, :], in_=pA[:, 512*(oc % 2):512*(oc % 2)+512],
                                    func=AF.Silu)
            waits = [(t4, p.rel(t4))]
            if p.rel(so) >= 32:
                waits.append((so, p.rel(so) - 16))
            p.step("scalar", dev, waits=waits, inc=(sc2, 1))
            p.step("sync", lambda e, oc=oc, tc=tc: e.dma_start(
                out=g_fm_out[128*oc:128*(oc+1), 512*tc:512*(tc+1)], in_=gs[oc % 2][:, :]),
                waits=[(sc2, p.rel(sc2))], inc=(so, 16))
    p.step("sync", lambda e: e.nop(), waits=[(so, p.rel(so))])
    return p


def make_combine(ctx, consts, tag, x_fm, s_fm, g_fm, out_fm):
    nc, st = ctx.nc, ctx.stack
    p = Plan(ctx, f"comb_{tag}")
    xb = [st.enter_context(nc.sbuf_tensor(f"cb_x{i}_{tag}", [128, 2048], F32)) for i in range(2)]
    sb = [st.enter_context(nc.sbuf_tensor(f"cb_s{i}_{tag}", [128, 2048], F32)) for i in range(2)]
    gb = [st.enter_context(nc.sbuf_tensor(f"cb_g{i}_{tag}", [128, 2048], F32)) for i in range(2)]
    ob = [st.enter_context(nc.sbuf_tensor(f"cb_o{i}_{tag}", [128, 2048], F32)) for i in range(2)]
    tb = [st.enter_context(nc.sbuf_tensor(f"cb_t{i}_{tag}", [128, 2048], F32)) for i in range(2)]
    sd = p.sem("dma_sync"); sg = p.sem("dma_gp"); so = p.sem("dma_out"); v1 = p.sem("v1")
    for kc in range(16):
        par = kc % 2
        waits = []
        if kc >= 2:
            waits.append((v1, p.rel(v1) - 1))
        def ld(e, kc=kc, par=par):
            e.dma_start(out=xb[par][:, :], in_=x_fm[128*kc:128*(kc+1), :]).then_inc(sd.h, 16)
            return e.dma_start(out=sb[par][:, :], in_=s_fm[128*kc:128*(kc+1), :])
        p.step("sync", ld, waits=waits, inc=(sd, 16), manual=((sd, 16),))
        p.step("gpsimd", lambda e, kc=kc, par=par: e.dma_start(
            out=gb[par][:, :], in_=g_fm[128*kc:128*(kc+1), :]), waits=waits, inc=(sg, 16))
        def cmb(e, par=par):
            e.tensor_mul(tb[par][:, :], sb[par][:, :], gb[par][:, :])
            return e.tensor_add(ob[par][:, :], tb[par][:, :], xb[par][:, :])
        waits2 = [(sd, p.rel(sd)), (sg, p.rel(sg))]
        if kc >= 2:
            waits2.append((so, p.rel(so) - 16))
        p.step("vector", cmb, waits=waits2, inc=(v1, 1))
        p.step("sync", lambda e, kc=kc, par=par: e.dma_start(
            out=out_fm[128*kc:128*(kc+1), :], in_=ob[par][:, :]),
            waits=[(v1, p.rel(v1))], inc=(so, 16))
    p.step("sync", lambda e: e.nop(), waits=[(so, p.rel(so))])
    return p


def make_precast(ctx, tag, pairs, wait=True):
    p = Plan(ctx, f"precast_{tag}")
    sg = p.sem("gp2")
    p.targets = {}
    for (src, dst) in pairs:
        p.step("gpsimd", lambda e, src=src, dst=dst: e.dma_start(out=dst, in_=src), inc=(sg, 16))
        p.targets[id(dst)] = ctx._sems["gp2"].n + p.rel(sg)
    if wait:
        p.step("gpsimd", lambda e: e.nop(), waits=[(sg, p.rel(sg))])
    return p


class Fence(PhaseBase):
    def __init__(self, ctx, sem_name, absolute):
        self.name = f"fence_{sem_name}_{absolute}"
        self.s = ctx.sem(sem_name)
        self.absolute = absolute
        self._totals = {}
    def sync(self, e):
        e.wait_ge(self.s.h, self.absolute)


def make_allgather(ctx, tag, in_ap, out_ap, wait=True):
    p = Plan(ctx, f"ag_{tag}")
    cc = p.sem("cc1")
    p.step("gpsimd", lambda e: e.collective_compute(
        "AllGather", mybir.AluOpType.bypass,
        replica_groups=[[0, 1], [2, 3], [4, 5], [6, 7]],
        ins=[in_ap], outs=[out_ap]), inc=(cc, 1))
    p.target = ctx._sems["cc1"].n + 1
    if wait:
        p.step("gpsimd", lambda e: e.nop(), waits=[(cc, p.rel(cc))])
    return p
from contextlib import ExitStack
from concourse.bass_utils import run_bass_kernel_spmd

POS_H = [
    np.concatenate([np.arange(256*i, 256*i+128) for i in range(16)]),
    np.concatenate([np.arange(256*i+128, 256*i+256) for i in range(16)]),
]
_CM = np.where(np.arange(128)[None, :] > np.arange(128)[:, None],
               np.float32(-1e30), np.float32(0.0))
AMASK_H = [np.concatenate([_CM, np.full((128, 128), -1e30, np.float32)], 1),
           np.concatenate([np.zeros((128, 128), np.float32), _CM], 1)]


def _tabs(pos):
    freqs = 1.0 / 10000.0 ** (np.arange(64, dtype=np.float64) / 64)
    ang = pos[:, None].astype(np.float64) * freqs[None, :]
    c = np.cos(ang).astype(np.float32)
    s = np.sin(ang).astype(np.float32)
    cd = np.ascontiguousarray(c.reshape(16, 128, 64).transpose(1, 0, 2).reshape(128, 1024))
    sd_ = np.ascontiguousarray(s.reshape(16, 128, 64).transpose(1, 0, 2).reshape(128, 1024))
    lp = np.ascontiguousarray((np.log(pos + 1.0) / np.sqrt(128.0)).astype(np.float32)[None, :])
    return cd, sd_, lp


def _cin_dram(nc, cin):
    return {k: nc.dram_tensor("cc_" + k, list(v.shape), BF16 if v.dtype == BF else F32,
                              kind="ExternalInput").ap() for k, v in cin.items()}


def _f32in(nc, name, shape):
    return nc.dram_tensor(name, list(shape), F32, kind="ExternalInput").ap()


def _precast_pair(nc, name, shape):
    s = nc.dram_tensor(name + "_f", list(shape), F32, kind="ExternalInput").ap()
    d = nc.dram_tensor(name + "_b", list(shape), BF16, kind="Internal").ap()
    return s, d


def _tabs_phase(ctx, cosd, sind, cos_sb, sin_sb):
    pl = Plan(ctx, "tabs")
    sd = pl.sem("dma_sync")
    pl.step("sync", lambda e: e.dma_start(out=cos_sb[:, :], in_=cosd), inc=(sd, 16))
    pl.step("sync", lambda e: e.dma_start(out=sin_sb[:, :], in_=sind), inc=(sd, 16))
    pl.step("sync", lambda e: e.nop(), waits=[(sd, pl.rel(sd))])
    return pl


def _build_launch_a(cin):
    nc = bass.Bass("TRN2", target_bir_lowering=False, debug=False)
    x = _f32in(nc, "x", [2048, 2048])
    wq = _precast_pair(nc, "wq", [2048, 2048])
    wk = _precast_pair(nc, "wk", [2048, 2048])
    wv = _precast_pair(nc, "wv", [2048, 2048])
    cosd = _f32in(nc, "cosd", [128, 1024]); sind = _f32in(nc, "sind", [128, 1024])
    logpos = _f32in(nc, "logpos", [1, 2048]); scaler = _f32in(nc, "scaler", [1, 16])
    xn0 = nc.dram_tensor("xn0", [2048, 2048], BF16, kind="ExternalOutput").ap()
    q0 = nc.dram_tensor("q0", [2048, 2048], BF16, kind="ExternalOutput").ap()
    kv0 = nc.dram_tensor("kv0", [4096, 2048], BF16, kind="ExternalOutput").ap()
    caps = _cin_dram(nc, cin)
    with ExitStack() as stack:
        ctx = Ctx(nc, stack)
        consts = ConstPool(ctx, caps)
        cos_sb = stack.enter_context(nc.sbuf_tensor("cos_sb", [128, 1024], F32))
        sin_sb = stack.enter_context(nc.sbuf_tensor("sin_sb", [128, 1024], F32))
        qscale_sb = stack.enter_context(nc.sbuf_tensor("qscale_sb", [128, 256], F32))
        ctx.run_phase(consts)
        ctx.run_phase(make_precast(ctx, "a", [wq, wk, wv]))
        ctx.run_phase(_tabs_phase(ctx, cosd, sind, cos_sb, sin_sb))
        with ctx.scope():
            ctx.run_phase(make_qscale_phase(ctx, consts, logpos, scaler, qscale_sb))
        with ctx.scope():
            ctx.run_phase(LNPhase(ctx, consts, x, xn0, "l0"))
        with ctx.scope():
            jobs = [{"w": wq[1], "out": q0, "mode": "q0"},
                    {"w": wk[1], "out": kv0[0:2048, :], "mode": "k0"},
                    {"w": wv[1], "out": kv0[2048:4096, :], "mode": "v"}]
            ctx.run_phase(make_qkv_phase(ctx, consts, "a", xn0, jobs, cos_sb, sin_sb, qscale_sb))
    return nc


def _build_launch_b(cin):
    nc = bass.Bass("TRN2", target_bir_lowering=False, debug=False)
    x = _f32in(nc, "x", [2048, 2048])
    xn0 = nc.dram_tensor("xn0", [2048, 2048], BF16, kind="ExternalInput").ap()
    q0 = nc.dram_tensor("q0", [2048, 2048], BF16, kind="ExternalInput").ap()
    ktm0 = nc.dram_tensor("ktm0", [4096, 2048], BF16, kind="ExternalInput").ap()
    vtm0 = nc.dram_tensor("vtm0", [4096, 2048], BF16, kind="ExternalInput").ap()
    lam = [_f32in(nc, f"lam{i}", [1, 64]) for i in range(4)]
    amask = _f32in(nc, "amask", [128, 256])
    cosd = _f32in(nc, "cosd", [128, 1024]); sind = _f32in(nc, "sind", [128, 1024])
    shapes0 = {"wg8": [2048, 8], "w1": [2048, 7168], "w3": [2048, 7168], "w2": [7168, 2048],
               "a1": [2048, 256], "a3": [2048, 256], "b1": [256, 7168], "b3": [256, 7168],
               "a2": [7168, 256], "b2": [256, 2048]}
    wp = {k: _precast_pair(nc, k, v) for k, v in shapes0.items()}
    wq1 = _precast_pair(nc, "wq1", [2048, 2048])
    wk1 = _precast_pair(nc, "wk1", [2048, 2048])
    wv1 = _precast_pair(nc, "wv1", [2048, 2048])
    x1 = nc.dram_tensor("x1", [2048, 2048], F32, kind="ExternalOutput").ap()
    xn1 = nc.dram_tensor("xn1", [2048, 2048], BF16, kind="ExternalOutput").ap()
    q1 = nc.dram_tensor("q1", [2048, 2048], BF16, kind="ExternalOutput").ap()
    kv1 = nc.dram_tensor("kv1", [4096, 2048], BF16, kind="ExternalOutput").ap()
    s0 = nc.dram_tensor("s0", [2048, 2048], F32, kind="Internal").ap()
    g0 = nc.dram_tensor("g0", [2048, 2048], F32, kind="Internal").ap()
    caps = _cin_dram(nc, cin)
    with ExitStack() as stack:
        ctx = Ctx(nc, stack)
        consts = ConstPool(ctx, caps)
        cos_sb = stack.enter_context(nc.sbuf_tensor("cos_sb", [128, 1024], F32))
        sin_sb = stack.enter_context(nc.sbuf_tensor("sin_sb", [128, 1024], F32))
        neglam = stack.enter_context(nc.sbuf_tensor("neglam", [128, 1], F32))
        ctx.run_phase(consts)
        ctx.run_phase(make_precast(ctx, "b", list(wp.values()) + [wq1, wk1, wv1]))
        ctx.run_phase(_tabs_phase(ctx, cosd, sind, cos_sb, sin_sb))
        with ctx.scope():
            ctx.run_phase(make_lambda_phase(ctx, consts, lam, neglam))
        with ctx.scope():
            ctx.run_phase(make_attn_l0(ctx, consts, "b", q0, ktm0, vtm0, amask, neglam, s0))
        with ctx.scope():
            ctx.run_phase(make_ffn_l0(ctx, consts, "b", xn0,
                                      {k: v[1] for k, v in wp.items()}, g0))
        with ctx.scope():
            ctx.run_phase(make_combine(ctx, consts, "b", x, s0, g0, x1))
        with ctx.scope():
            ctx.run_phase(LNPhase(ctx, consts, x1, xn1, "l1"))
        with ctx.scope():
            jobs = [{"w": wq1[1], "out": q1, "mode": "q1"},
                    {"w": wk1[1], "out": kv1[0:2048, :], "mode": "k1"},
                    {"w": wv1[1], "out": kv1[2048:4096, :], "mode": "v"}]
            ctx.run_phase(make_qkv_phase(ctx, consts, "b", xn1, jobs, cos_sb, sin_sb, None))
    return nc


def _build_launch_c(cin):
    nc = bass.Bass("TRN2", target_bir_lowering=False, debug=False)
    x1 = _f32in(nc, "x1", [2048, 2048])
    xn1 = nc.dram_tensor("xn1", [2048, 2048], BF16, kind="ExternalInput").ap()
    q1 = nc.dram_tensor("q1", [2048, 2048], BF16, kind="ExternalInput").ap()
    ktm1 = nc.dram_tensor("ktm1", [4096, 2048], BF16, kind="ExternalInput").ap()
    vtm1 = nc.dram_tensor("vtm1", [4096, 2048], BF16, kind="ExternalInput").ap()
    amask = _f32in(nc, "amask", [128, 256])
    wo = _precast_pair(nc, "wo", [2048, 2048])
    wgte = _precast_pair(nc, "wgte", [2048, 8192])
    wup = _precast_pair(nc, "wup", [2048, 8192])
    wdn = _precast_pair(nc, "wdn", [8192, 2048])
    out = nc.dram_tensor("out", [2048, 2048], F32, kind="ExternalOutput").ap()
    s1 = nc.dram_tensor("s1", [2048, 2048], F32, kind="Internal").ap()
    g1 = nc.dram_tensor("g1", [2048, 2048], F32, kind="Internal").ap()
    caps = _cin_dram(nc, cin)
    with ExitStack() as stack:
        ctx = Ctx(nc, stack)
        consts = ConstPool(ctx, caps)
        ctx.run_phase(consts)
        ctx.run_phase(make_precast(ctx, "c", [wo, wgte, wup, wdn]))
        with ctx.scope():
            o_fm = ctx.stack.enter_context(nc.sbuf_tensor("o_fm", [128, 16 * 2048], BF16))
            with ctx.scope():
                ctx.run_phase(make_attn_l1(ctx, consts, "c", q1, ktm1, vtm1, amask, o_fm))
            with ctx.scope():
                ctx.run_phase(make_wo_phase(ctx, consts, "c", o_fm, wo[1], s1))
        with ctx.scope():
            ctx.run_phase(make_ffn_l1(ctx, consts, "c", xn1, wgte[1], wup[1], wdn[1], g1))
        with ctx.scope():
            ctx.run_phase(make_combine(ctx, consts, "c", x1, s1, g1, out))
    return nc


_cache = {}


def kernel(**inputs):
    cin = host_consts()
    cink = {"cc_" + k: v for k, v in cin.items()}
    C = np.ascontiguousarray

    x = np.asarray(inputs["L0_attn_x"], np.float32)
    wq_d = C(np.asarray(inputs["L0_attn_Wq"], np.float32))
    wk_d = C(np.asarray(inputs["L0_attn_Wk"], np.float32))
    wv_d = C(np.asarray(inputs["L0_attn_Wv"], np.float32))
    scaler_d = C(np.asarray(inputs["L0_attn_softmax_scaler"], np.float32).reshape(1, 16))
    lam_d = [C(np.asarray(inputs[k], np.float32).reshape(1, 64)) for k in
             ("L0_attn_lambda_q1", "L0_attn_lambda_k1", "L0_attn_lambda_q2", "L0_attn_lambda_k2")]
    wg8_d = C(np.asarray(inputs["L0_ffn_Wg"], np.float32).T)
    w1_d = C(np.asarray(inputs["L0_ffn_W1"], np.float32).T)
    w3_d = C(np.asarray(inputs["L0_ffn_W3"], np.float32).T)
    w2_d = C(np.asarray(inputs["L0_ffn_W2"], np.float32).T)
    a1_d = C(np.asarray(inputs["L0_ffn_A1"], np.float32).transpose(2, 0, 1).reshape(2048, 256))
    a3_d = C(np.asarray(inputs["L0_ffn_A3"], np.float32).transpose(2, 0, 1).reshape(2048, 256))
    b1_d = C(np.asarray(inputs["L0_ffn_B1"], np.float32).transpose(0, 2, 1).reshape(256, 7168))
    b3_d = C(np.asarray(inputs["L0_ffn_B3"], np.float32).transpose(0, 2, 1).reshape(256, 7168))
    a2_d = C(np.asarray(inputs["L0_ffn_A2"], np.float32).transpose(2, 0, 1).reshape(7168, 256))
    b2_d = C(np.asarray(inputs["L0_ffn_B2"], np.float32).transpose(0, 2, 1).reshape(256, 2048))
    wq1_d = C(np.asarray(inputs["L1_attn_wq"], np.float32).reshape(2048, 2048))
    wk1_d = C(np.asarray(inputs["L1_attn_wk"], np.float32).reshape(2048, 2048))
    wv1_d = C(np.asarray(inputs["L1_attn_wv"], np.float32).reshape(2048, 2048))
    wo_d = C(np.asarray(inputs["L1_attn_wo"], np.float32).reshape(2048, 2048))
    wg_d = C(np.asarray(inputs["L1_ffn_w_gate"], np.float32).reshape(2048, 16, 128, 4)
             .transpose(0, 1, 3, 2).reshape(2048, 8192))
    wu_d = C(np.asarray(inputs["L1_ffn_w_up"], np.float32).reshape(2048, 16, 128, 4)
             .transpose(0, 1, 3, 2).reshape(2048, 8192))
    wd_d = C(np.asarray(inputs["L1_ffn_w_down"], np.float32).reshape(16, 128, 4, 2048)
             .transpose(0, 2, 1, 3).reshape(8192, 2048))

    tabs = [_tabs(POS_H[0]), _tabs(POS_H[1])]
    x_core = [C(x[c // 2][POS_H[c % 2]].T) for c in range(8)]

    if FUSED:
        wdevs = {"wq_f": wq_d, "wk_f": wk_d, "wv_f": wv_d,
                 "wg8_f": wg8_d, "w1_f": w1_d, "w3_f": w3_d, "w2_f": w2_d,
                 "a1_f": a1_d, "a3_f": a3_d, "b1_f": b1_d, "b3_f": b3_d,
                 "a2_f": a2_d, "b2_f": b2_d,
                 "wq1_f": wq1_d, "wk1_f": wk1_d, "wv1_f": wv1_d,
                 "wo_f": wo_d, "wgte_f": wg_d, "wup_f": wu_d, "wdn_f": wd_d}
        return kernel_fused(inputs, cin, cink, (x_core, tabs, wdevs, scaler_d, lam_d))

    if "A" not in _cache:
        _cache["A"] = _build_launch_a(cin)
    in_maps = []
    for c in range(8):
        h = c % 2
        in_maps.append({"x": x_core[c], "wq_f": wq_d, "wk_f": wk_d, "wv_f": wv_d,
                        "cosd": tabs[h][0], "sind": tabs[h][1], "logpos": tabs[h][2],
                        "scaler": scaler_d, **cink})
    ra = run_bass_kernel_spmd(_cache["A"], in_maps, core_ids=list(range(8)))
    kt0, vt0 = [], []
    for b in range(4):
        k_full = np.concatenate([ra.results[2*b]["kv0"][0:2048], ra.results[2*b+1]["kv0"][0:2048]], 0)
        v_full = np.concatenate([ra.results[2*b]["kv0"][2048:4096], ra.results[2*b+1]["kv0"][2048:4096]], 0)
        kt0.append(k_full); vt0.append(v_full)

    if "B" not in _cache:
        _cache["B"] = _build_launch_b(cin)
    in_maps = []
    for c in range(8):
        b, h = c // 2, c % 2
        in_maps.append({"x": x_core[c], "xn0": ra.results[c]["xn0"], "q0": ra.results[c]["q0"],
                        "ktm0": kt0[b], "vtm0": vt0[b], "amask": AMASK_H[h],
                        **{f"lam{i}": lam_d[i] for i in range(4)},
                        "cosd": tabs[h][0], "sind": tabs[h][1],
                        "wg8_f": wg8_d, "w1_f": w1_d, "w3_f": w3_d, "w2_f": w2_d,
                        "a1_f": a1_d, "a3_f": a3_d, "b1_f": b1_d, "b3_f": b3_d,
                        "a2_f": a2_d, "b2_f": b2_d,
                        "wq1_f": wq1_d, "wk1_f": wk1_d, "wv1_f": wv1_d, **cink})
    resb = run_bass_kernel_spmd(_cache["B"], in_maps, core_ids=list(range(8)))
    rb = list(resb.results)

    kt1, vt1 = [], []
    for b in range(4):
        kt1.append(np.concatenate([rb[2*b]["kv1"][0:2048], rb[2*b+1]["kv1"][0:2048]], 0))
        vt1.append(np.concatenate([rb[2*b]["kv1"][2048:4096], rb[2*b+1]["kv1"][2048:4096]], 0))

    if "C" not in _cache:
        _cache["C"] = _build_launch_c(cin)
    ims = []
    for c in range(8):
        b, h = c // 2, c % 2
        ims.append({"x1": rb[c]["x1"], "xn1": rb[c]["xn1"], "q1": rb[c]["q1"],
                    "ktm1": kt1[b], "vtm1": vt1[b], "amask": AMASK_H[h],
                    "wo_f": wo_d, "wgte_f": wg_d, "wup_f": wu_d, "wdn_f": wd_d, **cink})
    resc = run_bass_kernel_spmd(_cache["C"], ims, core_ids=list(range(8)))
    rc = list(resc.results)

    out = np.empty((4, 4096, 2048), np.float32)
    for c in range(8):
        b, h = c // 2, c % 2
        out[b][POS_H[h]] = rc[c]["out"].T
    return out


def _build_fused(cin):
    nc = bass.Bass("TRN2", target_bir_lowering=False, debug=False)
    x = _f32in(nc, "x", [2048, 2048])
    wq = _precast_pair(nc, "wq", [2048, 2048])
    wk = _precast_pair(nc, "wk", [2048, 2048])
    wv = _precast_pair(nc, "wv", [2048, 2048])
    cosd = _f32in(nc, "cosd", [128, 1024]); sind = _f32in(nc, "sind", [128, 1024])
    logpos = _f32in(nc, "logpos", [1, 2048]); scaler = _f32in(nc, "scaler", [1, 16])
    lam = [_f32in(nc, f"lam{i}", [1, 64]) for i in range(4)]
    amask = _f32in(nc, "amask", [128, 256])
    shapes0 = {"wg8": [2048, 8], "w1": [2048, 7168], "w3": [2048, 7168], "w2": [7168, 2048],
               "a1": [2048, 256], "a3": [2048, 256], "b1": [256, 7168], "b3": [256, 7168],
               "a2": [7168, 256], "b2": [256, 2048]}
    wp = {k: _precast_pair(nc, k, v) for k, v in shapes0.items()}
    wq1 = _precast_pair(nc, "wq1", [2048, 2048])
    wk1 = _precast_pair(nc, "wk1", [2048, 2048])
    wv1 = _precast_pair(nc, "wv1", [2048, 2048])
    wo = _precast_pair(nc, "wo", [2048, 2048])
    wgte = _precast_pair(nc, "wgte", [2048, 8192])
    wup = _precast_pair(nc, "wup", [2048, 8192])
    wdn = _precast_pair(nc, "wdn", [8192, 2048])
    out = nc.dram_tensor("out", [2048, 2048], F32, kind="ExternalOutput").ap()
    xn0 = nc.dram_tensor("xn0", [2048, 2048], BF16, kind="Internal").ap()
    q0 = nc.dram_tensor("q0", [2048, 2048], BF16, kind="Internal").ap()
    kv0 = nc.dram_tensor("kv0", [4096, 2048], BF16, kind="Internal").ap()
    kvf0 = nc.dram_tensor("kvf0", [8192, 2048], BF16, kind="Internal").ap()
    s0 = nc.dram_tensor("s0", [2048, 2048], F32, kind="Internal").ap()
    g0 = nc.dram_tensor("g0", [2048, 2048], F32, kind="Internal").ap()
    x1 = nc.dram_tensor("x1", [2048, 2048], F32, kind="Internal").ap()
    xn1 = nc.dram_tensor("xn1", [2048, 2048], BF16, kind="Internal").ap()
    q1 = nc.dram_tensor("q1", [2048, 2048], BF16, kind="Internal").ap()
    kv1 = nc.dram_tensor("kv1", [4096, 2048], BF16, kind="Internal").ap()
    kvf1 = nc.dram_tensor("kvf1", [8192, 2048], BF16, kind="Internal").ap()
    s1 = nc.dram_tensor("s1", [2048, 2048], F32, kind="Internal").ap()
    g1 = nc.dram_tensor("g1", [2048, 2048], F32, kind="Internal").ap()
    caps = _cin_dram(nc, cin)
    with ExitStack() as stack:
        ctx = Ctx(nc, stack)
        consts = ConstPool(ctx, caps)
        cos_sb = stack.enter_context(nc.sbuf_tensor("cos_sb", [128, 1024], F32))
        sin_sb = stack.enter_context(nc.sbuf_tensor("sin_sb", [128, 1024], F32))
        qscale_sb = stack.enter_context(nc.sbuf_tensor("qscale_sb", [128, 256], F32))
        neglam = stack.enter_context(nc.sbuf_tensor("neglam", [128, 1], F32))
        ctx.run_phase(consts)
        pc = make_precast(ctx, "all", [wq, wk, wv] + list(wp.values())
                          + [wq1, wk1, wv1, wo, wgte, wup, wdn], wait=False)
        ctx.run_phase(pc)
        ctx.run_phase(_tabs_phase(ctx, cosd, sind, cos_sb, sin_sb))
        with ctx.scope():
            ctx.run_phase(make_qscale_phase(ctx, consts, logpos, scaler, qscale_sb))
        with ctx.scope():
            ctx.run_phase(make_lambda_phase(ctx, consts, lam, neglam))
        with ctx.scope():
            ctx.run_phase(LNPhase(ctx, consts, x, xn0, "l0"))
        ctx.run_phase(Fence(ctx, "gp2", pc.targets[id(wv[1])]))
        with ctx.scope():
            jobs = [{"w": wq[1], "out": q0, "mode": "q0"},
                    {"w": wk[1], "out": kv0[0:2048, :], "mode": "k0"},
                    {"w": wv[1], "out": kv0[2048:4096, :], "mode": "v"}]
            ctx.run_phase(make_qkv_phase(ctx, consts, "a", xn0, jobs, cos_sb, sin_sb, qscale_sb))
        ag0 = make_allgather(ctx, "kv0", kv0, kvf0, wait=True)
        ctx.run_phase(ag0)
        ctx.run_phase(Fence(ctx, "gp2", pc.targets[id(wp["b2"][1])]))
        with ctx.scope():
            ctx.run_phase(make_ffn_l0(ctx, consts, "b", xn0,
                                      {k: v[1] for k, v in wp.items()}, g0))
        ctx.run_phase(Fence(ctx, "cc1", ag0.target))
        with ctx.scope():
            kblk = [kvf0[0:2048, :], kvf0[4096:6144, :]]
            vblk = [kvf0[2048:4096, :], kvf0[6144:8192, :]]
            ctx.run_phase(make_attn_l0(ctx, consts, "b", q0, kblk, vblk, amask, neglam, s0))
        with ctx.scope():
            ctx.run_phase(make_combine(ctx, consts, "b", x, s0, g0, x1))
        with ctx.scope():
            ctx.run_phase(LNPhase(ctx, consts, x1, xn1, "l1"))
        ctx.run_phase(Fence(ctx, "gp2", pc.targets[id(wv1[1])]))
        with ctx.scope():
            jobs = [{"w": wq1[1], "out": q1, "mode": "q1"},
                    {"w": wk1[1], "out": kv1[0:2048, :], "mode": "k1"},
                    {"w": wv1[1], "out": kv1[2048:4096, :], "mode": "v"}]
            ctx.run_phase(make_qkv_phase(ctx, consts, "b", xn1, jobs, cos_sb, sin_sb, None))
        ag1 = make_allgather(ctx, "kv1", kv1, kvf1, wait=True)
        ctx.run_phase(ag1)
        ctx.run_phase(Fence(ctx, "gp2", pc.targets[id(wdn[1])]))
        with ctx.scope():
            ctx.run_phase(make_ffn_l1(ctx, consts, "c", xn1, wgte[1], wup[1], wdn[1], g1))
        ctx.run_phase(Fence(ctx, "cc1", ag1.target))
        with ctx.scope():
            o_fm = ctx.stack.enter_context(nc.sbuf_tensor("o_fm", [128, 16 * 2048], BF16))
            kblk = [kvf1[0:2048, :], kvf1[4096:6144, :]]
            vblk = [kvf1[2048:4096, :], kvf1[6144:8192, :]]
            with ctx.scope():
                ctx.run_phase(make_attn_l1(ctx, consts, "c", q1, kblk, vblk, amask, o_fm))
            with ctx.scope():
                ctx.run_phase(make_wo_phase(ctx, consts, "c", o_fm, wo[1], s1))
        with ctx.scope():
            ctx.run_phase(make_combine(ctx, consts, "c", x1, s1, g1, out))
    return nc


FUSED = True


def kernel_fused(inputs, cin, cink, layouts):
    (x_core, tabs, wdevs, scaler_d, lam_d) = layouts
    if "F" not in _cache:
        _cache["F"] = _build_fused(cin)
    in_maps = []
    for c in range(8):
        h = c % 2
        im = {"x": x_core[c], "cosd": tabs[h][0], "sind": tabs[h][1], "logpos": tabs[h][2],
              "scaler": scaler_d, "amask": AMASK_H[h],
              **{f"lam{i}": lam_d[i] for i in range(4)}, **cink}
        im.update(wdevs)
        in_maps.append(im)
    res = run_bass_kernel_spmd(_cache["F"], in_maps, core_ids=list(range(8)))
    out = np.empty((4, 4096, 2048), np.float32)
    for c in range(8):
        b, h = c // 2, c % 2
        out[b][POS_H[h]] = res.results[c]["out"].T
    return out
